# version 09
# baseline (speedup 1.0000x reference)
import numpy as np
import ml_dtypes
import concourse.bass as bass
import concourse.mybir as mybir
from concourse.bass_utils import run_bass_kernel_spmd

F32 = mybir.dt.float32
BF16 = mybir.dt.bfloat16
I32 = mybir.dt.int32
AF = mybir.ActivationFunctionType
ALU = mybir.AluOpType
BF = ml_dtypes.bfloat16

NCORES = 8
S = 2048
D = 4096
FC = D // 128
QT = S // 512
LNEPS = 1e-5


_GSEQ = [0]


class Q:

    def __init__(self, name, sem, kind):
        self.name, self.sem, self.kind = name, sem, kind
        self.n = 0
        self.waited = {}
        self.items = []

    def _need_wait(self, sem, val):
        key = id(sem)
        if self.waited.get(key, 0) >= val:
            return False
        self.waited[key] = val
        return True

    def op(self, fn, *evs):
        waits = []
        for e in evs:
            if e is None:
                continue
            sem, val = e
            if sem is self.sem:
                if val <= self.n:
                    continue
                raise RuntimeError(f"{self.name}: wait on own future event")
            if self._need_wait(sem, val):
                waits.append((sem, val))
        if self.kind in ("cmp", "dma") and self.n > 0:
            if self._need_wait(self.sem, self.n):
                waits.append((self.sem, self.n))
        step = 16 if self.kind == "dma" else 1
        self.n += step
        _GSEQ[0] += 1
        self.items.append((_GSEQ[0], waits, fn, step))
        return (self.sem, self.n)

    def ev(self):
        return (self.sem, self.n) if self.n > 0 else None

    def barrier(self, *evs):
        return self.op(lambda q: q.nop(), *evs)

    def dma(self, out, in_, *evs, **kw):
        return self.op(lambda q: q.dma_start(out=out, in_=in_, **kw), *evs)

    def replay(self, q):
        replay_merged(q, [self])


def replay_merged(q, queues):
    items = []
    for qu in queues:
        items += [(seq, waits, fn, step, qu.sem) for seq, waits, fn, step in qu.items]
    items.sort(key=lambda x: x[0])
    for seq, waits, fn, step, sem in items:
        for wsem, val in waits:
            q.wait_ge(wsem, val)
        fn(q).then_inc(sem, step)


class LB:

    def __init__(self, name):
        self.nc = bass.Bass("TRN2", target_bir_lowering=False)
        self.name = name
        self._ctx = []
        self._by_name = {}

    def _enter(self, cm):
        h = cm.__enter__()
        self._ctx.append(cm)
        return h

    def sbuf(self, name, shape, dtype):
        self._uid = getattr(self, "_uid", 0) + 1
        h = self._enter(self.nc.sbuf_tensor(f"{name}_u{self._uid}", shape, dtype))
        self._by_name.setdefault(name, []).append(self._ctx[-1])
        return h

    def free(self, *names):
        for name in names:
            cm = self._by_name[name].pop()
            self._ctx.remove(cm)
            cm.__exit__(None, None, None)

    def psum(self, name, shape, dtype=F32):
        return self._enter(self.nc.psum_tensor(name, shape, dtype))

    def mkqueues(self):
        self.q = {}
        for k, kind in (("pe", "pe"), ("act", "cmp"), ("dve", "cmp"),
                        ("pool", "cmp"), ("sd", "dma"), ("gd", "dma"),
                        ("ad", "dma")):
            sem = self._enter(self.nc.semaphore("sem_" + k))
            self.q[k] = Q(k, sem, kind)
        return self.q

    def emit(self):
        blk = self._enter(self.nc.Block())
        q = self.q

        @blk.tensor
        def _(e):
            q["pe"].replay(e)

        @blk.scalar
        def _(e):
            q["act"].replay(e)

        @blk.vector
        def _(e):
            q["dve"].replay(e)

        @blk.gpsimd
        def _(e):
            replay_merged(e, [q["pool"], q["gd"]])

        @blk.sync
        def _(e):
            replay_merged(e, [q["sd"], q["ad"]])

        for t in reversed(self._ctx):
            t.__exit__(None, None, None)
        self._ctx = []



def col_load(gd, dst, wT, oc, ncols=128, dep=None, *deps):
    src = wT[:, oc * ncols:(oc + 1) * ncols].rearrange("(fc p) o -> p fc o", p=128)
    return gd.dma(dst.rearrange("p (fc o) -> p fc o", o=ncols), src, dep, *deps)


def emit_stats(P, st, src_view, n_fc, dep=None, from_f32=False, entry=(),
               tiles=None):
    pe, act, dve, sd = P["pe"], P["act"], P["dve"], P["sd"]
    pool = P["pool"]
    ps, xs, sq, onc = st["ps"], st["xs"], st["sq"], st["onc"]
    last_mm = None
    consumed = dep
    for fc in range(n_fc):
        if tiles is not None:
            if from_f32:
                cv = P["gd"].dma(tiles[fc][:], src_view(fc),
                                 *(entry if fc == 0 else ()))
            else:
                qd = sd if fc % 2 == 0 else P["gd"]
                cv = qd.dma(tiles[fc][:], src_view(fc),
                            *(entry if fc < 2 else ()))
            xb = tiles[fc]
        elif from_f32:
            ld = sd.dma(st["xsf"][:], src_view(fc), consumed,
                        *(entry if fc == 0 else ()))
            cv = pool.op(lambda q: q.tensor_copy(xs[:], st["xsf"][:]), ld)
            xb = xs
        else:
            ld = sd.dma(xs[:], src_view(fc), consumed,
                        *(entry if fc == 0 else ()))
            xb, cv = xs, ld
        for qt in range(QT):
            sqe = pool.op(lambda q, qt=qt, xb=xb: q.tensor_tensor(
                sq[:], xb[:, qt * 512:(qt + 1) * 512],
                xb[:, qt * 512:(qt + 1) * 512], ALU.mult), cv, last_mm)
            pe.op(lambda q, qt=qt, fc=fc, xb=xb: q.matmul(
                ps[0:1, qt, :], onc[:], xb[:, qt * 512:(qt + 1) * 512],
                start=(fc == 0), stop=(fc == n_fc - 1)), cv, dep,
                *(entry if fc == 0 else ()))
            last_mm = pe.op(lambda q, qt=qt, fc=fc: q.matmul(
                ps[0:1, 4 + qt, :], onc[:], sq[:],
                start=(fc == 0), stop=(fc == n_fc - 1)), sqe)
        consumed = last_mm
    ev = None
    for qt in range(QT):
        act.op(lambda q, qt=qt: q.copy(st["ra"][:, qt * 512:(qt + 1) * 512],
                                       ps[0:1, qt, :]), last_mm)
        ev = act.op(lambda q, qt=qt: q.copy(st["rb"][:, qt * 512:(qt + 1) * 512],
                                            ps[0:1, 4 + qt, :]), last_mm)
    return ev


def emit_mrstd(P, st, stats_ev, nfeat=D, eps=LNEPS):
    act, pool = P["act"], P["pool"]
    ra, rb, rc = st["ra"], st["rb"], st["rc"]
    ev = pool.op(lambda q: q.tensor_scalar(ra[:], ra[:], 1.0 / nfeat, None, ALU.mult), stats_ev)
    ev = pool.op(lambda q: q.tensor_tensor(rc[:], ra[:], ra[:], ALU.mult), ev)
    ev = pool.op(lambda q: q.tensor_scalar(rb[:], rb[:], 1.0 / nfeat, None, ALU.mult), ev)
    ev = pool.op(lambda q: q.tensor_tensor(rb[:], rb[:], rc[:], ALU.subtract), ev)
    ev = pool.op(lambda q: q.tensor_scalar(rb[:], rb[:], float(eps), None, ALU.add), ev)
    ev = act.op(lambda q: q.activation(rb[:], rb[:], AF.Ln), ev)
    ev = act.op(lambda q: q.activation(rb[:], rb[:], AF.Exp, scale=-0.5), ev)
    return ev


def emit_bcast_row(P, st, src_ev, row_ap, dst, bank, nrows=128, war=None):
    pe, act, pool = P["pe"], P["act"], P["pool"]
    ps, onr = st["ps"], st["onr"]
    rowb = st["rowb"][0:1, :]
    cv = pool.op(lambda q: q.tensor_copy(rowb, row_ap), src_ev, war)
    ev = None
    for qt in range(QT):
        mm = pe.op(lambda q, qt=qt: q.matmul(
            ps[0:nrows, bank, :], onr[0:1, 0:nrows],
            rowb[:, qt * 512:(qt + 1) * 512], start=True, stop=True),
            cv, ev, war)
        ev = act.op(lambda q, qt=qt: q.copy(
            dst[0:nrows, qt * 512:(qt + 1) * 512], ps[0:nrows, bank, :]), mm)
    return ev


def emit_rowsum8(P, st, src_b, dst_row, bank, deps):
    pe, act = P["pe"], P["act"]
    ps, onc = st["ps"], st["onc"]
    ev = None
    for qt in range(QT):
        mm = pe.op(lambda q, qt=qt: q.matmul(
            ps[0:1, bank, :], onc[0:8, :], src_b[0:8, qt * 512:(qt + 1) * 512],
            start=True, stop=True), *( list(deps) + [ev]))
        ev = act.op(lambda q, qt=qt: q.copy(
            dst_row[0:1, qt * 512:(qt + 1) * 512], ps[0:1, bank, :]), mm)
    return ev


def act_recip(P, row_ap, *evs):
    act = P["act"]
    ev = act.op(lambda q: q.activation(row_ap, row_ap, AF.Ln), *evs)
    return act.op(lambda q: q.activation(row_ap, row_ap, AF.Exp, scale=-1.0), ev)


def emit_ln(P, st, x_dram, ln_tiles, mB, rB, dep, n_fc=FC, from_f32=False,
            resident=False):
    pool, sd = P["pool"], P["sd"]
    ev = dep
    if resident:
        for fc in range(n_fc):
            e1 = pool.op(lambda q, fc=fc: q.tensor_tensor(
                ln_tiles[fc][:], ln_tiles[fc][:], mB[:], ALU.subtract), dep)
            ev = pool.op(lambda q, fc=fc: q.tensor_tensor(
                ln_tiles[fc][:], ln_tiles[fc][:], rB[:], ALU.mult), e1)
        return ev
    xb = [st.get("xsf"), st.get("xsf2", st.get("xsf"))] if from_f32 else None
    nb = 2 if (from_f32 and "xsf2" in st) else 1
    for fc in range(n_fc):
        if from_f32:
            ld = sd.dma(xb[fc % 2][:], x_dram[fc * 128:(fc + 1) * 128, :],
                        st.get(("lncv", fc - nb)),
                        dep if fc < nb else None)
            ld = pool.op(lambda q, fc=fc: q.tensor_copy(ln_tiles[fc][:], xb[fc % 2][:]), ld)
            st[("lncv", fc)] = ld
        else:
            ld = sd.dma(ln_tiles[fc][:], x_dram[fc * 128:(fc + 1) * 128, :])
        e1 = pool.op(lambda q, fc=fc: q.tensor_tensor(ln_tiles[fc][:], ln_tiles[fc][:], mB[:], ALU.subtract), ld, dep)
        ev = pool.op(lambda q, fc=fc: q.tensor_tensor(ln_tiles[fc][:], ln_tiles[fc][:], rB[:], ALU.mult), e1)
    return ev



FFN0_EXT = dict(x1=([D, S], BF16), wbt=([D, D], F32), bcols=([128, 32], F32),
                atp=([128, D], F32), gtp=([128, FC * 16], F32), bp=([128, D], F32),
                pat8=([8, 128], BF16), ones_col=([128, 1], BF16),
                ones_row=([1, 128], BF16), bthr=([1, 16], F32))


def emit_ffn0(lb, P, ext, ps, entry=()):
    x1, wbt, bcols, atp = ext["x1"], ext["wbt"], ext["bcols"], ext["atp"]
    gtp, bp, pat8 = ext["gtp"], ext["bp"], ext["pat8"]
    ones_col, ones_row, bthr = ext["ones_col"], ext["ones_row"], ext["bthr"]
    x2 = ext["x2"]

    ln = [lb.sbuf(f"ln{i}", [128, S], BF16) for i in range(FC)]
    xs = lb.sbuf("xs", [128, S], BF16)
    onc = lb.sbuf("onc", [128, 1], BF16)
    onr = lb.sbuf("onr", [1, 128], BF16)
    p8 = lb.sbuf("p8", [8, 128], BF16)
    bth = lb.sbuf("bth", [1, 16], F32)
    bcs = lb.sbuf("bcs", [128, 32], F32)
    hw_sb = lb.sbuf("hw_sb", [128, S], BF16)
    PH1 = ["sq", "ra", "rb", "rc", "rowbt", "mB", "rB", "acol", "gcol",
           "h_sb", "g_f", "g_b", "recB", "thrB", "wBq"]
    sq = lb.sbuf("sq", [128, 512], BF16)
    ra = lb.sbuf("ra", [1, S], F32)
    rb = lb.sbuf("rb", [1, S], F32)
    rc = lb.sbuf("rc", [1, S], F32)
    rowbt = lb.sbuf("rowbt", [1, S], BF16)
    mB = lb.sbuf("mB", [128, S], BF16)
    rB = lb.sbuf("rB", [128, S], BF16)
    acol = lb.sbuf("acol", [128, D], BF16)
    gcol = lb.sbuf("gcol", [128, FC * 16], BF16)
    h_sb = lb.sbuf("h_sb", [128, S], BF16)
    g_f = lb.sbuf("g_f", [8, S], F32)
    g_b = lb.sbuf("g_b", [8, S], BF16)
    recB = lb.sbuf("recB", [8, S], BF16)
    thrB = lb.sbuf("thrB", [8, S], BF16)
    wBq = lb.sbuf("wBq", [128, 512], BF16)
    pe, act, dve = P["pe"], P["act"], P["dve"]
    sd, gd, ad = P["sd"], P["gd"], P["ad"]

    st = dict(ps=ps, xs=xs, sq=sq, onc=onc, onr=onr, rowb=rowbt,
              ra=ra, rb=rb, rc=rc)

    c1 = sd.dma(onc[:], ones_col[:, :], *entry)
    c2 = sd.dma(onr[:], ones_row[:, :])
    c3 = sd.dma(p8[:], pat8[:, :])
    c4 = sd.dma(bth[:], bthr[:, :])
    c5 = sd.dma(bcs[:], bcols[:, :])
    a_ld = gd.dma(acol[:], atp[:, :], *entry)
    g_ld = gd.dma(gcol[:], gtp[:, :])

    sev = emit_stats(P, st, lambda fc: x1[fc * 128:(fc + 1) * 128, :], FC,
                     dep=c5, entry=entry, tiles=ln)
    mev = emit_mrstd(P, st, sev)
    bm = emit_bcast_row(P, st, mev, ra[:], mB, 2)
    br = emit_bcast_row(P, st, mev, rb[:], rB, 3, war=bm)
    lev = emit_ln(P, st, x1, ln, mB, rB, br, resident=True)

    mm_h = None
    for fc in range(FC):
        for qt in range(QT):
            mm_h = pe.op(lambda q, fc=fc, qt=qt: q.matmul(
                ps[:, 4 + qt, :], acol[:, fc * 128:(fc + 1) * 128],
                ln[fc][:, qt * 512:(qt + 1) * 512],
                start=(fc == 0), stop=(fc == FC - 1)),
                *([a_ld, lev] if fc == 0 and qt == 0 else []))
    eh = None
    for qt in range(QT):
        eh = act.op(lambda q, qt=qt: q.copy(h_sb[:, qt * 512:(qt + 1) * 512],
                                            ps[:, 4 + qt, :]), mm_h)
    mm_g = None
    for fc in range(FC):
        for qt in range(QT):
            mm_g = pe.op(lambda q, fc=fc, qt=qt: q.matmul(
                ps[0:8, qt, :], gcol[:, fc * 16:fc * 16 + 8],
                ln[fc][:, qt * 512:(qt + 1) * 512],
                start=(fc == 0), stop=(fc == FC - 1)),
                *([g_ld, lev, bm] if fc == 0 and qt == 0 else []))
    eg = None
    for qt in range(QT):
        eg = act.op(lambda q, qt=qt: q.copy(g_f[:, qt * 512:(qt + 1) * 512],
                                            ps[0:8, qt, :]), mm_g)
    mm_t = None
    for fc in range(FC):
        for qt in range(QT):
            mm_t = pe.op(lambda q, fc=fc, qt=qt: q.matmul(
                ps[0:1, qt, :], gcol[:, fc * 16 + 8:fc * 16 + 9],
                ln[fc][:, qt * 512:(qt + 1) * 512],
                start=(fc == 0), stop=(fc == FC - 1)),
                *([eg] if fc == 0 and qt == 0 else []))
    etv = None
    for qt in range(QT):
        etv = act.op(lambda q, qt=qt: q.copy(ra[0:1, qt * 512:(qt + 1) * 512],
                                             ps[0:1, qt, :]), mm_t, bm)

    pool = P["pool"]
    e1 = act.op(lambda q: q.activation(g_f[0:8, :], g_f[0:8, :], AF.Exp), eg)
    e2 = act.op(lambda q: q.activation(ra[:], ra[:], AF.Identity, bias=bth[0:1, 0:1]), etv, c4)
    e3 = act.op(lambda q: q.activation(ra[:], ra[:], AF.Exp, scale=-1.0), e2)
    e4 = pool.op(lambda q: q.tensor_scalar(ra[:], ra[:], 1.0, None, ALU.add), e3)
    e5 = act_recip(P, ra[:], e4)
    e6 = pool.op(lambda q: q.tensor_scalar(ra[:], ra[:], 0.125, None, ALU.mult), e5)
    e7 = pool.op(lambda q: q.tensor_copy(g_b[0:8, :], g_f[0:8, :]), e1)
    e8 = emit_rowsum8(P, st, g_b, rb[:], 4, [e7, eh, br])
    e9 = act_recip(P, rb[:], e8)
    eB = emit_bcast_row(P, st, e9, rb[:], recB, 5, nrows=8, war=eh)
    eD = emit_bcast_row(P, st, e6, ra[:], thrB, 5, nrows=8, war=eB)
    eE = pool.op(lambda q: q.tensor_tensor(g_f[0:8, :], g_f[0:8, :], recB[0:8, :], ALU.mult), eB, e1)
    eF = pool.op(lambda q: q.tensor_tensor(g_f[0:8, :], g_f[0:8, :], thrB[0:8, :], ALU.subtract), eD, eE)
    eG = pool.op(lambda q: q.tensor_scalar(g_f[0:8, :], g_f[0:8, :], 0.0, None, ALU.max), eF)
    eH = pool.op(lambda q: q.tensor_copy(g_b[0:8, :], g_f[0:8, :]), eG)
    eI = emit_rowsum8(P, st, g_b, rb[:], 4, [eH, e8])
    eJ = pool.op(lambda q: q.tensor_scalar(rc[:], rb[:], 0.0, None, ALU.is_equal), eI, eD)
    eK = pool.op(lambda q: q.tensor_tensor(rb[:], rb[:], rc[:], ALU.add), eJ)
    eL = act_recip(P, rb[:], eK)
    eM = emit_bcast_row(P, st, eL, rb[:], recB, 5, nrows=8, war=eD)
    eO = pool.op(lambda q: q.tensor_tensor(g_f[0:8, :], g_f[0:8, :], recB[0:8, :], ALU.mult), eM, eG)
    eP = pool.op(lambda q: q.tensor_copy(g_b[0:8, :], g_f[0:8, :]), eO)
    hw_ev = None
    eQ = None
    for qt in range(QT):
        mm5 = pe.op(lambda q, qt=qt: q.matmul(
            ps[:, 6, :], p8[0:8, :], g_b[0:8, qt * 512:(qt + 1) * 512],
            start=True, stop=True), eP, c3, hw_ev, eh)
        eQ = act.op(lambda q, qt=qt: q.copy(wBq[:], ps[:, 6, :]), mm5)
        hw_ev = pool.op(lambda q, qt=qt: q.tensor_tensor(
            hw_sb[:, qt * 512:(qt + 1) * 512], h_sb[:, qt * 512:(qt + 1) * 512],
            wBq[:], ALU.mult), eQ, eh)

    dbg_ev = None

    lb.free(*reversed(PH1))
    wcol = [lb.sbuf(f"wcol{i}", [128, D], BF16) for i in range(2)]
    bps = lb.sbuf("bps", [128, D], BF16)
    x1r = [lb.sbuf(f"x1r{i}", [128, S], BF16) for i in range(2)]
    x2t = [lb.sbuf(f"x2t{i}", [128, S], BF16) for i in range(2)]
    ev32 = [lb.sbuf(f"ev32_{i}", [128, 512], F32) for i in range(4)]
    b_ld = gd.dma(bps[:], bp[:, :], hw_ev, dbg_ev)
    wld = [None, None]
    x1ld = [None, None]
    mm_done = {}
    evac_done = {}
    out_done = {}
    wld[0] = col_load(gd, wcol[0], wbt, 0, dep=b_ld)
    x1ld[0] = sd.dma(x1r[0][:], x1[0:128, :], lev, hw_ev)
    for oc in range(FC):
        pb = oc % 2
        half = pb * 4
        if oc + 1 < FC:
            nb = (oc + 1) % 2
            wld[nb] = col_load(gd, wcol[nb], wbt, oc + 1, dep=mm_done.get(oc - 1))
            x1ld[nb] = sd.dma(x1r[nb][:], x1[(oc + 1) * 128:(oc + 2) * 128, :],
                              evac_done.get(oc - 1))
        deps = [wld[pb], lev, evac_done.get(oc - 2)]
        if oc == 0:
            deps.append(etv)
        if oc == 1:
            deps.append(hw_ev)
        mm = None
        for fc in range(FC):
            for qt in range(QT):
                mm = pe.op(lambda q, fc=fc, qt=qt, pb=pb, half=half: q.matmul(
                    ps[:, half + qt, :], wcol[pb][:, fc * 128:(fc + 1) * 128],
                    ln[fc][:, qt * 512:(qt + 1) * 512],
                    start=(fc == 0), stop=False),
                    *(deps if fc == 0 and qt == 0 else []))
        for qt in range(QT):
            mm = pe.op(lambda q, qt=qt, oc=oc, half=half: q.matmul(
                ps[:, half + qt, :], bps[:, oc * 128:(oc + 1) * 128],
                hw_sb[:, qt * 512:(qt + 1) * 512],
                start=False, stop=True), *([hw_ev] if qt == 0 else []))
        mm_done[oc] = mm
        ee = None
        for qt in range(QT):
            eb = qt
            ea = act.op(lambda q, qt=qt, half=half, eb=eb, oc=oc: q.activation(
                ev32[eb][:, :], ps[:, half + qt, :], AF.Identity,
                bias=bcs[:, oc:oc + 1]),
                mm, c5, evac_done.get(oc - 1))
            ee = pool.op(lambda q, qt=qt, oc=oc, pb=pb, eb=eb: q.tensor_tensor(
                x2t[pb][:, qt * 512:(qt + 1) * 512], ev32[eb][:, :],
                x1r[pb][:, qt * 512:(qt + 1) * 512], ALU.add),
                ea, x1ld[pb], out_done.get(oc - 2))
        evac_done[oc] = ee
        out_done[oc] = ad.dma(x2[oc * 128:(oc + 1) * 128, :], x2t[pb][:], ee)

    lb.free("ev32_3", "ev32_2", "ev32_1", "ev32_0", "x2t1", "x2t0", "x1r1",
            "x1r0", "bps", "wcol1", "wcol0", "hw_sb", "bcs", "bth", "p8",
            "onr", "onc", "xs")
    for i in reversed(range(FC)):
        lb.free(f"ln{i}")
    return out_done[FC - 1]


def build_ffn0():
    lb = LB("ffn0")
    nc = lb.nc
    ext = {k: nc.dram_tensor(k, sh, dt, kind="ExternalInput").ap()
           for k, (sh, dt) in FFN0_EXT.items()}
    ext["x2"] = nc.dram_tensor("x2", [D, S], BF16, kind="ExternalOutput").ap()
    ps = lb.psum("ps", [128, 8, 512], F32)
    P = lb.mkqueues()
    done = emit_ffn0(lb, P, ext, ps)
    P["pe"].barrier(done)
    lb.emit()
    return lb



def _prechunk_T(wT):
    Dd, O = wT.shape
    return np.ascontiguousarray(
        wT.reshape(Dd // 128, 128, O).transpose(1, 0, 2).reshape(128, (Dd // 128) * O))


def _pat8():
    p = np.zeros((8, 128), BF)
    for e in range(8):
        p[e, e * 16:(e + 1) * 16] = 2.0
    return p


def prep_ffn0_inputs(inp, x1_fm_bf16):
    d = dict(
        wbt=np.ascontiguousarray(inp["L0_ffn_W_base"].astype(np.float32).T),
        bcols=np.ascontiguousarray(inp["L0_ffn_b_base"].astype(np.float32).reshape(32, 128).T),
        atp=_prechunk_T(inp["L0_ffn_A_w"].astype(np.float32).reshape(128, D).T),
        gtp=_prechunk_T(np.ascontiguousarray(np.concatenate(
            [inp["L0_ffn_W_gate"].astype(np.float32),
             inp["L0_ffn_W_thr"].astype(np.float32)[None, :],
             np.zeros((7, D), np.float32)], axis=0).T)),
        bp=np.ascontiguousarray(
            inp["L0_ffn_B_w"].astype(np.float32).transpose(0, 2, 1).reshape(128, D)),
        pat8=_pat8(),
        ones_col=np.ones((128, 1), BF),
        ones_row=np.ones((1, 128), BF),
        bthr=np.concatenate([inp["L0_ffn_b_thr"].astype(np.float32),
                             np.zeros(15, np.float32)]).reshape(1, 16),
    )
    if x1_fm_bf16 is not None:
        d["x1"] = np.ascontiguousarray(x1_fm_bf16)
    return d


_BUILT = {}


def _get(name, fn):
    if name not in _BUILT:
        _BUILT[name] = fn()
    return _BUILT[name]


def build_mega():
    lb = LB("mega")
    nc = lb.nc
    ext = {}
    for table in (ATTN0_EXT, FFN0_EXT, ATTN1_EXT, FFN1_EXT, DEQ_EXT):
        for k, (sh, dt) in table.items():
            if k not in ext and k not in ("x1", "x2", "x3"):
                ext[k] = nc.dram_tensor(k, sh, dt, kind="ExternalInput").ap()
    ext["out"] = nc.dram_tensor("out", [D, S], F32, kind="ExternalOutput").ap()
    ext["qkv_s"] = nc.dram_tensor("qkv_s", [3 * D, S], BF16)
    ext["sigq_s"] = nc.dram_tensor("sigq_s", [1024, S], BF16).ap()
    ext["x1"] = nc.dram_tensor("x1_s", [D, S], BF16).ap()
    ext["x2"] = nc.dram_tensor("x2_s", [D, S], BF16).ap()
    ext["x3"] = nc.dram_tensor("x3_s", [D, S], BF16).ap()
    ext["wlT"] = nc.dram_tensor("wlT", [D, 512], BF16).ap()
    ext["whT"] = nc.dram_tensor("whT", [D, 512], BF16).ap()
    ext["wl_o"] = nc.dram_tensor("wl_o", [512, D], BF16).ap()
    ext["wh_o"] = nc.dram_tensor("wh_o", [512, D], BF16).ap()
    ext["gl"] = nc.dram_tensor("gl", [NCORES * D, 512], BF16, addr_space="Shared").ap()
    ext["gh"] = nc.dram_tensor("gh", [NCORES * D, 512], BF16, addr_space="Shared").ap()
    ps = lb.psum("ps", [128, 8, 512], F32)
    P = lb.mkqueues()

    def entry(*extra):
        evs = tuple(e for e in (P[k].ev() for k in ("pe", "act", "dve", "pool"))
                    if e is not None)
        return evs + tuple(e for e in extra if e is not None)

    hook = lambda dep, bufs: emit_dequant_gen(lb, P, ext, bufs, entry=(dep,))
    d1 = emit_attn0(lb, P, ext, ps, entry=entry(), mid_hook=hook)
    d2 = emit_ffn0(lb, P, ext, ps, entry=entry(d1))
    d3 = emit_attn1(lb, P, ext, ps, entry=entry(d2))
    e1, e2 = emit_dequant_tr(lb, P, ext, ps, ext["deq_ev"], entry=entry(d3))
    ent4 = entry(d3, e1, e2)
    cc1 = P["pool"].op(lambda q: q.collective_compute(
        "AllGather", ALU.bypass, replica_groups=[list(range(NCORES))],
        ins=[ext["wlT"][:, :]], outs=[ext["gl"][:, :]]), e1)
    cc2 = P["pool"].op(lambda q: q.collective_compute(
        "AllGather", ALU.bypass, replica_groups=[list(range(NCORES))],
        ins=[ext["whT"][:, :]], outs=[ext["gh"][:, :]]), e2)
    ext["cc_ev"] = (cc1, cc2)
    d4 = emit_ffn1(lb, P, ext, ps, entry=ent4)
    P["pe"].barrier(d4)
    lb.emit()
    return lb


def prep_mega_inputs(inputs, core):
    d = {}
    d.update(prep_dequant_inputs(inputs, core))
    d.update(prep_attn0_inputs(inputs, core))
    x = inputs["L0_attn_x"][core]
    d.update(prep_ffn0_inputs(inputs, None))
    d.update(prep_attn1_inputs(inputs, None))
    d.update(prep_ffn1_inputs(inputs, None, None, None))
    for k in ("x1", "x2", "x3", "gl", "gh"):
        d.pop(k, None)
    return d


def kernel(**inputs):
    cores = list(range(NCORES))
    lbM = _get("mega", build_mega)
    res = run_bass_kernel_spmd(
        lbM.nc, [prep_mega_inputs(inputs, c) for c in cores], core_ids=cores)
    out = np.stack([res.results[c]["out"].T for c in cores], axis=0)
    return np.ascontiguousarray(out)



def emit_eqsum(q, acc, ind, c_b, lut, t0, t1, deps):
    ev = q.op(lambda qq, t=t0: qq.tensor_scalar(
        acc[:], c_b[:], float(t), lut[:, t:t + 1], ALU.is_equal, ALU.mult),
        *deps)
    for t in range(t0 + 1, t1):
        q.op(lambda qq, t=t: qq.tensor_scalar(
            ind[:], c_b[:], float(t), lut[:, t:t + 1], ALU.is_equal, ALU.mult), ev)
        ev = q.op(lambda qq, t=t: qq.tensor_tensor(acc[:], acc[:], ind[:], ALU.add))
    return ev


DEQ_EXT = dict(qwp=([512, 1024], I32), maskc=([128, 32], I32),
               lut4c=([512, 16], F32), lut8c=([512, 256], F32),
               ident=([128, 128], BF16))

DVE_HI = 160


def alloc_dequant(lb):
    DQF = ["qwq", "tmask", "tti", "c8b", "indd", "accl", "acch", "l4", "l8"]
    b = dict(
        qw=lb.sbuf("qwq", [128, 1024], I32),
        tmask=lb.sbuf("tmask", [128, 32], I32),
        tti=lb.sbuf("tti", [128, 1024], I32),
        c8b=lb.sbuf("c8b", [128, 2048], BF16),
        indd=lb.sbuf("indd", [128, 2048], BF16),
        accl=lb.sbuf("accl", [128, 2048], BF16),
        acch=lb.sbuf("acch", [128, 2048], BF16),
        l4=lb.sbuf("l4", [128, 16], F32),
        l8=lb.sbuf("l8", [128, 256], F32),
    )
    b["names"] = DQF
    return b


def emit_dequant_gen(lb, P, ext, bufs, entry=()):
    qwp, maskc = ext["qwp"], ext["maskc"]
    lut4c, lut8c = ext["lut4c"], ext["lut8c"]
    wl_o, wh_o = ext["wl_o"], ext["wh_o"]

    DQF = bufs["names"]
    qw, tmask, tti = bufs["qw"], bufs["tmask"], bufs["tti"]
    c8b, indd = bufs["c8b"], bufs["indd"]
    accl, acch, l4, l8 = bufs["accl"], bufs["acch"], bufs["l4"], bufs["l8"]
    tb = indd[:, 0:1024]

    dve, pool, gd = P["dve"], P["pool"], P["gd"]
    cm = gd.dma(tmask[:], maskc[:, :], *entry)
    ld = gd.dma(qw[:], qwp[0:128, :])
    ll4 = gd.dma(l4[:], lut4c[0:128, :])
    ll8 = gd.dma(l8[:], lut8c[0:128, :])
    yield

    mexp = tmask[:].unsqueeze(1).broadcast_to([128, 32, 32])
    tti3 = tti[:].rearrange("p (w s) -> p w s", s=32)
    prev = {}
    for ot in range(4):
        for hf in range(2):
            def wexp(pl, hf=hf, sub=0):
                base = pl * 128 + hf * 64 + sub * 32
                return (qw[:, base:base + 32]
                        .unsqueeze(2).broadcast_to([128, 32, 32]))

            e = None
            for sub in range(2):
                csl = slice(sub * 1024, (sub + 1) * 1024)
                e = dve.op(lambda q, wexp=wexp, sub=sub: q.tensor_tensor(
                    tti3, wexp(0, sub=sub), mexp, ALU.bitwise_and), ld, cm)
                e = dve.op(lambda q, csl=csl: q.tensor_scalar(
                    c8b[:, csl], tti[:], 0, None, ALU.not_equal))
                for pl in range(1, 4):
                    e = dve.op(lambda q, pl=pl, wexp=wexp, sub=sub: q.tensor_tensor(
                        tti3, wexp(pl, sub=sub), mexp, ALU.bitwise_and))
                    e = dve.op(lambda q: q.tensor_scalar(
                        tb, tti[:], 0, None, ALU.not_equal))
                    e = dve.op(lambda q, pl=pl, csl=csl: q.scalar_tensor_tensor(
                        c8b[:, csl], tb, float(1 << pl), c8b[:, csl],
                        ALU.mult, ALU.add))
            evl = emit_eqsum(dve, accl, indd, c8b, l4, 0, 16,
                             [e, ll4, prev.get("sl")])
            e = evl
            for sub in range(2):
                csl = slice(sub * 1024, (sub + 1) * 1024)
                for pl in range(4, 8):
                    e = dve.op(lambda q, pl=pl, wexp=wexp, sub=sub: q.tensor_tensor(
                        tti3, wexp(pl, sub=sub), mexp, ALU.bitwise_and))
                    e = dve.op(lambda q: q.tensor_scalar(
                        tb, tti[:], 0, None, ALU.not_equal))
                    e = dve.op(lambda q, pl=pl, csl=csl: q.scalar_tensor_tensor(
                        c8b[:, csl], tb, float(1 << pl), c8b[:, csl],
                        ALU.mult, ALU.add))
            prev["extr"] = e
            prev["sl"] = gd.dma(wl_o[ot * 128:(ot + 1) * 128,
                                     hf * 2048:(hf + 1) * 2048], accl[:], evl)
            yield
            evh = emit_eqsum(dve, acch, indd, c8b, l8, 0, 256,
                             [e, ll8, prev.get("sh")])
            prev["sh"] = gd.dma(wh_o[ot * 128:(ot + 1) * 128,
                                     hf * 2048:(hf + 1) * 2048], acch[:], evh)
            if hf == 1 and ot + 1 < 4:
                ld = gd.dma(qw[:], qwp[(ot + 1) * 128:(ot + 2) * 128, :],
                            prev["extr"])
                ll4 = gd.dma(l4[:], lut4c[(ot + 1) * 128:(ot + 2) * 128, :],
                             evl)
                ll8 = gd.dma(l8[:], lut8c[(ot + 1) * 128:(ot + 2) * 128, :],
                             evh)
            yield
    ext["deq_ev"] = (prev["sl"], prev["sh"])
    ext["deq_done"] = (prev["sh"], prev["sl"])
    ext["deq_names"] = DQF


def emit_dequant_tr(lb, P, ext, ps, deps, entry=()):
    pe, act, ad, sd = P["pe"], P["act"], P["ad"], P["sd"]
    wl_o, wh_o = ext["wl_o"], ext["wh_o"]
    wlT, whT = ext["wlT"], ext["whT"]
    idt2 = lb.sbuf("idt2", [128, 128], BF16)
    acc2 = [lb.sbuf(f"acc2_{i}", [128, D], BF16) for i in range(2)]
    slab = lb.sbuf("slab", [128, FC * 512], BF16)
    pst = [ps[:, 2 * i, :].bitcast(BF16) for i in range(4)]
    ci = sd.dma(idt2[:], ext["ident"][:, :], *(deps + tuple(entry)))
    evs = []
    tr_ev = {}
    for mi, (src_d, dst_d) in enumerate(((wl_o, wlT), (wh_o, whT))):
        for ot in range(4):
            lda = sd.dma(acc2[ot % 2][:], src_d[ot * 128:(ot + 1) * 128, :],
                         tr_ev.get((mi, ot - 2, FC - 1)))
            for ic in range(FC):
                b = 2 * (ot % 2) + ic % 2
                mmt = pe.op(lambda q, ic=ic, b=b, ot=ot: q.transpose(
                    pst[b][:, 0:128], acc2[ot % 2][:, ic * 128:(ic + 1) * 128],
                    idt2[:]), lda, ci, tr_ev.get((mi, ot, ic - 2)),
                    tr_ev.get((mi - 1, ot, ic)) if False else None,
                    *(entry if mi == 0 and ot == 0 and ic < 2 else ()))
                tv = act.op(lambda q, ic=ic, ot=ot, b=b: q.copy(
                    slab[:, ic * 512 + ot * 128: ic * 512 + (ot + 1) * 128],
                    pst[b][:, 0:128]), mmt, evs[-1] if (ot == 0 and ic == 0 and evs) else None)
                tr_ev[(mi, ot, ic)] = tv
        evs.append(ad.dma(dst_d.rearrange("(ic p) o -> p ic o", p=128),
                          slab.rearrange("p (ic o) -> p ic o", o=512),
                          tr_ev[(mi, 3, FC - 1)]))
    lb.free("slab", "acc2_1", "acc2_0", "idt2")
    return tuple(evs)


def build_dequant():
    lb = LB("dequant")
    nc = lb.nc
    ext = {k: nc.dram_tensor(k, sh, dt, kind="ExternalInput").ap()
           for k, (sh, dt) in DEQ_EXT.items()}
    ext["wlT"] = nc.dram_tensor("wlT", [D, 512], BF16, kind="ExternalOutput").ap()
    ext["whT"] = nc.dram_tensor("whT", [D, 512], BF16, kind="ExternalOutput").ap()
    ext["wl_o"] = nc.dram_tensor("wl_o", [512, D], BF16).ap()
    ext["wh_o"] = nc.dram_tensor("wh_o", [512, D], BF16).ap()
    ps = lb.psum("ps", [128, 8, 512], F32)
    P = lb.mkqueues()
    e1, e2, _ = emit_dequant(lb, P, ext, ps)
    e3, e4 = emit_dequant_tr(lb, P, ext, ps, (e1, e2))
    P["pe"].barrier(e3, e4)
    lb.emit()
    return lb


def prep_dequant_inputs(inp, core):
    qw = inp["L1_ffn_qweight"]
    rows = slice(512 * core, 512 * (core + 1))
    return dict(
        qwp=np.ascontiguousarray(qw[:, rows, :].transpose(1, 0, 2).reshape(512, 1024)),
        maskc=np.broadcast_to((np.int32(1) << np.arange(32, dtype=np.int32))[None, :], (128, 32)).copy(),
        lut4c=np.ascontiguousarray(inp["L1_ffn_lut4"][rows].astype(np.float32)),
        lut8c=np.ascontiguousarray(inp["L1_ffn_lut8"][rows].astype(np.float32)),
        ident=np.eye(128, dtype=BF),
    )



FFN1_EXT = dict(x3=([D, S], BF16), bcols1=([128, 32], F32),
                ones_col=([128, 1], BF16), ones_row=([1, 128], BF16))


def emit_ffn1(lb, P, ext, ps, entry=()):
    x3, gl, gh = ext["x3"], ext["gl"], ext["gh"]
    bcols = ext["bcols1"]
    ones_col, ones_row = ext["ones_col"], ext["ones_row"]
    out = ext["out"]

    ln = [lb.sbuf(f"ln{i}", [128, S], BF16) for i in range(FC)]
    onc = lb.sbuf("onc", [128, 1], BF16)
    onr = lb.sbuf("onr", [1, 128], BF16)
    bcs = lb.sbuf("bcs", [128, 32], F32)
    mB = lb.sbuf("mB", [128, S], BF16)
    rB = lb.sbuf("rB", [128, S], BF16)
    maskB = lb.sbuf("maskB", [128, S], BF16)
    pe, act, dve = P["pe"], P["act"], P["dve"]
    pool = P["pool"]
    sd, gd, ad = P["sd"], P["gd"], P["ad"]

    PH1 = ["xs", "sq", "ra", "rb", "rc", "rd", "rowbt"]
    xs = lb.sbuf("xs", [128, S], BF16)
    sq = lb.sbuf("sq", [128, 512], BF16)
    ra = lb.sbuf("ra", [1, S], F32)
    rb = lb.sbuf("rb", [1, S], F32)
    rc = lb.sbuf("rc", [1, S], F32)
    rd = lb.sbuf("rd", [1, S], F32)
    rowbt = lb.sbuf("rowbt", [1, S], BF16)

    st = dict(ps=ps, xs=xs, sq=sq, onc=onc, onr=onr, rowb=rowbt,
              ra=ra, rb=rb, rc=rc)

    c1 = sd.dma(onc[:], ones_col[:, :], *entry)
    c2 = sd.dma(onr[:], ones_row[:, :])
    c5 = sd.dma(bcs[:], bcols[:, :])

    sev = emit_stats(P, st, lambda fc: x3[fc * 128:(fc + 1) * 128, :], FC,
                     dep=c5, entry=entry, tiles=ln)
    me = pool.op(lambda q: q.tensor_tensor(rc[:], ra[:], ra[:], ALU.mult), sev)
    me = pool.op(lambda q: q.tensor_scalar(rc[:], rc[:], -1.0 / D, None, ALU.mult), me)
    me = pool.op(lambda q: q.tensor_tensor(rc[:], rc[:], rb[:], ALU.add), me)
    me = pool.op(lambda q: q.tensor_scalar(rd[:], rc[:], 1.0 / D, None, ALU.mult), me)
    me = pool.op(lambda q: q.tensor_scalar(rd[:], rd[:], float(LNEPS), None, ALU.add), me)
    me = act_recip(P, rd[:], me)
    me = pool.op(lambda q: q.tensor_tensor(rd[:], rc[:], rd[:], ALU.mult), me)
    me = pool.op(lambda q: q.tensor_scalar(rd[:], rd[:], float(D), None, ALU.is_gt), me)
    mev = emit_mrstd(P, st, me)
    bm = emit_bcast_row(P, st, mev, ra[:], mB, 2)
    br = emit_bcast_row(P, st, mev, rb[:], rB, 3, war=bm)
    bk = emit_bcast_row(P, st, me, rd[:], maskB, 4, war=br)
    lev = emit_ln(P, st, x3, ln, mB, rB, br, resident=True)

    lb.free(*reversed(PH1))
    wcol = [lb.sbuf(f"wcol{i}", [128, D], BF16) for i in range(2)]
    hcol = [lb.sbuf(f"hcol{i}", [128, D], BF16) for i in range(2)]
    x3r = [lb.sbuf(f"x3r{i}", [128, S], BF16) for i in range(2)]
    o32 = [lb.sbuf(f"o32_{i}", [128, S], F32) for i in range(2)]
    ev32 = [lb.sbuf(f"ev32_{i}", [128, 512], F32) for i in range(4)]

    def gcol_load(dst, g, oc, dep, dep2=None):
        c = oc // 4
        off = (oc % 4) * 128
        src = g[c * D:(c + 1) * D, off:off + 128].rearrange("(fc p) o -> p fc o", p=128)
        return sd.dma(dst.rearrange("p (fc o) -> p fc o", o=128), src, dep, dep2)

    wld = [None, None]
    hld = [None, None]
    x3ld = [None, None]
    mm_done = {}
    evac_done = {}
    pdone = {}
    out_done = {}
    if ext.get("cc_ev") is not None:
        for _cce in ext["cc_ev"]:
            sd.op(lambda q: q.nop(), _cce)
    wld[0] = gcol_load(wcol[0], gl, 0, bk, lev)
    hld[0] = gcol_load(hcol[0], gh, 0, None)
    x3ld[0] = sd.dma(x3r[0][:], x3[0:128, :], lev)
    for oc in range(FC):
        pb = oc % 2
        if oc + 1 < FC:
            nb = (oc + 1) % 2
            wld[nb] = gcol_load(wcol[nb], gl, oc + 1, mm_done.get(oc - 1))
            hld[nb] = gcol_load(hcol[nb], gh, oc + 1, None)
            x3ld[nb] = sd.dma(x3r[nb][:], x3[(oc + 1) * 128:(oc + 2) * 128, :],
                              evac_done.get(oc - 1))
        deps = [wld[pb], hld[pb], lev, pdone.get(oc - 1), bk]
        mm = None
        for fc in range(FC):
            for qt in range(QT):
                mm = pe.op(lambda q, fc=fc, qt=qt, pb=pb: q.matmul(
                    ps[:, qt, :], wcol[pb][:, fc * 128:(fc + 1) * 128],
                    ln[fc][:, qt * 512:(qt + 1) * 512],
                    start=(fc == 0), stop=(fc == FC - 1)),
                    *(deps if fc == 0 and qt == 0 else []))
        for fc in range(FC):
            for qt in range(QT):
                mm = pe.op(lambda q, fc=fc, qt=qt, pb=pb: q.matmul(
                    ps[:, 4 + qt, :], hcol[pb][:, fc * 128:(fc + 1) * 128],
                    ln[fc][:, qt * 512:(qt + 1) * 512],
                    start=(fc == 0), stop=(fc == FC - 1)))
        mm_done[oc] = mm
        ee = None
        for qt in range(QT):
            ea = act.op(lambda q, qt=qt, oc=oc: q.activation(
                ev32[qt][:, :], ps[:, qt, :], AF.Identity,
                bias=bcs[:, oc:oc + 1]), mm, c5, evac_done.get(oc - 1))
            eh2 = act.op(lambda q, qt=qt, oc=oc, pb=pb: q.activation(
                o32[pb][:, qt * 512:(qt + 1) * 512], ps[:, 4 + qt, :],
                AF.Identity, bias=bcs[:, oc:oc + 1]),
                out_done.get(oc - 2))
            e1 = pool.op(lambda q, qt=qt, pb=pb: q.tensor_tensor(
                o32[pb][:, qt * 512:(qt + 1) * 512],
                o32[pb][:, qt * 512:(qt + 1) * 512], ev32[qt][:, :],
                ALU.subtract), ea, eh2)
            e2 = pool.op(lambda q, qt=qt, pb=pb: q.tensor_tensor(
                o32[pb][:, qt * 512:(qt + 1) * 512],
                o32[pb][:, qt * 512:(qt + 1) * 512],
                maskB[:, qt * 512:(qt + 1) * 512], ALU.mult), bk)
            e3 = pool.op(lambda q, qt=qt, pb=pb: q.tensor_tensor(
                o32[pb][:, qt * 512:(qt + 1) * 512],
                o32[pb][:, qt * 512:(qt + 1) * 512], ev32[qt][:, :], ALU.add))
            ee = pool.op(lambda q, qt=qt, pb=pb: q.tensor_tensor(
                o32[pb][:, qt * 512:(qt + 1) * 512],
                o32[pb][:, qt * 512:(qt + 1) * 512],
                x3r[pb][:, qt * 512:(qt + 1) * 512], ALU.add), x3ld[pb])
            pdone[oc] = eh2
        evac_done[oc] = ee
        out_done[oc] = ad.dma(out[oc * 128:(oc + 1) * 128, :], o32[pb][:], ee)

    return out_done[FC - 1]


def build_ffn1():
    lb = LB("ffn1")
    nc = lb.nc
    ext = {k: nc.dram_tensor(k, sh, dt, kind="ExternalInput").ap()
           for k, (sh, dt) in FFN1_EXT.items()}
    ext["gl"] = nc.dram_tensor("gl", [NCORES * D, 512], BF16, kind="ExternalInput").ap()
    ext["gh"] = nc.dram_tensor("gh", [NCORES * D, 512], BF16, kind="ExternalInput").ap()
    ext["out"] = nc.dram_tensor("out", [D, S], F32, kind="ExternalOutput").ap()
    ps = lb.psum("ps", [128, 8, 512], F32)
    P = lb.mkqueues()
    done = emit_ffn1(lb, P, ext, ps)
    P["pe"].barrier(done)
    lb.emit()
    return lb


def prep_ffn1_inputs(inp, x3_fm_bf16, gl, gh):
    d = dict(
        bcols1=np.ascontiguousarray(inp["L1_ffn_bias"].astype(np.float32).reshape(32, 128).T),
        ones_col=np.ones((128, 1), BF),
        ones_row=np.ones((1, 128), BF),
    )
    if x3_fm_bf16 is not None:
        d["x3"] = np.ascontiguousarray(x3_fm_bf16)
        d["gl"] = gl
        d["gh"] = gh
    return d



ATTN1_EXT = dict(x2=([D, S], BF16), wqt=([D, 1024], F32), bqc=([128, 8], F32),
                 mrows=([1024, 128], F32), zc=([128, 8], F32), wot=([D, D], F32),
                 ones_col=([128, 1], BF16), ones_row=([1, 128], BF16))


def emit_attn1(lb, P, ext, ps, entry=()):
    pool = None
    x2, wqt, bqc = ext["x2"], ext["wqt"], ext["bqc"]
    mrows, zc, wot = ext["mrows"], ext["zc"], ext["wot"]
    ones_col, ones_row = ext["ones_col"], ext["ones_row"]
    x3, sigq_s = ext["x3"], ext["sigq_s"]

    mcols = lb.sbuf("mcols", [128, 8 * 128], BF16)
    zcs = lb.sbuf("zcs", [128, 8], BF16)
    bqs = lb.sbuf("bqs", [128, 8], F32)
    onc = lb.sbuf("onc", [128, 1], BF16)
    onr = lb.sbuf("onr", [1, 128], BF16)
    pe, act, dve = P["pe"], P["act"], P["dve"]
    pool = P["pool"]
    sd, gd, ad = P["sd"], P["gd"], P["ad"]

    c1 = sd.dma(onc[:], ones_col[:, :], *entry)
    c2 = sd.dma(onr[:], ones_row[:, :])
    c3 = sd.dma(bqs[:], bqc[:, :])
    cm = gd.dma(mcols.rearrange("p (kv e) -> p kv e", e=128),
                mrows.rearrange("(kv p) e -> p kv e", p=128), *entry)
    cz = gd.dma(zcs[:], zc[:, :])

    PH1 = ["xs", "sq", "ra", "rb", "rc", "rowbt", "mB", "rB"] + \
          [f"lnt{i}" for i in range(FC)] + ["wq0", "wq1", "e32a", "e32b", "sgb"]
    xs = lb.sbuf("xs", [128, S], BF16)
    sq = lb.sbuf("sq", [128, 512], BF16)
    ra = lb.sbuf("ra", [1, S], F32)
    rb = lb.sbuf("rb", [1, S], F32)
    rc = lb.sbuf("rc", [1, S], F32)
    rowbt = lb.sbuf("rowbt", [1, S], BF16)
    mB = lb.sbuf("mB", [128, S], BF16)
    rB = lb.sbuf("rB", [128, S], BF16)
    ln = [lb.sbuf(f"lnt{i}", [128, S], BF16) for i in range(FC)]
    wq = [lb.sbuf("wq0", [128, D], BF16), lb.sbuf("wq1", [128, D], BF16)]
    e32 = [lb.sbuf("e32a", [128, 512], F32), lb.sbuf("e32b", [128, 512], F32)]
    sgb = lb.sbuf("sgb", [128, S], BF16)

    st = dict(ps=ps, xs=xs, sq=sq, onc=onc, onr=onr, rowb=rowbt,
              ra=ra, rb=rb, rc=rc)
    sev = emit_stats(P, st, lambda fc: x2[fc * 128:(fc + 1) * 128, :], FC,
                     dep=c3, entry=entry, tiles=ln)
    mev = emit_mrstd(P, st, sev)
    bm = emit_bcast_row(P, st, mev, ra[:], mB, 2)
    br = emit_bcast_row(P, st, mev, rb[:], rB, 3, war=bm)
    lev = emit_ln(P, st, x2, ln, mB, rB, br, resident=True)

    wld = [None, None]
    mm_done = {}
    evac_done = {}
    sg_done = {}
    wld[0] = col_load(gd, wq[0], wqt, 0, dep=cz)
    for oc in range(8):
        pb = oc % 2
        half = pb * 4
        if oc + 1 < 8:
            nb = (oc + 1) % 2
            wld[nb] = col_load(gd, wq[nb], wqt, oc + 1, dep=mm_done.get(oc - 1))
        deps = [wld[pb], lev, evac_done.get(oc - 2)]
        mm = None
        for fc in range(FC):
            for qt in range(QT):
                mm = pe.op(lambda q, fc=fc, qt=qt, pb=pb, half=half: q.matmul(
                    ps[:, half + qt, :], wq[pb][:, fc * 128:(fc + 1) * 128],
                    ln[fc][:, qt * 512:(qt + 1) * 512],
                    start=(fc == 0), stop=(fc == FC - 1)),
                    *(deps if fc == 0 and qt == 0 else []))
        mm_done[oc] = mm
        ee = None
        for qt in range(QT):
            eb = qt % 2
            ea = act.op(lambda q, qt=qt, half=half, eb=eb, oc=oc: q.activation(
                e32[eb][:, :], ps[:, half + qt, :], AF.Identity,
                bias=bqs[:, oc:oc + 1]), mm, c3, evac_done.get(oc - 1),
                sg_done.get((oc, qt - 2)))
            e1 = pool.op(lambda q, qt=qt, eb=eb: q.tensor_scalar(
                sgb[:, qt * 512:(qt + 1) * 512], e32[eb][:, :], 0.0, None, ALU.min),
                ea, sg_done.get(oc - 1) if qt == 0 else None)
            e2 = act.op(lambda q, qt=qt: q.activation(
                sgb[:, qt * 512:(qt + 1) * 512], sgb[:, qt * 512:(qt + 1) * 512],
                AF.Exp), e1)
            e3 = pool.op(lambda q, qt=qt, eb=eb: q.tensor_scalar(
                e32[eb][:, :], e32[eb][:, :], 0.0, None, ALU.max), e2)
            ee = pool.op(lambda q, qt=qt, eb=eb: q.tensor_tensor(
                sgb[:, qt * 512:(qt + 1) * 512],
                sgb[:, qt * 512:(qt + 1) * 512], e32[eb][:, :], ALU.add))
            sg_done[(oc, qt)] = ee
        evac_done[oc] = ee
        sg_done[oc] = ad.dma(sigq_s[oc * 128:(oc + 1) * 128, :], sgb[:], ee)

    lb.free(*reversed(PH1))
    ret = [lb.sbuf(f"ret{i}", [128, S], BF16) for i in range(8)]
    sgt = lb.sbuf("sgt", [128, S], BF16)
    recB = lb.sbuf("recB", [128, S], F32)
    rd2 = lb.sbuf("rd2", [1, S], F32)
    rw2 = lb.sbuf("rw2", [1, S], BF16)
    x2r = [lb.sbuf(f"x2r{i}", [128, S], BF16) for i in range(2)]
    x3t = [lb.sbuf(f"x3t{i}", [128, S], BF16) for i in range(2)]
    wo = [lb.sbuf("wo0", [128, D], BF16), lb.sbuf("wo1", [128, D], BF16)]
    f32e = [lb.sbuf(f"f32e{i}", [128, 512], F32) for i in range(4)]

    ret_done = {}
    prevkv = {}
    ph1_done = sg_done[7]
    for kv in range(8):
        sld = sd.dma(sgt[:], sigq_s[kv * 128:(kv + 1) * 128, :], ph1_done,
                     prevkv.get(kv - 1))
        mmn = None
        for qt in range(QT):
            mmn = pe.op(lambda q, qt=qt, kv=kv: q.matmul(
                ps[:, qt, :], mcols[:, kv * 128:(kv + 1) * 128],
                sgt[:, qt * 512:(qt + 1) * 512], start=True, stop=True),
                sld, cm, prevkv.get(("num", kv - 1)))
        edn = None
        for qt in range(QT):
            mmd = pe.op(lambda q, qt=qt, kv=kv: q.matmul(
                ps[0:1, 4, :], zcs[:, kv:kv + 1],
                sgt[:, qt * 512:(qt + 1) * 512], start=True, stop=True),
                edn if edn else (prevkv.get(("den", kv - 1)) or sld))
            edn = act.op(lambda q, qt=qt: q.copy(rd2[0:1, qt * 512:(qt + 1) * 512],
                                                 ps[0:1, 4, :]), mmd)
        er = pool.op(lambda q: q.tensor_scalar(rd2[:], rd2[:], 1e-6, None, ALU.add), edn)
        er = act_recip(P, rd2[:], er)
        er = pool.op(lambda q: q.tensor_copy(rw2[0:1, :], rd2[:]), er, prevkv.get(("bc", kv - 1)))
        eb2 = None
        for qt in range(QT):
            mmb = pe.op(lambda q, qt=qt: q.matmul(
                ps[:, 5, :], onr[:], rw2[0:1, qt * 512:(qt + 1) * 512],
                start=True, stop=True), er, eb2)
            eb2 = act.op(lambda q, qt=qt: q.copy(
                recB[:, qt * 512:(qt + 1) * 512], ps[:, 5, :]), mmb)
        prevkv[("bc", kv)] = eb2
        ev = None
        for qt in range(QT):
            eva = act.op(lambda q, qt=qt: q.copy(
                sgt[:, qt * 512:(qt + 1) * 512], ps[:, qt, :]), mmn, eb2,
                prevkv.get(("rm", kv - 1)))
            ev = pool.op(lambda q, qt=qt, kv=kv: q.tensor_tensor(
                ret[kv][:, qt * 512:(qt + 1) * 512],
                sgt[:, qt * 512:(qt + 1) * 512],
                recB[:, qt * 512:(qt + 1) * 512], ALU.mult), eva)
        prevkv[("rm", kv)] = ev
        prevkv[("num", kv)] = ev
        prevkv[("den", kv)] = edn
        prevkv[kv] = ev
        ret_done[kv] = ev
    all_ret = ret_done[7]

    wld = [None, None]
    mm_done = {}
    evac_done = {}
    out_done = {}
    x2ld = [None, None]
    wld[0] = col_load(gd, wo[0], wot, 0, dep=ph1_done)
    x2ld[0] = sd.dma(x2r[0][:], x2[0:128, :], all_ret)
    for oc in range(FC):
        pb = oc % 2
        half = pb * 4
        if oc + 1 < FC:
            nb = (oc + 1) % 2
            wld[nb] = col_load(gd, wo[nb], wot, oc + 1, dep=mm_done.get(oc - 1))
            x2ld[nb] = sd.dma(x2r[nb][:], x2[(oc + 1) * 128:(oc + 2) * 128, :],
                              evac_done.get(oc - 1))
        deps = [wld[pb], all_ret, evac_done.get(oc - 2)]
        mm = None
        for hc in range(FC):
            for qt in range(QT):
                mm = pe.op(lambda q, hc=hc, qt=qt, pb=pb, half=half: q.matmul(
                    ps[:, half + qt, :], wo[pb][:, hc * 128:(hc + 1) * 128],
                    ret[hc // 4][:, qt * 512:(qt + 1) * 512],
                    start=(hc == 0), stop=(hc == FC - 1)),
                    *(deps if hc == 0 and qt == 0 else []))
        mm_done[oc] = mm
        ee = None
        for qt in range(QT):
            ea = act.op(lambda q, qt=qt, half=half: q.copy(
                f32e[qt][:, :], ps[:, half + qt, :]), mm, evac_done.get(oc - 1))
            ee = pool.op(lambda q, qt=qt, pb=pb: q.tensor_tensor(
                x3t[pb][:, qt * 512:(qt + 1) * 512], f32e[qt][:, :],
                x2r[pb][:, qt * 512:(qt + 1) * 512], ALU.add),
                ea, x2ld[pb], out_done.get(oc - 2))
        evac_done[oc] = ee
        out_done[oc] = ad.dma(x3[oc * 128:(oc + 1) * 128, :], x3t[pb][:], ee)

    lb.free("f32e3", "f32e2", "f32e1", "f32e0", "wo1", "wo0", "x3t1", "x3t0",
            "x2r1", "x2r0", "rw2", "rd2", "recB", "sgt",
            "ret7", "ret6", "ret5", "ret4", "ret3", "ret2", "ret1", "ret0",
            "onr", "onc", "bqs", "zcs", "mcols")
    return out_done[FC - 1]


def build_attn1():
    lb = LB("attn1")
    nc = lb.nc
    ext = {k: nc.dram_tensor(k, sh, dt, kind="ExternalInput").ap()
           for k, (sh, dt) in ATTN1_EXT.items()}
    ext["x3"] = nc.dram_tensor("x3", [D, S], BF16, kind="ExternalOutput").ap()
    ext["sigq_s"] = nc.dram_tensor("sigq_s", [1024, S], BF16).ap()
    ps = lb.psum("ps", [128, 8, 512], F32)
    P = lb.mkqueues()
    done = emit_attn1(lb, P, ext, ps)
    P["pe"].barrier(done)
    lb.emit()
    return lb


def prep_attn1_inputs(inp, x2_fm_bf16):
    sel = np.concatenate([np.arange(h * 128, (h + 1) * 128) for h in range(0, 32, 4)])
    Wq = inp["L1_attn_Wq"].astype(np.float32)
    bq = inp["L1_attn_bq"].astype(np.float32)
    d = dict(
        wqt=np.ascontiguousarray(Wq[sel].T),
        bqc=np.ascontiguousarray(bq[sel].reshape(8, 128).T),
        mrows=np.ascontiguousarray(inp["L1_attn_M"].astype(np.float32).reshape(1024, 128)),
        zc=np.ascontiguousarray(inp["L1_attn_z"].astype(np.float32).T),
        wot=np.ascontiguousarray(inp["L1_attn_Wo"].astype(np.float32).T),
        ones_col=np.ones((128, 1), BF),
        ones_row=np.ones((1, 128), BF),
    )
    if x2_fm_bf16 is not None:
        d["x2"] = np.ascontiguousarray(x2_fm_bf16)
    return d



ATTN0_EXT = dict(xf=([D, S], F32), wqkvt=([D, 3 * D], F32), bqkc=([128, 96], F32),
                 wout=([D, D], F32), boc=([128, 32], F32), alibi=([128, 512], F32),
                 masks=([128, 4 * 512], BF16), qcorr=([32, S], BF16),
                 identm=([128, 128], BF16), ones_col=([128, 1], BF16),
                 ones_row=([1, 128], BF16))


def emit_attn0(lb, P, ext, ps, entry=(), nheads=32, mid_hook=None):
    no_tr = no_mask = no_den = no_av = False
    xf, wqkvt, bqkc, wout, boc = (ext["xf"], ext["wqkvt"], ext["bqkc"],
                                  ext["wout"], ext["boc"])
    alibi, masks, qcorr, identm = (ext["alibi"], ext["masks"], ext["qcorr"],
                                   ext["identm"])
    ones_col, ones_row = ext["ones_col"], ext["ones_row"]
    x1, qkv_s = ext["x1"], ext["qkv_s"]
    pool = P["pool"]

    alb = lb.sbuf("alb", [128, 512], F32)
    msk = lb.sbuf("msk", [128, 4 * 512], BF16)
    onc = lb.sbuf("onc", [128, 1], BF16)
    onr = lb.sbuf("onr", [1, 128], BF16)
    bqk = lb.sbuf("bqk", [128, 96], F32)
    bos = lb.sbuf("bos", [128, 32], F32)
    idt = lb.sbuf("idt", [128, 128], BF16)
    pe, act, dve = P["pe"], P["act"], P["dve"]
    sd, gd, ad = P["sd"], P["gd"], P["ad"]

    c1 = sd.dma(onc[:], ones_col[:, :], *entry)
    c2 = sd.dma(onr[:], ones_row[:, :])
    c3 = sd.dma(bqk[:], bqkc[:, :])
    c4 = sd.dma(alb[:], alibi[:, :])
    c5 = sd.dma(msk[:], masks[:, :])
    c6 = sd.dma(bos[:], boc[:, :])
    c8 = sd.dma(idt[:], identm[:, :])

    deq_bufs = alloc_dequant(lb) if mid_hook is not None else None
    ln = [lb.sbuf(f"ln{i}", [128, S], BF16) for i in range(FC)]
    PH0 = ["sq", "ra", "rb", "rc", "rowbt", "mB", "rB"]
    sq = lb.sbuf("sq", [128, 512], BF16)
    ra = lb.sbuf("ra", [1, S], F32)
    rb = lb.sbuf("rb", [1, S], F32)
    rc = lb.sbuf("rc", [1, S], F32)
    rowbt = lb.sbuf("rowbt", [1, S], BF16)
    mB = lb.sbuf("mB", [128, S], BF16)
    rB = lb.sbuf("rB", [128, S], BF16)
    st = dict(ps=ps, xs=None, sq=sq, onc=onc, onr=onr,
              rowb=rowbt, ra=ra, rb=rb, rc=rc)
    sev = emit_stats(P, st, lambda fc: xf[fc * 128:(fc + 1) * 128, :], FC,
                     dep=c8, from_f32=True, entry=entry, tiles=ln)
    mev = emit_mrstd(P, st, sev)
    bm = emit_bcast_row(P, st, mev, ra[:], mB, 2)
    br = emit_bcast_row(P, st, mev, rb[:], rB, 3, war=bm)
    lev = emit_ln(P, st, xf, ln, mB, rB, br, resident=True)
    lb.free(*reversed(PH0))

    deq_gen, deq_done, deq_names = None, (), None
    if mid_hook is not None:
        deq_gen = mid_hook(lev, deq_bufs)
        next(deq_gen, None)

    PH1 = ["wc0", "wc1", "qkb0", "qkb1"]
    wc = [lb.sbuf("wc0", [128, D], BF16), lb.sbuf("wc1", [128, D], BF16)]
    qkb = [lb.sbuf("qkb0", [128, S], BF16), lb.sbuf("qkb1", [128, S], BF16)]

    wld = [None, None]
    mm_done = {}
    evac_done = {}
    out_done = {}
    wld[0] = col_load(gd, wc[0], wqkvt, 0, 128, lev, *entry)
    for oc in range(96):
        if deq_gen is not None and oc >= 12 and (oc - 12) % 9 == 0:
            next(deq_gen, None)
        pb = oc % 2
        if oc + 1 < 96:
            nb = (oc + 1) % 2
            wld[nb] = col_load(gd, wc[nb], wqkvt, oc + 1, dep=mm_done.get(oc - 1))
        half = pb * 4
        deps = [wld[pb], lev, evac_done.get(oc - 2)]
        mm = None
        for fc in range(FC):
            for qt in range(QT):
                mm = pe.op(lambda q, fc=fc, qt=qt, pb=pb, half=half: q.matmul(
                    ps[:, half + qt, :], wc[pb][:, fc * 128:(fc + 1) * 128],
                    ln[fc][:, qt * 512:(qt + 1) * 512],
                    start=(fc == 0), stop=(fc == FC - 1)),
                    *(deps if fc == 0 and qt == 0 else []))
        ee = None
        for qt in range(QT):
            ee = act.op(lambda q, qt=qt, half=half, oc=oc, pb=pb: q.activation(
                qkb[pb][:, qt * 512:(qt + 1) * 512], ps[:, half + qt, :],
                AF.Identity, bias=bqk[:, oc:oc + 1]), mm, c3,
                evac_done.get(oc - 1), out_done.get(oc - 2))
            evac_done[(oc, qt)] = ee
        evac_done[oc] = ee
        mm_done[oc] = mm
        out_done[oc] = ad.dma(qkv_s[oc * 128:(oc + 1) * 128, :], qkb[pb][:],
                              evac_done[oc])
    qkv_done = out_done[95]
    pe_a1 = pe.ev()
    act_a1 = act.ev()
    lb.free(*reversed(PH1))
    lb.free(*[f"ln{i}" for i in reversed(range(FC))])

    SCL = float(1.0 / np.sqrt(128.0))
    afm = [lb.sbuf(f"afm{i}", [128, S], BF16) for i in range(FC)]
    PH2 = ["qh_sb", "qcr", "kh_sb", "vh_sb", "vht", "ex0", "ex1", "rdn", "rwb",
           "recB", "ava"]
    qh_sb = lb.sbuf("qh_sb", [128, S], BF16)
    qcr = lb.sbuf("qcr", [1, S], BF16)
    kh_sb = lb.sbuf("kh_sb", [128, S], BF16)
    vh_sb = lb.sbuf("vh_sb", [128, S], BF16)
    vht = lb.sbuf("vht", [128, S], BF16)
    ex = [lb.sbuf("ex0", [128, 512], BF16), lb.sbuf("ex1", [128, 512], BF16)]
    rdn = lb.sbuf("rdn", [1, S], F32)
    rwb = lb.sbuf("rwb", [1, S], BF16)
    recB = lb.sbuf("recB", [128, 512], F32)
    ava = lb.sbuf("ava", [128, 512], F32)

    hd_ld = {}
    hev = {}
    for h in range(nheads):
        if deq_gen is not None and h in (1, 4, 6, 9, 11, 14):
            next(deq_gen, None)
        lq = sd.dma(qh_sb[:], qkv_s[h * 128:(h + 1) * 128, :],
                    qkv_done, hev.get(("smm", h - 1)),
                    pe_a1 if h == 0 else None, act_a1 if h == 0 else None)
        lk = sd.dma(kh_sb[:], qkv_s[D + h * 128:D + (h + 1) * 128, :])
        lqc = sd.dma(qcr[:], qcorr[h:h + 1, :], hev.get(("smm", h - 1)))
        lv = sd.dma(vh_sb[:], qkv_s[2 * D + h * 128:2 * D + (h + 1) * 128, :],
                    hev.get(("tr", h - 1)))
        tv = None
        pstv = [ps[:, 4, :].bitcast(BF16), ps[:, 5, :].bitcast(BF16)]
        for kc in range(0 if no_tr else 16):
            tb = kc % 2
            mmt = pe.op(lambda q, kc=kc, tb=tb, pstv=pstv: q.transpose(
                pstv[tb][:, 0:128],
                vh_sb[:, kc * 128:(kc + 1) * 128], idt[:]),
                lv, c8, hev.get(("av", h - 1)), hev.get(("trev", (h, kc - 2))),
                hev.get(("rb", (h - 1, 3))))
            tv = act.op(lambda q, kc=kc, tb=tb, pstv=pstv: q.copy(
                vht[:, kc * 128:(kc + 1) * 128], pstv[tb][:, 0:128]), mmt)
            hev[("trev", (h, kc))] = tv
        hev[("tr", h)] = tv
        smm_last = None
        for qtg in range(4):
            kmax = 4 * qtg + 4

            def smm_emit(kc, qtg=qtg, h=h):
                sb = kc % 2
                diag = kc >= 4 * qtg
                pe.op(lambda q, kc=kc, sb=sb, qtg=qtg: q.matmul(
                    ps[:, sb, :], kh_sb[:, kc * 128:(kc + 1) * 128],
                    qh_sb[:, qtg * 512:(qtg + 1) * 512], start=True, stop=False),
                    lq, lk, hev.get(("exp", (h, qtg, kc - 2))),
                    hev.get(("sprev", h)) if qtg == 0 and kc == 0 else None)
                mm = pe.op(lambda q, sb=sb, qtg=qtg, diag=diag: q.matmul(
                    ps[:, sb, :], onr[:], qcr[0:1, qtg * 512:(qtg + 1) * 512],
                    start=False, stop=not diag), lqc)
                if diag:
                    v = kc - 4 * qtg
                    mm = pe.op(lambda q, sb=sb, v=v: q.matmul(
                        ps[:, sb, :], idt[:], msk[:, v * 512:(v + 1) * 512],
                        start=False, stop=True), c5, c8)
                return mm

            def exp_emit(kc, smm, qtg=qtg, h=h):
                sb = kc % 2
                eex = act.op(lambda q, kc=kc, sb=sb, h=h: q.activation(
                    ex[sb][:, :], ps[:, sb, :], AF.Exp, scale=SCL,
                    bias=alb[:, h * 16 + kc:h * 16 + kc + 1]),
                    smm, c4, hev.get(("avrd", (h, qtg, kc - 2))))
                hev[("exp", (h, qtg, kc))] = eex
                return eex

            def avden_emit(kc, qtg=qtg, h=h, kmax=kmax):
                sb = kc % 2
                eex = hev[("exp", (h, qtg, kc))]
                pe.op(lambda q, kc=kc, sb=sb, kmax=kmax: q.matmul(
                    ps[:, 2, :], vht[:, kc * 128:(kc + 1) * 128], ex[sb][:, :],
                    start=(kc == 0), stop=(kc == kmax - 1)),
                    eex, tv, hev.get(("attn", (h, qtg - 1))) if kc == 0 else None)
                den_mm = pe.op(lambda q, kc=kc, sb=sb, kmax=kmax: q.matmul(
                    ps[0:1, 3, :], onc[:], ex[sb][:, :],
                    start=(kc == 0), stop=(kc == kmax - 1)),
                    hev.get(("dene", (h, qtg - 1))) if kc == 0 else None)
                hev[("avrd", (h, qtg, kc))] = den_mm
                return den_mm

            for kc in range(kmax):
                smm = smm_emit(kc)
                smm_last = smm
                if kc >= 1:
                    avden_emit(kc - 1)
                exp_emit(kc, smm)
            den_mm = avden_emit(kmax - 1)
            av_last = den_mm
            ed = act.op(lambda q, qtg=qtg: q.copy(rdn[0:1, qtg * 512:(qtg + 1) * 512],
                                                  ps[0:1, 3, :]), den_mm)
            hev[("dene", (h, qtg))] = ed
            er = act_recip(P, rdn[0:1, qtg * 512:(qtg + 1) * 512], ed)
            er = pool.op(lambda q, qtg=qtg: q.tensor_copy(
                rwb[0:1, qtg * 512:(qtg + 1) * 512],
                rdn[0:1, qtg * 512:(qtg + 1) * 512]), er)
            bmm = pe.op(lambda q, qtg=qtg: q.matmul(
                ps[:, 4, :], onr[:], rwb[0:1, qtg * 512:(qtg + 1) * 512],
                start=True, stop=True), er, hev.get(("rb", (h, qtg - 1))),
                hev.get(("tr", h)))
            eb = act.op(lambda q: q.copy(recB[:, :], ps[:, 4, :]), bmm)
            hev[("rb", (h, qtg))] = eb
            eva = act.op(lambda q, qtg=qtg: q.copy(ava[:, :], ps[:, 2, :]),
                         av_last, hev.get(("ava", (h, qtg - 1))))
            ea = pool.op(lambda q, h=h, qtg=qtg: q.tensor_tensor(
                afm[h][:, qtg * 512:(qtg + 1) * 512], ava[:, :], recB[:, :],
                ALU.mult), eva, eb)
            hev[("ava", (h, qtg))] = ea
            hev[("attn", (h, qtg))] = ea
        hev[("smm", h)] = smm_last
        hev[("av", h)] = hev[("attn", (h, 3))]
        hev[("sprev", h + 1)] = hev[("exp", (h, 3, 15))]
    attn_all = hev[("attn", (nheads - 1, 3))]
    pe_a2 = pe.ev()
    act_a2 = act.ev()
    if deq_gen is not None:
        for _ in deq_gen:
            pass
        deq_done = ext["deq_done"]
        deq_names = ext["deq_names"]
    lb.free(*reversed(PH2))

    wc3 = [lb.sbuf("wo0", [128, D], BF16), lb.sbuf("wo1", [128, D], BF16)]
    xr = [lb.sbuf(f"xr{i}", [128, S], F32) for i in range(2)]
    x1t = [lb.sbuf(f"x1t{i}", [128, S], BF16) for i in range(2)]
    f32e = [lb.sbuf(f"f32o{i}", [128, 512], F32) for i in range(2)]
    wld = [None, None]
    mm_done = {}
    evac_done = {}
    out_done = {}
    xld = [None, None]
    wld[0] = col_load(gd, wc3[0], wout, 0, 128, attn_all, pe_a2, act_a2)
    xld[0] = sd.dma(xr[0][:], xf[0:128, :], attn_all, pe_a2, act_a2,
                    *deq_done)
    for oc in range(FC):
        pb = oc % 2
        if oc + 1 < FC:
            nb = (oc + 1) % 2
            wld[nb] = col_load(gd, wc3[nb], wout, oc + 1, dep=mm_done.get(oc - 1))
            xld[nb] = sd.dma(xr[nb][:], xf[(oc + 1) * 128:(oc + 2) * 128, :],
                             evac_done.get(oc - 1))
        half = pb * 4
        deps = [wld[pb], attn_all, evac_done.get(oc - 2)]
        mm = None
        for hc in range(FC):
            for qt in range(QT):
                mm = pe.op(lambda q, hc=hc, qt=qt, pb=pb, half=half: q.matmul(
                    ps[:, half + qt, :], wc3[pb][:, hc * 128:(hc + 1) * 128],
                    afm[hc][:, qt * 512:(qt + 1) * 512],
                    start=(hc == 0), stop=(hc == FC - 1)),
                    *(deps if hc == 0 and qt == 0 else []))
        ee = None
        ees = {}
        for qt in range(QT):
            ea = act.op(lambda q, qt=qt, half=half, oc=oc: q.activation(
                f32e[qt % 2][:, :], ps[:, half + qt, :], AF.Identity,
                bias=bos[:, oc:oc + 1]), mm, c6, evac_done.get(oc - 1),
                out_done.get(oc - 2), ees.get(qt - 2))
            ee = pool.op(lambda q, qt=qt, pb=pb: q.tensor_tensor(
                x1t[pb][:, qt * 512:(qt + 1) * 512], f32e[qt % 2][:, :],
                xr[pb][:, qt * 512:(qt + 1) * 512], ALU.add), ea, xld[pb])
            ees[qt] = ee
        evac_done[oc] = ee
        mm_done[oc] = mm
        out_done[oc] = ad.dma(x1[oc * 128:(oc + 1) * 128, :], x1t[pb][:], ee)

    lb.free("f32o1", "f32o0", "x1t1", "x1t0", "xr1", "xr0",
            "wo1", "wo0")
    for i in reversed(range(FC)):
        lb.free(f"afm{i}")
    if deq_names:
        lb.free(*reversed(deq_names))
    lb.free("idt", "bos", "bqk", "onr", "onc", "msk", "alb")
    return out_done[FC - 1]


def build_attn0():
    lb = LB("attn0")
    nc = lb.nc
    ext = {k: nc.dram_tensor(k, sh, dt, kind="ExternalInput").ap()
           for k, (sh, dt) in ATTN0_EXT.items()}
    ext["x1"] = nc.dram_tensor("x1", [D, S], BF16, kind="ExternalOutput").ap()
    ext["qkv_s"] = nc.dram_tensor("qkv_s", [3 * D, S], BF16)
    ps = lb.psum("ps", [128, 8, 512], F32)
    P = lb.mkqueues()
    done = emit_attn0(lb, P, ext, ps)
    P["pe"].barrier(done)
    lb.emit()
    return lb


def prep_attn0_inputs(inp, core):
    x = inp["L0_attn_x"][core].astype(np.float32)
    wqkv = inp["L0_attn_w_qkv"].astype(np.float32)
    bqkv = inp["L0_attn_b_qkv"].astype(np.float32)
    slopes = 1.0 / np.power(2.0, np.arange(1, 33, dtype=np.float32) * 0.25)
    alibi = np.zeros((128, 512), np.float32)
    for h in range(32):
        for kc in range(16):
            k = 128 * kc + np.arange(128)
            alibi[:, h * 16 + kc] = slopes[h] * (k - 2047.0)
    masks = np.zeros((128, 4 * 512), BF)
    for v in range(4):
        for p in range(128):
            js = np.arange(512)
            masks[p, v * 512 + js] = np.where(js >= 128 * v + p, 0.0, -1.0e9)
    qs = np.arange(2048, dtype=np.float32)
    qcorr = np.stack([(-slopes[h] * (qs - 2047.0) - 8.0) * np.sqrt(128.0)
                      for h in range(32)]).astype(BF)
    return dict(
        xf=np.ascontiguousarray(x.T),
        wqkvt=np.ascontiguousarray(wqkv.T),
        bqkc=np.ascontiguousarray(bqkv.reshape(96, 128).T),
        wout=np.ascontiguousarray(inp["L0_attn_w_out"].astype(np.float32).T),
        boc=np.ascontiguousarray(inp["L0_attn_b_out"].astype(np.float32).reshape(32, 128).T),
        alibi=alibi,
        masks=masks,
        qcorr=qcorr,
        identm=np.eye(128, dtype=BF),
        ones_col=np.ones((128, 1), BF),
        ones_row=np.ones((1, 128), BF),
    )


# version 10
# speedup vs baseline: 1.1495x; 1.1495x over previous
import numpy as np
import ml_dtypes
import concourse.bass as bass
import concourse.mybir as mybir
from concourse.bass_utils import run_bass_kernel_spmd

F32 = mybir.dt.float32
BF16 = mybir.dt.bfloat16
I32 = mybir.dt.int32
AF = mybir.ActivationFunctionType
ALU = mybir.AluOpType
BF = ml_dtypes.bfloat16

NCORES = 8
S = 2048
D = 4096
FC = D // 128
QT = S // 512
LNEPS = 1e-5


_GSEQ = [0]


class Q:

    def __init__(self, name, sem, kind):
        self.name, self.sem, self.kind = name, sem, kind
        self.n = 0
        self.waited = {}
        self.items = []

    def _need_wait(self, sem, val):
        key = id(sem)
        if self.waited.get(key, 0) >= val:
            return False
        self.waited[key] = val
        return True

    def op(self, fn, *evs):
        waits = []
        for e in evs:
            if e is None:
                continue
            sem, val = e
            if sem is self.sem:
                if val <= self.n:
                    continue
                raise RuntimeError(f"{self.name}: wait on own future event")
            if self._need_wait(sem, val):
                waits.append((sem, val))
        if self.kind in ("cmp", "dma") and self.n > 0:
            if self._need_wait(self.sem, self.n):
                waits.append((self.sem, self.n))
        step = 16 if self.kind == "dma" else 1
        self.n += step
        _GSEQ[0] += 1
        self.items.append((_GSEQ[0], waits, fn, step))
        return (self.sem, self.n)

    def ev(self):
        return (self.sem, self.n) if self.n > 0 else None

    def barrier(self, *evs):
        return self.op(lambda q: q.nop(), *evs)

    def dma(self, out, in_, *evs, **kw):
        return self.op(lambda q: q.dma_start(out=out, in_=in_, **kw), *evs)

    def replay(self, q):
        replay_merged(q, [self])


def replay_merged(q, queues):
    items = []
    for qu in queues:
        items += [(seq, waits, fn, step, qu.sem) for seq, waits, fn, step in qu.items]
    items.sort(key=lambda x: x[0])
    for seq, waits, fn, step, sem in items:
        for wsem, val in waits:
            q.wait_ge(wsem, val)
        fn(q).then_inc(sem, step)


class LB:

    def __init__(self, name):
        self.nc = bass.Bass("TRN2", target_bir_lowering=False)
        self.name = name
        self._ctx = []
        self._by_name = {}

    def _enter(self, cm):
        h = cm.__enter__()
        self._ctx.append(cm)
        return h

    def sbuf(self, name, shape, dtype):
        self._uid = getattr(self, "_uid", 0) + 1
        h = self._enter(self.nc.sbuf_tensor(f"{name}_u{self._uid}", shape, dtype))
        self._by_name.setdefault(name, []).append(self._ctx[-1])
        return h

    def free(self, *names):
        for name in names:
            cm = self._by_name[name].pop()
            self._ctx.remove(cm)
            cm.__exit__(None, None, None)

    def psum(self, name, shape, dtype=F32):
        return self._enter(self.nc.psum_tensor(name, shape, dtype))

    def mkqueues(self):
        self.q = {}
        for k, kind in (("pe", "pe"), ("act", "cmp"), ("dve", "cmp"),
                        ("pool", "cmp"), ("sd", "dma"), ("gd", "dma"),
                        ("ad", "dma")):
            sem = self._enter(self.nc.semaphore("sem_" + k))
            self.q[k] = Q(k, sem, kind)
        return self.q

    def emit(self):
        blk = self._enter(self.nc.Block())
        q = self.q

        @blk.tensor
        def _(e):
            q["pe"].replay(e)

        @blk.scalar
        def _(e):
            q["act"].replay(e)

        @blk.vector
        def _(e):
            q["dve"].replay(e)

        @blk.gpsimd
        def _(e):
            replay_merged(e, [q["pool"], q["gd"]])

        @blk.sync
        def _(e):
            replay_merged(e, [q["sd"], q["ad"]])

        for t in reversed(self._ctx):
            t.__exit__(None, None, None)
        self._ctx = []



def col_load(gd, dst, wT, oc, ncols=128, dep=None, *deps):
    src = wT[:, oc * ncols:(oc + 1) * ncols].rearrange("(fc p) o -> p fc o", p=128)
    return gd.dma(dst.rearrange("p (fc o) -> p fc o", o=ncols), src, dep, *deps)


def emit_stats(P, st, src_view, n_fc, dep=None, from_f32=False, entry=(),
               tiles=None):
    pe, act, dve, sd = P["pe"], P["act"], P["dve"], P["sd"]
    pool = P["pool"]
    ps, xs, sq, onc = st["ps"], st["xs"], st["sq"], st["onc"]
    last_mm = None
    consumed = dep
    for fc in range(n_fc):
        if tiles is not None:
            if from_f32:
                cv = P["gd"].dma(tiles[fc][:], src_view(fc),
                                 *(entry if fc == 0 else ()))
            else:
                qd = sd if fc % 2 == 0 else P["gd"]
                cv = qd.dma(tiles[fc][:], src_view(fc),
                            *(entry if fc < 2 else ()))
            xb = tiles[fc]
        elif from_f32:
            ld = sd.dma(st["xsf"][:], src_view(fc), consumed,
                        *(entry if fc == 0 else ()))
            cv = pool.op(lambda q: q.tensor_copy(xs[:], st["xsf"][:]), ld)
            xb = xs
        else:
            ld = sd.dma(xs[:], src_view(fc), consumed,
                        *(entry if fc == 0 else ()))
            xb, cv = xs, ld
        for qt in range(QT):
            sqe = dve.op(lambda q, qt=qt, xb=xb: q.tensor_tensor(
                sq[:], xb[:, qt * 512:(qt + 1) * 512],
                xb[:, qt * 512:(qt + 1) * 512], ALU.mult), cv, last_mm)
            pe.op(lambda q, qt=qt, fc=fc, xb=xb: q.matmul(
                ps[0:1, qt, :], onc[:], xb[:, qt * 512:(qt + 1) * 512],
                start=(fc == 0), stop=(fc == n_fc - 1)), cv, dep,
                *(entry if fc == 0 else ()))
            last_mm = pe.op(lambda q, qt=qt, fc=fc: q.matmul(
                ps[0:1, 4 + qt, :], onc[:], sq[:],
                start=(fc == 0), stop=(fc == n_fc - 1)), sqe)
        consumed = last_mm
    ev = None
    for qt in range(QT):
        act.op(lambda q, qt=qt: q.copy(st["ra"][:, qt * 512:(qt + 1) * 512],
                                       ps[0:1, qt, :]), last_mm)
        ev = act.op(lambda q, qt=qt: q.copy(st["rb"][:, qt * 512:(qt + 1) * 512],
                                            ps[0:1, 4 + qt, :]), last_mm)
    return ev


def emit_mrstd(P, st, stats_ev, nfeat=D, eps=LNEPS):
    act, dve = P["act"], P["dve"]
    ra, rb, rc = st["ra"], st["rb"], st["rc"]
    ev = dve.op(lambda q: q.tensor_scalar(ra[:], ra[:], 1.0 / nfeat, None, ALU.mult), stats_ev)
    ev = dve.op(lambda q: q.tensor_tensor(rc[:], ra[:], ra[:], ALU.mult), ev)
    ev = dve.op(lambda q: q.tensor_scalar(rb[:], rb[:], 1.0 / nfeat, None, ALU.mult), ev)
    ev = dve.op(lambda q: q.tensor_tensor(rb[:], rb[:], rc[:], ALU.subtract), ev)
    ev = dve.op(lambda q: q.tensor_scalar(rb[:], rb[:], float(eps), None, ALU.add), ev)
    ev = act.op(lambda q: q.activation(rb[:], rb[:], AF.Ln), ev)
    ev = act.op(lambda q: q.activation(rb[:], rb[:], AF.Exp, scale=-0.5), ev)
    return ev


def emit_bcast_row(P, st, src_ev, row_ap, dst, bank, nrows=128, war=None):
    pe, act, dve = P["pe"], P["act"], P["dve"]
    ps, onr = st["ps"], st["onr"]
    rowb = st["rowb"][0:1, :]
    cv = dve.op(lambda q: q.tensor_copy(rowb, row_ap), src_ev, war)
    ev = None
    for qt in range(QT):
        mm = pe.op(lambda q, qt=qt: q.matmul(
            ps[0:nrows, bank, :], onr[0:1, 0:nrows],
            rowb[:, qt * 512:(qt + 1) * 512], start=True, stop=True),
            cv, ev, war)
        ev = act.op(lambda q, qt=qt: q.copy(
            dst[0:nrows, qt * 512:(qt + 1) * 512], ps[0:nrows, bank, :]), mm)
    return ev


def emit_rowsum8(P, st, src_b, dst_row, bank, deps):
    pe, act = P["pe"], P["act"]
    ps, onc = st["ps"], st["onc"]
    ev = None
    for qt in range(QT):
        mm = pe.op(lambda q, qt=qt: q.matmul(
            ps[0:1, bank, :], onc[0:8, :], src_b[0:8, qt * 512:(qt + 1) * 512],
            start=True, stop=True), *( list(deps) + [ev]))
        ev = act.op(lambda q, qt=qt: q.copy(
            dst_row[0:1, qt * 512:(qt + 1) * 512], ps[0:1, bank, :]), mm)
    return ev


def act_recip(P, row_ap, *evs):
    act = P["act"]
    ev = act.op(lambda q: q.activation(row_ap, row_ap, AF.Ln), *evs)
    return act.op(lambda q: q.activation(row_ap, row_ap, AF.Exp, scale=-1.0), ev)


def emit_ln(P, st, x_dram, ln_tiles, mB, rB, dep, n_fc=FC, from_f32=False,
            resident=False):
    pool, sd, dve = P["pool"], P["sd"], P["dve"]
    ev = dep
    if resident:
        for fc in range(n_fc):
            e1 = dve.op(lambda q, fc=fc: q.tensor_tensor(
                ln_tiles[fc][:], ln_tiles[fc][:], mB[:], ALU.subtract), dep)
            ev = dve.op(lambda q, fc=fc: q.tensor_tensor(
                ln_tiles[fc][:], ln_tiles[fc][:], rB[:], ALU.mult), e1)
        return ev
    xb = [st.get("xsf"), st.get("xsf2", st.get("xsf"))] if from_f32 else None
    nb = 2 if (from_f32 and "xsf2" in st) else 1
    for fc in range(n_fc):
        if from_f32:
            ld = sd.dma(xb[fc % 2][:], x_dram[fc * 128:(fc + 1) * 128, :],
                        st.get(("lncv", fc - nb)),
                        dep if fc < nb else None)
            ld = pool.op(lambda q, fc=fc: q.tensor_copy(ln_tiles[fc][:], xb[fc % 2][:]), ld)
            st[("lncv", fc)] = ld
        else:
            ld = sd.dma(ln_tiles[fc][:], x_dram[fc * 128:(fc + 1) * 128, :])
        e1 = pool.op(lambda q, fc=fc: q.tensor_tensor(ln_tiles[fc][:], ln_tiles[fc][:], mB[:], ALU.subtract), ld, dep)
        ev = pool.op(lambda q, fc=fc: q.tensor_tensor(ln_tiles[fc][:], ln_tiles[fc][:], rB[:], ALU.mult), e1)
    return ev



FFN0_EXT = dict(x1=([D, S], BF16), wbt=([D, D], F32), bcols=([128, 32], F32),
                atp=([128, D], F32), gtp=([128, FC * 16], F32), bp=([128, D], F32),
                pat8=([8, 128], BF16), ones_col=([128, 1], BF16),
                ones_row=([1, 128], BF16), bthr=([1, 16], F32))


def emit_ffn0(lb, P, ext, ps, entry=()):
    x1, wbt, bcols, atp = ext["x1"], ext["wbt"], ext["bcols"], ext["atp"]
    gtp, bp, pat8 = ext["gtp"], ext["bp"], ext["pat8"]
    ones_col, ones_row, bthr = ext["ones_col"], ext["ones_row"], ext["bthr"]
    x2 = ext["x2"]

    ln = [lb.sbuf(f"ln{i}", [128, S], BF16) for i in range(FC)]
    xs = lb.sbuf("xs", [128, S], BF16)
    onc = lb.sbuf("onc", [128, 1], BF16)
    onr = lb.sbuf("onr", [1, 128], BF16)
    p8 = lb.sbuf("p8", [8, 128], BF16)
    bth = lb.sbuf("bth", [1, 16], F32)
    bcs = lb.sbuf("bcs", [128, 32], F32)
    hw_sb = lb.sbuf("hw_sb", [128, S], BF16)
    PH1 = ["sq", "ra", "rb", "rc", "rowbt", "mB", "rB", "acol", "gcol",
           "h_sb", "g_f", "g_b", "recB", "thrB", "wBq"]
    sq = lb.sbuf("sq", [128, 512], BF16)
    ra = lb.sbuf("ra", [1, S], F32)
    rb = lb.sbuf("rb", [1, S], F32)
    rc = lb.sbuf("rc", [1, S], F32)
    rowbt = lb.sbuf("rowbt", [1, S], BF16)
    mB = lb.sbuf("mB", [128, S], BF16)
    rB = lb.sbuf("rB", [128, S], BF16)
    acol = lb.sbuf("acol", [128, D], BF16)
    gcol = lb.sbuf("gcol", [128, FC * 16], BF16)
    h_sb = lb.sbuf("h_sb", [128, S], BF16)
    g_f = lb.sbuf("g_f", [8, S], F32)
    g_b = lb.sbuf("g_b", [8, S], BF16)
    recB = lb.sbuf("recB", [8, S], BF16)
    thrB = lb.sbuf("thrB", [8, S], BF16)
    wBq = lb.sbuf("wBq", [128, 512], BF16)
    pe, act, dve = P["pe"], P["act"], P["dve"]
    sd, gd, ad = P["sd"], P["gd"], P["ad"]

    st = dict(ps=ps, xs=xs, sq=sq, onc=onc, onr=onr, rowb=rowbt,
              ra=ra, rb=rb, rc=rc)

    c1 = sd.dma(onc[:], ones_col[:, :], *entry)
    c2 = sd.dma(onr[:], ones_row[:, :])
    c3 = sd.dma(p8[:], pat8[:, :])
    c4 = sd.dma(bth[:], bthr[:, :])
    c5 = sd.dma(bcs[:], bcols[:, :])
    a_ld = gd.dma(acol[:], atp[:, :], *entry)
    g_ld = gd.dma(gcol[:], gtp[:, :])

    sev = emit_stats(P, st, lambda fc: x1[fc * 128:(fc + 1) * 128, :], FC,
                     dep=c5, entry=entry, tiles=ln)
    mev = emit_mrstd(P, st, sev)
    bm = emit_bcast_row(P, st, mev, ra[:], mB, 2)
    br = emit_bcast_row(P, st, mev, rb[:], rB, 3, war=bm)
    lev = emit_ln(P, st, x1, ln, mB, rB, br, resident=True)

    mm_h = None
    for fc in range(FC):
        for qt in range(QT):
            mm_h = pe.op(lambda q, fc=fc, qt=qt: q.matmul(
                ps[:, 4 + qt, :], acol[:, fc * 128:(fc + 1) * 128],
                ln[fc][:, qt * 512:(qt + 1) * 512],
                start=(fc == 0), stop=(fc == FC - 1)),
                *([a_ld, lev] if fc == 0 and qt == 0 else []))
    eh = None
    for qt in range(QT):
        eh = act.op(lambda q, qt=qt: q.copy(h_sb[:, qt * 512:(qt + 1) * 512],
                                            ps[:, 4 + qt, :]), mm_h)
    mm_g = None
    for fc in range(FC):
        for qt in range(QT):
            mm_g = pe.op(lambda q, fc=fc, qt=qt: q.matmul(
                ps[0:8, qt, :], gcol[:, fc * 16:fc * 16 + 8],
                ln[fc][:, qt * 512:(qt + 1) * 512],
                start=(fc == 0), stop=(fc == FC - 1)),
                *([g_ld, lev, bm] if fc == 0 and qt == 0 else []))
    eg = None
    for qt in range(QT):
        eg = act.op(lambda q, qt=qt: q.copy(g_f[:, qt * 512:(qt + 1) * 512],
                                            ps[0:8, qt, :]), mm_g)
    mm_t = None
    for fc in range(FC):
        for qt in range(QT):
            mm_t = pe.op(lambda q, fc=fc, qt=qt: q.matmul(
                ps[0:1, qt, :], gcol[:, fc * 16 + 8:fc * 16 + 9],
                ln[fc][:, qt * 512:(qt + 1) * 512],
                start=(fc == 0), stop=(fc == FC - 1)),
                *([eg] if fc == 0 and qt == 0 else []))
    etv = None
    for qt in range(QT):
        etv = act.op(lambda q, qt=qt: q.copy(ra[0:1, qt * 512:(qt + 1) * 512],
                                             ps[0:1, qt, :]), mm_t, bm)

    pool = P["pool"]
    e1 = act.op(lambda q: q.activation(g_f[0:8, :], g_f[0:8, :], AF.Exp), eg)
    e2 = act.op(lambda q: q.activation(ra[:], ra[:], AF.Identity, bias=bth[0:1, 0:1]), etv, c4)
    e3 = act.op(lambda q: q.activation(ra[:], ra[:], AF.Exp, scale=-1.0), e2)
    e4 = dve.op(lambda q: q.tensor_scalar(ra[:], ra[:], 1.0, None, ALU.add), e3)
    e5 = act_recip(P, ra[:], e4)
    e6 = dve.op(lambda q: q.tensor_scalar(ra[:], ra[:], 0.125, None, ALU.mult), e5)
    e7 = dve.op(lambda q: q.tensor_copy(g_b[0:8, :], g_f[0:8, :]), e1)
    e8 = emit_rowsum8(P, st, g_b, rb[:], 4, [e7, eh, br])
    e9 = act_recip(P, rb[:], e8)
    eB = emit_bcast_row(P, st, e9, rb[:], recB, 5, nrows=8, war=eh)
    eD = emit_bcast_row(P, st, e6, ra[:], thrB, 5, nrows=8, war=eB)
    eE = dve.op(lambda q: q.tensor_tensor(g_f[0:8, :], g_f[0:8, :], recB[0:8, :], ALU.mult), eB, e1)
    eF = dve.op(lambda q: q.tensor_tensor(g_f[0:8, :], g_f[0:8, :], thrB[0:8, :], ALU.subtract), eD, eE)
    eG = dve.op(lambda q: q.tensor_scalar(g_f[0:8, :], g_f[0:8, :], 0.0, None, ALU.max), eF)
    eH = dve.op(lambda q: q.tensor_copy(g_b[0:8, :], g_f[0:8, :]), eG)
    eI = emit_rowsum8(P, st, g_b, rb[:], 4, [eH, e8])
    eJ = dve.op(lambda q: q.tensor_scalar(rc[:], rb[:], 0.0, None, ALU.is_equal), eI, eD)
    eK = dve.op(lambda q: q.tensor_tensor(rb[:], rb[:], rc[:], ALU.add), eJ)
    eL = act_recip(P, rb[:], eK)
    eM = emit_bcast_row(P, st, eL, rb[:], recB, 5, nrows=8, war=eD)
    eO = dve.op(lambda q: q.tensor_tensor(g_f[0:8, :], g_f[0:8, :], recB[0:8, :], ALU.mult), eM, eG)
    eP = dve.op(lambda q: q.tensor_copy(g_b[0:8, :], g_f[0:8, :]), eO)
    hw_ev = None
    eQ = None
    for qt in range(QT):
        mm5 = pe.op(lambda q, qt=qt: q.matmul(
            ps[:, 6, :], p8[0:8, :], g_b[0:8, qt * 512:(qt + 1) * 512],
            start=True, stop=True), eP, c3, hw_ev, eh)
        eQ = act.op(lambda q, qt=qt: q.copy(wBq[:], ps[:, 6, :]), mm5)
        hw_ev = dve.op(lambda q, qt=qt: q.tensor_tensor(
            hw_sb[:, qt * 512:(qt + 1) * 512], h_sb[:, qt * 512:(qt + 1) * 512],
            wBq[:], ALU.mult), eQ, eh)

    dbg_ev = None

    lb.free(*reversed(PH1))
    wcol = [lb.sbuf(f"wcol{i}", [128, D], BF16) for i in range(2)]
    bps = lb.sbuf("bps", [128, D], BF16)
    x1r = [lb.sbuf(f"x1r{i}", [128, S], BF16) for i in range(2)]
    x2t = [lb.sbuf(f"x2t{i}", [128, S], BF16) for i in range(2)]
    ev32 = [lb.sbuf(f"ev32_{i}", [128, 512], F32) for i in range(4)]
    b_ld = gd.dma(bps[:], bp[:, :], hw_ev, dbg_ev)
    wld = [None, None]
    x1ld = [None, None]
    mm_done = {}
    evac_done = {}
    out_done = {}
    wld[0] = col_load(gd, wcol[0], wbt, 0, dep=b_ld)
    x1ld[0] = sd.dma(x1r[0][:], x1[0:128, :], lev, hw_ev)
    for oc in range(FC):
        pb = oc % 2
        half = pb * 4
        if oc + 1 < FC:
            nb = (oc + 1) % 2
            wld[nb] = col_load(gd, wcol[nb], wbt, oc + 1, dep=mm_done.get(oc - 1))
            x1ld[nb] = sd.dma(x1r[nb][:], x1[(oc + 1) * 128:(oc + 2) * 128, :],
                              evac_done.get(oc - 1))
        deps = [wld[pb], lev, evac_done.get(oc - 2)]
        if oc == 0:
            deps.append(etv)
        if oc == 1:
            deps.append(hw_ev)
        mm = None
        for fc in range(FC):
            for qt in range(QT):
                mm = pe.op(lambda q, fc=fc, qt=qt, pb=pb, half=half: q.matmul(
                    ps[:, half + qt, :], wcol[pb][:, fc * 128:(fc + 1) * 128],
                    ln[fc][:, qt * 512:(qt + 1) * 512],
                    start=(fc == 0), stop=False),
                    *(deps if fc == 0 and qt == 0 else []))
        for qt in range(QT):
            mm = pe.op(lambda q, qt=qt, oc=oc, half=half: q.matmul(
                ps[:, half + qt, :], bps[:, oc * 128:(oc + 1) * 128],
                hw_sb[:, qt * 512:(qt + 1) * 512],
                start=False, stop=True), *([hw_ev] if qt == 0 else []))
        mm_done[oc] = mm
        ee = None
        for qt in range(QT):
            eb = qt
            ea = act.op(lambda q, qt=qt, half=half, eb=eb, oc=oc: q.activation(
                ev32[eb][:, :], ps[:, half + qt, :], AF.Identity,
                bias=bcs[:, oc:oc + 1]),
                mm, c5, evac_done.get(oc - 1))
            ee = pool.op(lambda q, qt=qt, oc=oc, pb=pb, eb=eb: q.tensor_tensor(
                x2t[pb][:, qt * 512:(qt + 1) * 512], ev32[eb][:, :],
                x1r[pb][:, qt * 512:(qt + 1) * 512], ALU.add),
                ea, x1ld[pb], out_done.get(oc - 2))
        evac_done[oc] = ee
        out_done[oc] = ad.dma(x2[oc * 128:(oc + 1) * 128, :], x2t[pb][:], ee)

    lb.free("ev32_3", "ev32_2", "ev32_1", "ev32_0", "x2t1", "x2t0", "x1r1",
            "x1r0", "bps", "wcol1", "wcol0", "hw_sb", "bcs", "bth", "p8",
            "onr", "onc", "xs")
    for i in reversed(range(FC)):
        lb.free(f"ln{i}")
    return out_done[FC - 1]


def build_ffn0():
    lb = LB("ffn0")
    nc = lb.nc
    ext = {k: nc.dram_tensor(k, sh, dt, kind="ExternalInput").ap()
           for k, (sh, dt) in FFN0_EXT.items()}
    ext["x2"] = nc.dram_tensor("x2", [D, S], BF16, kind="ExternalOutput").ap()
    ps = lb.psum("ps", [128, 8, 512], F32)
    P = lb.mkqueues()
    done = emit_ffn0(lb, P, ext, ps)
    P["pe"].barrier(done)
    lb.emit()
    return lb



def _prechunk_T(wT):
    Dd, O = wT.shape
    return np.ascontiguousarray(
        wT.reshape(Dd // 128, 128, O).transpose(1, 0, 2).reshape(128, (Dd // 128) * O))


def _pat8():
    p = np.zeros((8, 128), BF)
    for e in range(8):
        p[e, e * 16:(e + 1) * 16] = 2.0
    return p


def prep_ffn0_inputs(inp, x1_fm_bf16):
    d = dict(
        wbt=np.ascontiguousarray(inp["L0_ffn_W_base"].astype(np.float32).T),
        bcols=np.ascontiguousarray(inp["L0_ffn_b_base"].astype(np.float32).reshape(32, 128).T),
        atp=_prechunk_T(inp["L0_ffn_A_w"].astype(np.float32).reshape(128, D).T),
        gtp=_prechunk_T(np.ascontiguousarray(np.concatenate(
            [inp["L0_ffn_W_gate"].astype(np.float32),
             inp["L0_ffn_W_thr"].astype(np.float32)[None, :],
             np.zeros((7, D), np.float32)], axis=0).T)),
        bp=np.ascontiguousarray(
            inp["L0_ffn_B_w"].astype(np.float32).transpose(0, 2, 1).reshape(128, D)),
        pat8=_pat8(),
        ones_col=np.ones((128, 1), BF),
        ones_row=np.ones((1, 128), BF),
        bthr=np.concatenate([inp["L0_ffn_b_thr"].astype(np.float32),
                             np.zeros(15, np.float32)]).reshape(1, 16),
    )
    if x1_fm_bf16 is not None:
        d["x1"] = np.ascontiguousarray(x1_fm_bf16)
    return d


_BUILT = {}


def _get(name, fn):
    if name not in _BUILT:
        _BUILT[name] = fn()
    return _BUILT[name]


def build_mega():
    lb = LB("mega")
    nc = lb.nc
    ext = {}
    for table in (ATTN0_EXT, FFN0_EXT, ATTN1_EXT, FFN1_EXT, DEQ_EXT):
        for k, (sh, dt) in table.items():
            if k not in ext and k not in ("x1", "x2", "x3"):
                ext[k] = nc.dram_tensor(k, sh, dt, kind="ExternalInput").ap()
    ext["out"] = nc.dram_tensor("out", [D, S], F32, kind="ExternalOutput").ap()
    ext["qkv_s"] = nc.dram_tensor("qkv_s", [3 * D, S], BF16)
    ext["sigq_s"] = nc.dram_tensor("sigq_s", [1024, S], BF16).ap()
    ext["x1"] = nc.dram_tensor("x1_s", [D, S], BF16).ap()
    ext["x2"] = nc.dram_tensor("x2_s", [D, S], BF16).ap()
    ext["x3"] = nc.dram_tensor("x3_s", [D, S], BF16).ap()
    ext["wlT"] = nc.dram_tensor("wlT", [D, 512], BF16).ap()
    ext["whT"] = nc.dram_tensor("whT", [D, 512], BF16).ap()
    ext["wl_o"] = nc.dram_tensor("wl_o", [512, D], BF16).ap()
    ext["wh_o"] = nc.dram_tensor("wh_o", [512, D], BF16).ap()
    ext["gl"] = nc.dram_tensor("gl", [NCORES * D, 512], BF16, addr_space="Shared").ap()
    ext["gh"] = nc.dram_tensor("gh", [NCORES * D, 512], BF16, addr_space="Shared").ap()
    ps = lb.psum("ps", [128, 8, 512], F32)
    P = lb.mkqueues()

    def entry(*extra):
        evs = tuple(e for e in (P[k].ev() for k in ("pe", "act", "dve", "pool"))
                    if e is not None)
        return evs + tuple(e for e in extra if e is not None)

    hook = lambda dep, bufs: emit_dequant_gen(lb, P, ext, bufs, entry=(dep,))
    d1 = emit_attn0(lb, P, ext, ps, entry=entry(), mid_hook=hook)
    d2 = emit_ffn0(lb, P, ext, ps, entry=entry(d1))
    d3 = emit_attn1(lb, P, ext, ps, entry=entry(d2))
    e1, e2 = emit_dequant_tr(lb, P, ext, ps, ext["deq_ev"], entry=entry(d3))
    ent4 = entry(d3, e1, e2)
    cc1 = P["pool"].op(lambda q: q.collective_compute(
        "AllGather", ALU.bypass, replica_groups=[list(range(NCORES))],
        ins=[ext["wlT"][:, :]], outs=[ext["gl"][:, :]]), e1)
    cc2 = P["pool"].op(lambda q: q.collective_compute(
        "AllGather", ALU.bypass, replica_groups=[list(range(NCORES))],
        ins=[ext["whT"][:, :]], outs=[ext["gh"][:, :]]), e2)
    ext["cc_ev"] = (cc1, cc2)
    d4 = emit_ffn1(lb, P, ext, ps, entry=ent4)
    P["pe"].barrier(d4)
    lb.emit()
    return lb


def prep_mega_inputs(inputs, core):
    d = {}
    d.update(prep_dequant_inputs(inputs, core))
    d.update(prep_attn0_inputs(inputs, core))
    x = inputs["L0_attn_x"][core]
    d.update(prep_ffn0_inputs(inputs, None))
    d.update(prep_attn1_inputs(inputs, None))
    d.update(prep_ffn1_inputs(inputs, None, None, None))
    for k in ("x1", "x2", "x3", "gl", "gh"):
        d.pop(k, None)
    return d


def kernel(**inputs):
    cores = list(range(NCORES))
    lbM = _get("mega", build_mega)
    res = run_bass_kernel_spmd(
        lbM.nc, [prep_mega_inputs(inputs, c) for c in cores], core_ids=cores)
    out = np.stack([res.results[c]["out"].T for c in cores], axis=0)
    return np.ascontiguousarray(out)



def emit_eqsum(q, acc, ind, c_b, lut, t0, t1, deps):
    ev = q.op(lambda qq, t=t0: qq.tensor_scalar(
        acc[:], c_b[:], float(t), lut[:, t:t + 1], ALU.is_equal, ALU.mult),
        *deps)
    for t in range(t0 + 1, t1):
        q.op(lambda qq, t=t: qq.tensor_scalar(
            ind[:], c_b[:], float(t), lut[:, t:t + 1], ALU.is_equal, ALU.mult), ev)
        ev = q.op(lambda qq, t=t: qq.tensor_tensor(acc[:], acc[:], ind[:], ALU.add))
    return ev


DEQ_EXT = dict(qwp=([512, 1024], I32), maskc=([128, 32], I32),
               lut4c=([512, 16], F32), lut8c=([512, 256], F32),
               ident=([128, 128], BF16))

DVE_HI = 160


def alloc_dequant(lb):
    DQF = ["qwq", "tmask", "tti", "c8b", "indd", "accl", "acch", "l4", "l8"]
    b = dict(
        qw=lb.sbuf("qwq", [128, 1024], I32),
        tmask=lb.sbuf("tmask", [128, 32], I32),
        tti=lb.sbuf("tti", [128, 1024], I32),
        c8b=lb.sbuf("c8b", [128, 2048], BF16),
        indd=lb.sbuf("indd", [128, 2048], BF16),
        accl=lb.sbuf("accl", [128, 2048], BF16),
        acch=lb.sbuf("acch", [128, 2048], BF16),
        l4=lb.sbuf("l4", [128, 16], F32),
        l8=lb.sbuf("l8", [128, 256], F32),
    )
    b["names"] = DQF
    return b


def emit_dequant_gen(lb, P, ext, bufs, entry=()):
    qwp, maskc = ext["qwp"], ext["maskc"]
    lut4c, lut8c = ext["lut4c"], ext["lut8c"]
    wl_o, wh_o = ext["wl_o"], ext["wh_o"]

    DQF = bufs["names"]
    qw, tmask, tti = bufs["qw"], bufs["tmask"], bufs["tti"]
    c8b, indd = bufs["c8b"], bufs["indd"]
    accl, acch, l4, l8 = bufs["accl"], bufs["acch"], bufs["l4"], bufs["l8"]
    tb = indd[:, 0:1024]

    dve, pool, gd = P["dve"], P["pool"], P["gd"]
    cm = gd.dma(tmask[:], maskc[:, :], *entry)
    ld = gd.dma(qw[:], qwp[0:128, :])
    ll4 = gd.dma(l4[:], lut4c[0:128, :])
    ll8 = gd.dma(l8[:], lut8c[0:128, :])
    yield

    mexp = tmask[:].unsqueeze(1).broadcast_to([128, 32, 32])
    tti3 = tti[:].rearrange("p (w s) -> p w s", s=32)
    prev = {}
    for ot in range(4):
        for hf in range(2):
            def wexp(pl, hf=hf, sub=0):
                base = pl * 128 + hf * 64 + sub * 32
                return (qw[:, base:base + 32]
                        .unsqueeze(2).broadcast_to([128, 32, 32]))

            e = None
            for sub in range(2):
                csl = slice(sub * 1024, (sub + 1) * 1024)
                e = dve.op(lambda q, wexp=wexp, sub=sub: q.tensor_tensor(
                    tti3, wexp(0, sub=sub), mexp, ALU.bitwise_and), ld, cm)
                e = dve.op(lambda q, csl=csl: q.tensor_scalar(
                    c8b[:, csl], tti[:], 0, None, ALU.not_equal))
                for pl in range(1, 4):
                    e = dve.op(lambda q, pl=pl, wexp=wexp, sub=sub: q.tensor_tensor(
                        tti3, wexp(pl, sub=sub), mexp, ALU.bitwise_and))
                    e = dve.op(lambda q: q.tensor_scalar(
                        tb, tti[:], 0, None, ALU.not_equal))
                    e = dve.op(lambda q, pl=pl, csl=csl: q.scalar_tensor_tensor(
                        c8b[:, csl], tb, float(1 << pl), c8b[:, csl],
                        ALU.mult, ALU.add))
            evl = emit_eqsum(dve, accl, indd, c8b, l4, 0, 16,
                             [e, ll4, prev.get("sl")])
            e = evl
            for sub in range(2):
                csl = slice(sub * 1024, (sub + 1) * 1024)
                for pl in range(4, 8):
                    e = dve.op(lambda q, pl=pl, wexp=wexp, sub=sub: q.tensor_tensor(
                        tti3, wexp(pl, sub=sub), mexp, ALU.bitwise_and))
                    e = dve.op(lambda q: q.tensor_scalar(
                        tb, tti[:], 0, None, ALU.not_equal))
                    e = dve.op(lambda q, pl=pl, csl=csl: q.scalar_tensor_tensor(
                        c8b[:, csl], tb, float(1 << pl), c8b[:, csl],
                        ALU.mult, ALU.add))
            prev["extr"] = e
            prev["sl"] = gd.dma(wl_o[ot * 128:(ot + 1) * 128,
                                     hf * 2048:(hf + 1) * 2048], accl[:], evl)
            yield
            evh = emit_eqsum(dve, acch, indd, c8b, l8, 0, 256,
                             [e, ll8, prev.get("sh")])
            prev["sh"] = gd.dma(wh_o[ot * 128:(ot + 1) * 128,
                                     hf * 2048:(hf + 1) * 2048], acch[:], evh)
            if hf == 1 and ot + 1 < 4:
                ld = gd.dma(qw[:], qwp[(ot + 1) * 128:(ot + 2) * 128, :],
                            prev["extr"])
                ll4 = gd.dma(l4[:], lut4c[(ot + 1) * 128:(ot + 2) * 128, :],
                             evl)
                ll8 = gd.dma(l8[:], lut8c[(ot + 1) * 128:(ot + 2) * 128, :],
                             evh)
            yield
    ext["deq_ev"] = (prev["sl"], prev["sh"])
    ext["deq_done"] = (prev["sh"], prev["sl"])
    ext["deq_names"] = DQF


def emit_dequant_tr(lb, P, ext, ps, deps, entry=()):
    pe, act, ad, sd = P["pe"], P["act"], P["ad"], P["sd"]
    wl_o, wh_o = ext["wl_o"], ext["wh_o"]
    wlT, whT = ext["wlT"], ext["whT"]
    idt2 = lb.sbuf("idt2", [128, 128], BF16)
    acc2 = [lb.sbuf(f"acc2_{i}", [128, D], BF16) for i in range(2)]
    slab = lb.sbuf("slab", [128, FC * 512], BF16)
    pst = [ps[:, 2 * i, :].bitcast(BF16) for i in range(4)]
    ci = sd.dma(idt2[:], ext["ident"][:, :], *(deps + tuple(entry)))
    evs = []
    tr_ev = {}
    for mi, (src_d, dst_d) in enumerate(((wl_o, wlT), (wh_o, whT))):
        for ot in range(4):
            lda = sd.dma(acc2[ot % 2][:], src_d[ot * 128:(ot + 1) * 128, :],
                         tr_ev.get((mi, ot - 2, FC - 1)))
            for ic in range(FC):
                b = 2 * (ot % 2) + ic % 2
                mmt = pe.op(lambda q, ic=ic, b=b, ot=ot: q.transpose(
                    pst[b][:, 0:128], acc2[ot % 2][:, ic * 128:(ic + 1) * 128],
                    idt2[:]), lda, ci, tr_ev.get((mi, ot, ic - 2)),
                    tr_ev.get((mi - 1, ot, ic)) if False else None,
                    *(entry if mi == 0 and ot == 0 and ic < 2 else ()))
                tv = act.op(lambda q, ic=ic, ot=ot, b=b: q.copy(
                    slab[:, ic * 512 + ot * 128: ic * 512 + (ot + 1) * 128],
                    pst[b][:, 0:128]), mmt, evs[-1] if (ot == 0 and ic == 0 and evs) else None)
                tr_ev[(mi, ot, ic)] = tv
        evs.append(ad.dma(dst_d.rearrange("(ic p) o -> p ic o", p=128),
                          slab.rearrange("p (ic o) -> p ic o", o=512),
                          tr_ev[(mi, 3, FC - 1)]))
    lb.free("slab", "acc2_1", "acc2_0", "idt2")
    return tuple(evs)


def build_dequant():
    lb = LB("dequant")
    nc = lb.nc
    ext = {k: nc.dram_tensor(k, sh, dt, kind="ExternalInput").ap()
           for k, (sh, dt) in DEQ_EXT.items()}
    ext["wlT"] = nc.dram_tensor("wlT", [D, 512], BF16, kind="ExternalOutput").ap()
    ext["whT"] = nc.dram_tensor("whT", [D, 512], BF16, kind="ExternalOutput").ap()
    ext["wl_o"] = nc.dram_tensor("wl_o", [512, D], BF16).ap()
    ext["wh_o"] = nc.dram_tensor("wh_o", [512, D], BF16).ap()
    ps = lb.psum("ps", [128, 8, 512], F32)
    P = lb.mkqueues()
    e1, e2, _ = emit_dequant(lb, P, ext, ps)
    e3, e4 = emit_dequant_tr(lb, P, ext, ps, (e1, e2))
    P["pe"].barrier(e3, e4)
    lb.emit()
    return lb


def prep_dequant_inputs(inp, core):
    qw = inp["L1_ffn_qweight"]
    rows = slice(512 * core, 512 * (core + 1))
    return dict(
        qwp=np.ascontiguousarray(qw[:, rows, :].transpose(1, 0, 2).reshape(512, 1024)),
        maskc=np.broadcast_to((np.int32(1) << np.arange(32, dtype=np.int32))[None, :], (128, 32)).copy(),
        lut4c=np.ascontiguousarray(inp["L1_ffn_lut4"][rows].astype(np.float32)),
        lut8c=np.ascontiguousarray(inp["L1_ffn_lut8"][rows].astype(np.float32)),
        ident=np.eye(128, dtype=BF),
    )



FFN1_EXT = dict(x3=([D, S], BF16), bcols1=([128, 32], F32),
                ones_col=([128, 1], BF16), ones_row=([1, 128], BF16))


def emit_ffn1(lb, P, ext, ps, entry=()):
    x3, gl, gh = ext["x3"], ext["gl"], ext["gh"]
    bcols = ext["bcols1"]
    ones_col, ones_row = ext["ones_col"], ext["ones_row"]
    out = ext["out"]

    ln = [lb.sbuf(f"ln{i}", [128, S], BF16) for i in range(FC)]
    onc = lb.sbuf("onc", [128, 1], BF16)
    onr = lb.sbuf("onr", [1, 128], BF16)
    bcs = lb.sbuf("bcs", [128, 32], F32)
    mB = lb.sbuf("mB", [128, S], BF16)
    rB = lb.sbuf("rB", [128, S], BF16)
    maskB = lb.sbuf("maskB", [128, S], BF16)
    pe, act, dve = P["pe"], P["act"], P["dve"]
    pool = P["pool"]
    sd, gd, ad = P["sd"], P["gd"], P["ad"]

    PH1 = ["xs", "sq", "ra", "rb", "rc", "rd", "rowbt"]
    xs = lb.sbuf("xs", [128, S], BF16)
    sq = lb.sbuf("sq", [128, 512], BF16)
    ra = lb.sbuf("ra", [1, S], F32)
    rb = lb.sbuf("rb", [1, S], F32)
    rc = lb.sbuf("rc", [1, S], F32)
    rd = lb.sbuf("rd", [1, S], F32)
    rowbt = lb.sbuf("rowbt", [1, S], BF16)

    st = dict(ps=ps, xs=xs, sq=sq, onc=onc, onr=onr, rowb=rowbt,
              ra=ra, rb=rb, rc=rc)

    c1 = sd.dma(onc[:], ones_col[:, :], *entry)
    c2 = sd.dma(onr[:], ones_row[:, :])
    c5 = sd.dma(bcs[:], bcols[:, :])

    sev = emit_stats(P, st, lambda fc: x3[fc * 128:(fc + 1) * 128, :], FC,
                     dep=c5, entry=entry, tiles=ln)
    me = dve.op(lambda q: q.tensor_tensor(rc[:], ra[:], ra[:], ALU.mult), sev)
    me = dve.op(lambda q: q.tensor_scalar(rc[:], rc[:], -1.0 / D, None, ALU.mult), me)
    me = dve.op(lambda q: q.tensor_tensor(rc[:], rc[:], rb[:], ALU.add), me)
    me = dve.op(lambda q: q.tensor_scalar(rd[:], rc[:], 1.0 / D, None, ALU.mult), me)
    me = dve.op(lambda q: q.tensor_scalar(rd[:], rd[:], float(LNEPS), None, ALU.add), me)
    me = act_recip(P, rd[:], me)
    me = dve.op(lambda q: q.tensor_tensor(rd[:], rc[:], rd[:], ALU.mult), me)
    me = dve.op(lambda q: q.tensor_scalar(rd[:], rd[:], float(D), None, ALU.is_gt), me)
    mev = emit_mrstd(P, st, me)
    bm = emit_bcast_row(P, st, mev, ra[:], mB, 2)
    br = emit_bcast_row(P, st, mev, rb[:], rB, 3, war=bm)
    bk = emit_bcast_row(P, st, me, rd[:], maskB, 4, war=br)
    lev = emit_ln(P, st, x3, ln, mB, rB, br, resident=True)

    lb.free(*reversed(PH1))
    wcol = [lb.sbuf(f"wcol{i}", [128, D], BF16) for i in range(2)]
    hcol = [lb.sbuf(f"hcol{i}", [128, D], BF16) for i in range(2)]
    x3r = [lb.sbuf(f"x3r{i}", [128, S], BF16) for i in range(2)]
    o32 = [lb.sbuf(f"o32_{i}", [128, S], F32) for i in range(2)]
    ev32 = [lb.sbuf(f"ev32_{i}", [128, 512], F32) for i in range(4)]

    def gcol_load(dst, g, oc, dep, dep2=None):
        c = oc // 4
        off = (oc % 4) * 128
        src = g[c * D:(c + 1) * D, off:off + 128].rearrange("(fc p) o -> p fc o", p=128)
        return sd.dma(dst.rearrange("p (fc o) -> p fc o", o=128), src, dep, dep2)

    wld = [None, None]
    hld = [None, None]
    x3ld = [None, None]
    mm_done = {}
    evac_done = {}
    pdone = {}
    out_done = {}
    if ext.get("cc_ev") is not None:
        for _cce in ext["cc_ev"]:
            sd.op(lambda q: q.nop(), _cce)
    wld[0] = gcol_load(wcol[0], gl, 0, bk, lev)
    hld[0] = gcol_load(hcol[0], gh, 0, None)
    x3ld[0] = sd.dma(x3r[0][:], x3[0:128, :], lev)
    for oc in range(FC):
        pb = oc % 2
        if oc + 1 < FC:
            nb = (oc + 1) % 2
            wld[nb] = gcol_load(wcol[nb], gl, oc + 1, mm_done.get(oc - 1))
            hld[nb] = gcol_load(hcol[nb], gh, oc + 1, None)
            x3ld[nb] = sd.dma(x3r[nb][:], x3[(oc + 1) * 128:(oc + 2) * 128, :],
                              evac_done.get(oc - 1))
        deps = [wld[pb], hld[pb], lev, pdone.get(oc - 1), bk]
        mm = None
        for fc in range(FC):
            for qt in range(QT):
                mm = pe.op(lambda q, fc=fc, qt=qt, pb=pb: q.matmul(
                    ps[:, qt, :], wcol[pb][:, fc * 128:(fc + 1) * 128],
                    ln[fc][:, qt * 512:(qt + 1) * 512],
                    start=(fc == 0), stop=(fc == FC - 1)),
                    *(deps if fc == 0 and qt == 0 else []))
        for fc in range(FC):
            for qt in range(QT):
                mm = pe.op(lambda q, fc=fc, qt=qt, pb=pb: q.matmul(
                    ps[:, 4 + qt, :], hcol[pb][:, fc * 128:(fc + 1) * 128],
                    ln[fc][:, qt * 512:(qt + 1) * 512],
                    start=(fc == 0), stop=(fc == FC - 1)))
        mm_done[oc] = mm
        ee = None
        for qt in range(QT):
            ea = act.op(lambda q, qt=qt, oc=oc: q.activation(
                ev32[qt][:, :], ps[:, qt, :], AF.Identity,
                bias=bcs[:, oc:oc + 1]), mm, c5, evac_done.get(oc - 1))
            eh2 = act.op(lambda q, qt=qt, oc=oc, pb=pb: q.activation(
                o32[pb][:, qt * 512:(qt + 1) * 512], ps[:, 4 + qt, :],
                AF.Identity, bias=bcs[:, oc:oc + 1]),
                out_done.get(oc - 2))
            e1 = pool.op(lambda q, qt=qt, pb=pb: q.tensor_tensor(
                o32[pb][:, qt * 512:(qt + 1) * 512],
                o32[pb][:, qt * 512:(qt + 1) * 512], ev32[qt][:, :],
                ALU.subtract), ea, eh2)
            e2 = pool.op(lambda q, qt=qt, pb=pb: q.tensor_tensor(
                o32[pb][:, qt * 512:(qt + 1) * 512],
                o32[pb][:, qt * 512:(qt + 1) * 512],
                maskB[:, qt * 512:(qt + 1) * 512], ALU.mult), bk)
            e3 = pool.op(lambda q, qt=qt, pb=pb: q.tensor_tensor(
                o32[pb][:, qt * 512:(qt + 1) * 512],
                o32[pb][:, qt * 512:(qt + 1) * 512], ev32[qt][:, :], ALU.add))
            ee = pool.op(lambda q, qt=qt, pb=pb: q.tensor_tensor(
                o32[pb][:, qt * 512:(qt + 1) * 512],
                o32[pb][:, qt * 512:(qt + 1) * 512],
                x3r[pb][:, qt * 512:(qt + 1) * 512], ALU.add), x3ld[pb])
            pdone[oc] = eh2
        evac_done[oc] = ee
        out_done[oc] = ad.dma(out[oc * 128:(oc + 1) * 128, :], o32[pb][:], ee)

    return out_done[FC - 1]


def build_ffn1():
    lb = LB("ffn1")
    nc = lb.nc
    ext = {k: nc.dram_tensor(k, sh, dt, kind="ExternalInput").ap()
           for k, (sh, dt) in FFN1_EXT.items()}
    ext["gl"] = nc.dram_tensor("gl", [NCORES * D, 512], BF16, kind="ExternalInput").ap()
    ext["gh"] = nc.dram_tensor("gh", [NCORES * D, 512], BF16, kind="ExternalInput").ap()
    ext["out"] = nc.dram_tensor("out", [D, S], F32, kind="ExternalOutput").ap()
    ps = lb.psum("ps", [128, 8, 512], F32)
    P = lb.mkqueues()
    done = emit_ffn1(lb, P, ext, ps)
    P["pe"].barrier(done)
    lb.emit()
    return lb


def prep_ffn1_inputs(inp, x3_fm_bf16, gl, gh):
    d = dict(
        bcols1=np.ascontiguousarray(inp["L1_ffn_bias"].astype(np.float32).reshape(32, 128).T),
        ones_col=np.ones((128, 1), BF),
        ones_row=np.ones((1, 128), BF),
    )
    if x3_fm_bf16 is not None:
        d["x3"] = np.ascontiguousarray(x3_fm_bf16)
        d["gl"] = gl
        d["gh"] = gh
    return d



ATTN1_EXT = dict(x2=([D, S], BF16), wqt=([D, 1024], F32), bqc=([128, 8], F32),
                 mrows=([1024, 128], F32), zc=([128, 8], F32), wot=([D, D], F32),
                 ones_col=([128, 1], BF16), ones_row=([1, 128], BF16))


def emit_attn1(lb, P, ext, ps, entry=()):
    pool = None
    x2, wqt, bqc = ext["x2"], ext["wqt"], ext["bqc"]
    mrows, zc, wot = ext["mrows"], ext["zc"], ext["wot"]
    ones_col, ones_row = ext["ones_col"], ext["ones_row"]
    x3, sigq_s = ext["x3"], ext["sigq_s"]

    mcols = lb.sbuf("mcols", [128, 8 * 128], BF16)
    zcs = lb.sbuf("zcs", [128, 8], BF16)
    bqs = lb.sbuf("bqs", [128, 8], F32)
    onc = lb.sbuf("onc", [128, 1], BF16)
    onr = lb.sbuf("onr", [1, 128], BF16)
    pe, act, dve = P["pe"], P["act"], P["dve"]
    pool = P["pool"]
    sd, gd, ad = P["sd"], P["gd"], P["ad"]

    c1 = sd.dma(onc[:], ones_col[:, :], *entry)
    c2 = sd.dma(onr[:], ones_row[:, :])
    c3 = sd.dma(bqs[:], bqc[:, :])
    cm = gd.dma(mcols.rearrange("p (kv e) -> p kv e", e=128),
                mrows.rearrange("(kv p) e -> p kv e", p=128), *entry)
    cz = gd.dma(zcs[:], zc[:, :])

    PH1 = ["xs", "sq", "ra", "rb", "rc", "rowbt", "mB", "rB"] + \
          [f"lnt{i}" for i in range(FC)] + ["wq0", "wq1", "e32a", "e32b", "sgb"]
    xs = lb.sbuf("xs", [128, S], BF16)
    sq = lb.sbuf("sq", [128, 512], BF16)
    ra = lb.sbuf("ra", [1, S], F32)
    rb = lb.sbuf("rb", [1, S], F32)
    rc = lb.sbuf("rc", [1, S], F32)
    rowbt = lb.sbuf("rowbt", [1, S], BF16)
    mB = lb.sbuf("mB", [128, S], BF16)
    rB = lb.sbuf("rB", [128, S], BF16)
    ln = [lb.sbuf(f"lnt{i}", [128, S], BF16) for i in range(FC)]
    wq = [lb.sbuf("wq0", [128, D], BF16), lb.sbuf("wq1", [128, D], BF16)]
    e32 = [lb.sbuf("e32a", [128, 512], F32), lb.sbuf("e32b", [128, 512], F32)]
    sgb = lb.sbuf("sgb", [128, S], BF16)

    st = dict(ps=ps, xs=xs, sq=sq, onc=onc, onr=onr, rowb=rowbt,
              ra=ra, rb=rb, rc=rc)
    sev = emit_stats(P, st, lambda fc: x2[fc * 128:(fc + 1) * 128, :], FC,
                     dep=c3, entry=entry, tiles=ln)
    mev = emit_mrstd(P, st, sev)
    bm = emit_bcast_row(P, st, mev, ra[:], mB, 2)
    br = emit_bcast_row(P, st, mev, rb[:], rB, 3, war=bm)
    lev = emit_ln(P, st, x2, ln, mB, rB, br, resident=True)

    wld = [None, None]
    mm_done = {}
    evac_done = {}
    sg_done = {}
    wld[0] = col_load(gd, wq[0], wqt, 0, dep=cz)
    for oc in range(8):
        pb = oc % 2
        half = pb * 4
        if oc + 1 < 8:
            nb = (oc + 1) % 2
            wld[nb] = col_load(gd, wq[nb], wqt, oc + 1, dep=mm_done.get(oc - 1))
        deps = [wld[pb], lev, evac_done.get(oc - 2)]
        mm = None
        for fc in range(FC):
            for qt in range(QT):
                mm = pe.op(lambda q, fc=fc, qt=qt, pb=pb, half=half: q.matmul(
                    ps[:, half + qt, :], wq[pb][:, fc * 128:(fc + 1) * 128],
                    ln[fc][:, qt * 512:(qt + 1) * 512],
                    start=(fc == 0), stop=(fc == FC - 1)),
                    *(deps if fc == 0 and qt == 0 else []))
        mm_done[oc] = mm
        ee = None
        for qt in range(QT):
            eb = qt % 2
            ea = act.op(lambda q, qt=qt, half=half, eb=eb, oc=oc: q.activation(
                e32[eb][:, :], ps[:, half + qt, :], AF.Identity,
                bias=bqs[:, oc:oc + 1]), mm, c3, evac_done.get(oc - 1),
                sg_done.get((oc, qt - 2)))
            e1 = pool.op(lambda q, qt=qt, eb=eb: q.tensor_scalar(
                sgb[:, qt * 512:(qt + 1) * 512], e32[eb][:, :], 0.0, None, ALU.min),
                ea, sg_done.get(oc - 1) if qt == 0 else None)
            e2 = act.op(lambda q, qt=qt: q.activation(
                sgb[:, qt * 512:(qt + 1) * 512], sgb[:, qt * 512:(qt + 1) * 512],
                AF.Exp), e1)
            e3 = pool.op(lambda q, qt=qt, eb=eb: q.tensor_scalar(
                e32[eb][:, :], e32[eb][:, :], 0.0, None, ALU.max), e2)
            ee = pool.op(lambda q, qt=qt, eb=eb: q.tensor_tensor(
                sgb[:, qt * 512:(qt + 1) * 512],
                sgb[:, qt * 512:(qt + 1) * 512], e32[eb][:, :], ALU.add))
            sg_done[(oc, qt)] = ee
        evac_done[oc] = ee
        sg_done[oc] = ad.dma(sigq_s[oc * 128:(oc + 1) * 128, :], sgb[:], ee)

    lb.free(*reversed(PH1))
    ret = [lb.sbuf(f"ret{i}", [128, S], BF16) for i in range(8)]
    sgt = lb.sbuf("sgt", [128, S], BF16)
    recB = lb.sbuf("recB", [128, S], F32)
    rd2 = lb.sbuf("rd2", [1, S], F32)
    rw2 = lb.sbuf("rw2", [1, S], BF16)
    x2r = [lb.sbuf(f"x2r{i}", [128, S], BF16) for i in range(2)]
    x3t = [lb.sbuf(f"x3t{i}", [128, S], BF16) for i in range(2)]
    wo = [lb.sbuf("wo0", [128, D], BF16), lb.sbuf("wo1", [128, D], BF16)]
    f32e = [lb.sbuf(f"f32e{i}", [128, 512], F32) for i in range(4)]

    ret_done = {}
    prevkv = {}
    ph1_done = sg_done[7]
    for kv in range(8):
        sld = sd.dma(sgt[:], sigq_s[kv * 128:(kv + 1) * 128, :], ph1_done,
                     prevkv.get(kv - 1))
        mmn = None
        for qt in range(QT):
            mmn = pe.op(lambda q, qt=qt, kv=kv: q.matmul(
                ps[:, qt, :], mcols[:, kv * 128:(kv + 1) * 128],
                sgt[:, qt * 512:(qt + 1) * 512], start=True, stop=True),
                sld, cm, prevkv.get(("num", kv - 1)))
        edn = None
        for qt in range(QT):
            mmd = pe.op(lambda q, qt=qt, kv=kv: q.matmul(
                ps[0:1, 4, :], zcs[:, kv:kv + 1],
                sgt[:, qt * 512:(qt + 1) * 512], start=True, stop=True),
                edn if edn else (prevkv.get(("den", kv - 1)) or sld))
            edn = act.op(lambda q, qt=qt: q.copy(rd2[0:1, qt * 512:(qt + 1) * 512],
                                                 ps[0:1, 4, :]), mmd)
        er = dve.op(lambda q: q.tensor_scalar(rd2[:], rd2[:], 1e-6, None, ALU.add), edn)
        er = act_recip(P, rd2[:], er)
        er = dve.op(lambda q: q.tensor_copy(rw2[0:1, :], rd2[:]), er, prevkv.get(("bc", kv - 1)))
        eb2 = None
        for qt in range(QT):
            mmb = pe.op(lambda q, qt=qt: q.matmul(
                ps[:, 5, :], onr[:], rw2[0:1, qt * 512:(qt + 1) * 512],
                start=True, stop=True), er, eb2)
            eb2 = act.op(lambda q, qt=qt: q.copy(
                recB[:, qt * 512:(qt + 1) * 512], ps[:, 5, :]), mmb)
        prevkv[("bc", kv)] = eb2
        ev = None
        for qt in range(QT):
            eva = act.op(lambda q, qt=qt: q.copy(
                sgt[:, qt * 512:(qt + 1) * 512], ps[:, qt, :]), mmn, eb2,
                prevkv.get(("rm", kv - 1)))
            ev = dve.op(lambda q, qt=qt, kv=kv: q.tensor_tensor(
                ret[kv][:, qt * 512:(qt + 1) * 512],
                sgt[:, qt * 512:(qt + 1) * 512],
                recB[:, qt * 512:(qt + 1) * 512], ALU.mult), eva)
        prevkv[("rm", kv)] = ev
        prevkv[("num", kv)] = ev
        prevkv[("den", kv)] = edn
        prevkv[kv] = ev
        ret_done[kv] = ev
    all_ret = ret_done[7]

    wld = [None, None]
    mm_done = {}
    evac_done = {}
    out_done = {}
    x2ld = [None, None]
    wld[0] = col_load(gd, wo[0], wot, 0, dep=ph1_done)
    x2ld[0] = sd.dma(x2r[0][:], x2[0:128, :], all_ret)
    for oc in range(FC):
        pb = oc % 2
        half = pb * 4
        if oc + 1 < FC:
            nb = (oc + 1) % 2
            wld[nb] = col_load(gd, wo[nb], wot, oc + 1, dep=mm_done.get(oc - 1))
            x2ld[nb] = sd.dma(x2r[nb][:], x2[(oc + 1) * 128:(oc + 2) * 128, :],
                              evac_done.get(oc - 1))
        deps = [wld[pb], all_ret, evac_done.get(oc - 2)]
        mm = None
        for hc in range(FC):
            for qt in range(QT):
                mm = pe.op(lambda q, hc=hc, qt=qt, pb=pb, half=half: q.matmul(
                    ps[:, half + qt, :], wo[pb][:, hc * 128:(hc + 1) * 128],
                    ret[hc // 4][:, qt * 512:(qt + 1) * 512],
                    start=(hc == 0), stop=(hc == FC - 1)),
                    *(deps if hc == 0 and qt == 0 else []))
        mm_done[oc] = mm
        ee = None
        for qt in range(QT):
            ea = act.op(lambda q, qt=qt, half=half: q.copy(
                f32e[qt][:, :], ps[:, half + qt, :]), mm, evac_done.get(oc - 1))
            ee = pool.op(lambda q, qt=qt, pb=pb: q.tensor_tensor(
                x3t[pb][:, qt * 512:(qt + 1) * 512], f32e[qt][:, :],
                x2r[pb][:, qt * 512:(qt + 1) * 512], ALU.add),
                ea, x2ld[pb], out_done.get(oc - 2))
        evac_done[oc] = ee
        out_done[oc] = ad.dma(x3[oc * 128:(oc + 1) * 128, :], x3t[pb][:], ee)

    lb.free("f32e3", "f32e2", "f32e1", "f32e0", "wo1", "wo0", "x3t1", "x3t0",
            "x2r1", "x2r0", "rw2", "rd2", "recB", "sgt",
            "ret7", "ret6", "ret5", "ret4", "ret3", "ret2", "ret1", "ret0",
            "onr", "onc", "bqs", "zcs", "mcols")
    return out_done[FC - 1]


def build_attn1():
    lb = LB("attn1")
    nc = lb.nc
    ext = {k: nc.dram_tensor(k, sh, dt, kind="ExternalInput").ap()
           for k, (sh, dt) in ATTN1_EXT.items()}
    ext["x3"] = nc.dram_tensor("x3", [D, S], BF16, kind="ExternalOutput").ap()
    ext["sigq_s"] = nc.dram_tensor("sigq_s", [1024, S], BF16).ap()
    ps = lb.psum("ps", [128, 8, 512], F32)
    P = lb.mkqueues()
    done = emit_attn1(lb, P, ext, ps)
    P["pe"].barrier(done)
    lb.emit()
    return lb


def prep_attn1_inputs(inp, x2_fm_bf16):
    sel = np.concatenate([np.arange(h * 128, (h + 1) * 128) for h in range(0, 32, 4)])
    Wq = inp["L1_attn_Wq"].astype(np.float32)
    bq = inp["L1_attn_bq"].astype(np.float32)
    d = dict(
        wqt=np.ascontiguousarray(Wq[sel].T),
        bqc=np.ascontiguousarray(bq[sel].reshape(8, 128).T),
        mrows=np.ascontiguousarray(inp["L1_attn_M"].astype(np.float32).reshape(1024, 128)),
        zc=np.ascontiguousarray(inp["L1_attn_z"].astype(np.float32).T),
        wot=np.ascontiguousarray(inp["L1_attn_Wo"].astype(np.float32).T),
        ones_col=np.ones((128, 1), BF),
        ones_row=np.ones((1, 128), BF),
    )
    if x2_fm_bf16 is not None:
        d["x2"] = np.ascontiguousarray(x2_fm_bf16)
    return d



ATTN0_EXT = dict(xf=([D, S], F32), wqkvt=([D, 3 * D], F32), bqkc=([128, 96], F32),
                 wout=([D, D], F32), boc=([128, 32], F32), alibi=([128, 512], F32),
                 masks=([128, 4 * 512], BF16), qcorr=([32, S], BF16),
                 identm=([128, 128], BF16), ones_col=([128, 1], BF16),
                 ones_row=([1, 128], BF16))


def emit_attn0(lb, P, ext, ps, entry=(), nheads=32, mid_hook=None):
    no_tr = no_mask = no_den = no_av = False
    xf, wqkvt, bqkc, wout, boc = (ext["xf"], ext["wqkvt"], ext["bqkc"],
                                  ext["wout"], ext["boc"])
    alibi, masks, qcorr, identm = (ext["alibi"], ext["masks"], ext["qcorr"],
                                   ext["identm"])
    ones_col, ones_row = ext["ones_col"], ext["ones_row"]
    x1, qkv_s = ext["x1"], ext["qkv_s"]
    pool = P["pool"]

    alb = lb.sbuf("alb", [128, 512], F32)
    msk = lb.sbuf("msk", [128, 4 * 512], BF16)
    onc = lb.sbuf("onc", [128, 1], BF16)
    onr = lb.sbuf("onr", [1, 128], BF16)
    bqk = lb.sbuf("bqk", [128, 96], F32)
    bos = lb.sbuf("bos", [128, 32], F32)
    idt = lb.sbuf("idt", [128, 128], BF16)
    pe, act, dve = P["pe"], P["act"], P["dve"]
    sd, gd, ad = P["sd"], P["gd"], P["ad"]

    c1 = sd.dma(onc[:], ones_col[:, :], *entry)
    c2 = sd.dma(onr[:], ones_row[:, :])
    c3 = sd.dma(bqk[:], bqkc[:, :])
    c4 = sd.dma(alb[:], alibi[:, :])
    c5 = sd.dma(msk[:], masks[:, :])
    c6 = sd.dma(bos[:], boc[:, :])
    c8 = sd.dma(idt[:], identm[:, :])

    deq_bufs = alloc_dequant(lb) if mid_hook is not None else None
    ln = [lb.sbuf(f"ln{i}", [128, S], BF16) for i in range(FC)]
    PH0 = ["sq", "ra", "rb", "rc", "rowbt", "mB", "rB"]
    sq = lb.sbuf("sq", [128, 512], BF16)
    ra = lb.sbuf("ra", [1, S], F32)
    rb = lb.sbuf("rb", [1, S], F32)
    rc = lb.sbuf("rc", [1, S], F32)
    rowbt = lb.sbuf("rowbt", [1, S], BF16)
    mB = lb.sbuf("mB", [128, S], BF16)
    rB = lb.sbuf("rB", [128, S], BF16)
    st = dict(ps=ps, xs=None, sq=sq, onc=onc, onr=onr,
              rowb=rowbt, ra=ra, rb=rb, rc=rc)
    sev = emit_stats(P, st, lambda fc: xf[fc * 128:(fc + 1) * 128, :], FC,
                     dep=c8, from_f32=True, entry=entry, tiles=ln)
    mev = emit_mrstd(P, st, sev)
    bm = emit_bcast_row(P, st, mev, ra[:], mB, 2)
    br = emit_bcast_row(P, st, mev, rb[:], rB, 3, war=bm)
    lev = emit_ln(P, st, xf, ln, mB, rB, br, resident=True)
    lb.free(*reversed(PH0))

    deq_gen, deq_done, deq_names = None, (), None
    if mid_hook is not None:
        deq_gen = mid_hook(lev, deq_bufs)
        next(deq_gen, None)

    PH1 = ["wc0", "wc1", "qkb0", "qkb1"]
    wc = [lb.sbuf("wc0", [128, D], BF16), lb.sbuf("wc1", [128, D], BF16)]
    qkb = [lb.sbuf("qkb0", [128, S], BF16), lb.sbuf("qkb1", [128, S], BF16)]

    wld = [None, None]
    mm_done = {}
    evac_done = {}
    out_done = {}
    wld[0] = col_load(gd, wc[0], wqkvt, 0, 128, lev, *entry)
    for oc in range(96):
        if deq_gen is not None and oc >= 12 and (oc - 12) % 9 == 0:
            next(deq_gen, None)
        pb = oc % 2
        if oc + 1 < 96:
            nb = (oc + 1) % 2
            wld[nb] = col_load(gd, wc[nb], wqkvt, oc + 1, dep=mm_done.get(oc - 1))
        half = pb * 4
        deps = [wld[pb], lev, evac_done.get(oc - 2)]
        mm = None
        for fc in range(FC):
            for qt in range(QT):
                mm = pe.op(lambda q, fc=fc, qt=qt, pb=pb, half=half: q.matmul(
                    ps[:, half + qt, :], wc[pb][:, fc * 128:(fc + 1) * 128],
                    ln[fc][:, qt * 512:(qt + 1) * 512],
                    start=(fc == 0), stop=(fc == FC - 1)),
                    *(deps if fc == 0 and qt == 0 else []))
        ee = None
        for qt in range(QT):
            ee = act.op(lambda q, qt=qt, half=half, oc=oc, pb=pb: q.activation(
                qkb[pb][:, qt * 512:(qt + 1) * 512], ps[:, half + qt, :],
                AF.Identity, bias=bqk[:, oc:oc + 1]), mm, c3,
                evac_done.get(oc - 1), out_done.get(oc - 2))
            evac_done[(oc, qt)] = ee
        evac_done[oc] = ee
        mm_done[oc] = mm
        out_done[oc] = ad.dma(qkv_s[oc * 128:(oc + 1) * 128, :], qkb[pb][:],
                              evac_done[oc])
    qkv_done = out_done[95]
    pe_a1 = pe.ev()
    act_a1 = act.ev()
    lb.free(*reversed(PH1))
    lb.free(*[f"ln{i}" for i in reversed(range(FC))])

    SCL = float(1.0 / np.sqrt(128.0))
    afm = [lb.sbuf(f"afm{i}", [128, S], BF16) for i in range(FC)]
    PH2 = ["qh_sb", "qcr", "kh_sb", "vh_sb", "vht", "ex0", "ex1", "rdn", "rwb",
           "recB", "ava"]
    qh_sb = lb.sbuf("qh_sb", [128, S], BF16)
    qcr = lb.sbuf("qcr", [1, S], BF16)
    kh_sb = lb.sbuf("kh_sb", [128, S], BF16)
    vh_sb = lb.sbuf("vh_sb", [128, S], BF16)
    vht = lb.sbuf("vht", [128, S], BF16)
    ex = [lb.sbuf("ex0", [128, 512], BF16), lb.sbuf("ex1", [128, 512], BF16)]
    rdn = lb.sbuf("rdn", [1, S], F32)
    rwb = lb.sbuf("rwb", [1, S], BF16)
    recB = lb.sbuf("recB", [128, 512], F32)
    ava = lb.sbuf("ava", [128, 512], F32)

    hd_ld = {}
    hev = {}
    for h in range(nheads):
        if deq_gen is not None and h in (1, 4, 6, 9, 11, 14):
            next(deq_gen, None)
        lq = sd.dma(qh_sb[:], qkv_s[h * 128:(h + 1) * 128, :],
                    qkv_done, hev.get(("smm", h - 1)),
                    pe_a1 if h == 0 else None, act_a1 if h == 0 else None)
        lk = sd.dma(kh_sb[:], qkv_s[D + h * 128:D + (h + 1) * 128, :])
        lqc = sd.dma(qcr[:], qcorr[h:h + 1, :], hev.get(("smm", h - 1)))
        lv = sd.dma(vh_sb[:], qkv_s[2 * D + h * 128:2 * D + (h + 1) * 128, :],
                    hev.get(("tr", h - 1)))
        tv = None
        pstv = [ps[:, 4, :].bitcast(BF16), ps[:, 5, :].bitcast(BF16)]
        for kc in range(0 if no_tr else 16):
            tb = kc % 2
            mmt = pe.op(lambda q, kc=kc, tb=tb, pstv=pstv: q.transpose(
                pstv[tb][:, 0:128],
                vh_sb[:, kc * 128:(kc + 1) * 128], idt[:]),
                lv, c8, hev.get(("av", h - 1)), hev.get(("trev", (h, kc - 2))),
                hev.get(("rb", (h - 1, 3))))
            tv = act.op(lambda q, kc=kc, tb=tb, pstv=pstv: q.copy(
                vht[:, kc * 128:(kc + 1) * 128], pstv[tb][:, 0:128]), mmt)
            hev[("trev", (h, kc))] = tv
        hev[("tr", h)] = tv
        smm_last = None
        for qtg in range(4):
            kmax = 4 * qtg + 4

            def smm_emit(kc, qtg=qtg, h=h):
                sb = kc % 2
                diag = kc >= 4 * qtg
                pe.op(lambda q, kc=kc, sb=sb, qtg=qtg: q.matmul(
                    ps[:, sb, :], kh_sb[:, kc * 128:(kc + 1) * 128],
                    qh_sb[:, qtg * 512:(qtg + 1) * 512], start=True, stop=False),
                    lq, lk, hev.get(("exp", (h, qtg, kc - 2))),
                    hev.get(("sprev", h)) if qtg == 0 and kc == 0 else None)
                mm = pe.op(lambda q, sb=sb, qtg=qtg, diag=diag: q.matmul(
                    ps[:, sb, :], onr[:], qcr[0:1, qtg * 512:(qtg + 1) * 512],
                    start=False, stop=not diag), lqc)
                if diag:
                    v = kc - 4 * qtg
                    mm = pe.op(lambda q, sb=sb, v=v: q.matmul(
                        ps[:, sb, :], idt[:], msk[:, v * 512:(v + 1) * 512],
                        start=False, stop=True), c5, c8)
                return mm

            def exp_emit(kc, smm, qtg=qtg, h=h):
                sb = kc % 2
                eex = act.op(lambda q, kc=kc, sb=sb, h=h: q.activation(
                    ex[sb][:, :], ps[:, sb, :], AF.Exp, scale=SCL,
                    bias=alb[:, h * 16 + kc:h * 16 + kc + 1]),
                    smm, c4, hev.get(("avrd", (h, qtg, kc - 2))))
                hev[("exp", (h, qtg, kc))] = eex
                return eex

            def avden_emit(kc, qtg=qtg, h=h, kmax=kmax):
                sb = kc % 2
                eex = hev[("exp", (h, qtg, kc))]
                pe.op(lambda q, kc=kc, sb=sb, kmax=kmax: q.matmul(
                    ps[:, 2, :], vht[:, kc * 128:(kc + 1) * 128], ex[sb][:, :],
                    start=(kc == 0), stop=(kc == kmax - 1)),
                    eex, tv, hev.get(("attn", (h, qtg - 1))) if kc == 0 else None)
                den_mm = pe.op(lambda q, kc=kc, sb=sb, kmax=kmax: q.matmul(
                    ps[0:1, 3, :], onc[:], ex[sb][:, :],
                    start=(kc == 0), stop=(kc == kmax - 1)),
                    hev.get(("dene", (h, qtg - 1))) if kc == 0 else None)
                hev[("avrd", (h, qtg, kc))] = den_mm
                return den_mm

            for kc in range(kmax):
                smm = smm_emit(kc)
                smm_last = smm
                if kc >= 1:
                    avden_emit(kc - 1)
                exp_emit(kc, smm)
            den_mm = avden_emit(kmax - 1)
            av_last = den_mm
            ed = act.op(lambda q, qtg=qtg: q.copy(rdn[0:1, qtg * 512:(qtg + 1) * 512],
                                                  ps[0:1, 3, :]), den_mm)
            hev[("dene", (h, qtg))] = ed
            er = act_recip(P, rdn[0:1, qtg * 512:(qtg + 1) * 512], ed)
            er = pool.op(lambda q, qtg=qtg: q.tensor_copy(
                rwb[0:1, qtg * 512:(qtg + 1) * 512],
                rdn[0:1, qtg * 512:(qtg + 1) * 512]), er)
            bmm = pe.op(lambda q, qtg=qtg: q.matmul(
                ps[:, 4, :], onr[:], rwb[0:1, qtg * 512:(qtg + 1) * 512],
                start=True, stop=True), er, hev.get(("rb", (h, qtg - 1))),
                hev.get(("tr", h)))
            eb = act.op(lambda q: q.copy(recB[:, :], ps[:, 4, :]), bmm)
            hev[("rb", (h, qtg))] = eb
            eva = act.op(lambda q, qtg=qtg: q.copy(ava[:, :], ps[:, 2, :]),
                         av_last, hev.get(("ava", (h, qtg - 1))))
            ea = pool.op(lambda q, h=h, qtg=qtg: q.tensor_tensor(
                afm[h][:, qtg * 512:(qtg + 1) * 512], ava[:, :], recB[:, :],
                ALU.mult), eva, eb)
            hev[("ava", (h, qtg))] = ea
            hev[("attn", (h, qtg))] = ea
        hev[("smm", h)] = smm_last
        hev[("av", h)] = hev[("attn", (h, 3))]
        hev[("sprev", h + 1)] = hev[("exp", (h, 3, 15))]
    attn_all = hev[("attn", (nheads - 1, 3))]
    pe_a2 = pe.ev()
    act_a2 = act.ev()
    if deq_gen is not None:
        for _ in deq_gen:
            pass
        deq_done = ext["deq_done"]
        deq_names = ext["deq_names"]
    lb.free(*reversed(PH2))

    wc3 = [lb.sbuf("wo0", [128, D], BF16), lb.sbuf("wo1", [128, D], BF16)]
    xr = [lb.sbuf(f"xr{i}", [128, S], F32) for i in range(2)]
    x1t = [lb.sbuf(f"x1t{i}", [128, S], BF16) for i in range(2)]
    f32e = [lb.sbuf(f"f32o{i}", [128, 512], F32) for i in range(2)]
    wld = [None, None]
    mm_done = {}
    evac_done = {}
    out_done = {}
    xld = [None, None]
    wld[0] = col_load(gd, wc3[0], wout, 0, 128, attn_all, pe_a2, act_a2)
    xld[0] = sd.dma(xr[0][:], xf[0:128, :], attn_all, pe_a2, act_a2,
                    *deq_done)
    for oc in range(FC):
        pb = oc % 2
        if oc + 1 < FC:
            nb = (oc + 1) % 2
            wld[nb] = col_load(gd, wc3[nb], wout, oc + 1, dep=mm_done.get(oc - 1))
            xld[nb] = sd.dma(xr[nb][:], xf[(oc + 1) * 128:(oc + 2) * 128, :],
                             evac_done.get(oc - 1))
        half = pb * 4
        deps = [wld[pb], attn_all, evac_done.get(oc - 2)]
        mm = None
        for hc in range(FC):
            for qt in range(QT):
                mm = pe.op(lambda q, hc=hc, qt=qt, pb=pb, half=half: q.matmul(
                    ps[:, half + qt, :], wc3[pb][:, hc * 128:(hc + 1) * 128],
                    afm[hc][:, qt * 512:(qt + 1) * 512],
                    start=(hc == 0), stop=(hc == FC - 1)),
                    *(deps if hc == 0 and qt == 0 else []))
        ee = None
        ees = {}
        for qt in range(QT):
            ea = act.op(lambda q, qt=qt, half=half, oc=oc: q.activation(
                f32e[qt % 2][:, :], ps[:, half + qt, :], AF.Identity,
                bias=bos[:, oc:oc + 1]), mm, c6, evac_done.get(oc - 1),
                out_done.get(oc - 2), ees.get(qt - 2))
            ee = pool.op(lambda q, qt=qt, pb=pb: q.tensor_tensor(
                x1t[pb][:, qt * 512:(qt + 1) * 512], f32e[qt % 2][:, :],
                xr[pb][:, qt * 512:(qt + 1) * 512], ALU.add), ea, xld[pb])
            ees[qt] = ee
        evac_done[oc] = ee
        mm_done[oc] = mm
        out_done[oc] = ad.dma(x1[oc * 128:(oc + 1) * 128, :], x1t[pb][:], ee)

    lb.free("f32o1", "f32o0", "x1t1", "x1t0", "xr1", "xr0",
            "wo1", "wo0")
    for i in reversed(range(FC)):
        lb.free(f"afm{i}")
    if deq_names:
        lb.free(*reversed(deq_names))
    lb.free("idt", "bos", "bqk", "onr", "onc", "msk", "alb")
    return out_done[FC - 1]


def build_attn0():
    lb = LB("attn0")
    nc = lb.nc
    ext = {k: nc.dram_tensor(k, sh, dt, kind="ExternalInput").ap()
           for k, (sh, dt) in ATTN0_EXT.items()}
    ext["x1"] = nc.dram_tensor("x1", [D, S], BF16, kind="ExternalOutput").ap()
    ext["qkv_s"] = nc.dram_tensor("qkv_s", [3 * D, S], BF16)
    ps = lb.psum("ps", [128, 8, 512], F32)
    P = lb.mkqueues()
    done = emit_attn0(lb, P, ext, ps)
    P["pe"].barrier(done)
    lb.emit()
    return lb


def prep_attn0_inputs(inp, core):
    x = inp["L0_attn_x"][core].astype(np.float32)
    wqkv = inp["L0_attn_w_qkv"].astype(np.float32)
    bqkv = inp["L0_attn_b_qkv"].astype(np.float32)
    slopes = 1.0 / np.power(2.0, np.arange(1, 33, dtype=np.float32) * 0.25)
    alibi = np.zeros((128, 512), np.float32)
    for h in range(32):
        for kc in range(16):
            k = 128 * kc + np.arange(128)
            alibi[:, h * 16 + kc] = slopes[h] * (k - 2047.0)
    masks = np.zeros((128, 4 * 512), BF)
    for v in range(4):
        for p in range(128):
            js = np.arange(512)
            masks[p, v * 512 + js] = np.where(js >= 128 * v + p, 0.0, -1.0e9)
    qs = np.arange(2048, dtype=np.float32)
    qcorr = np.stack([(-slopes[h] * (qs - 2047.0) - 8.0) * np.sqrt(128.0)
                      for h in range(32)]).astype(BF)
    return dict(
        xf=np.ascontiguousarray(x.T),
        wqkvt=np.ascontiguousarray(wqkv.T),
        bqkc=np.ascontiguousarray(bqkv.reshape(96, 128).T),
        wout=np.ascontiguousarray(inp["L0_attn_w_out"].astype(np.float32).T),
        boc=np.ascontiguousarray(inp["L0_attn_b_out"].astype(np.float32).reshape(32, 128).T),
        alibi=alibi,
        masks=masks,
        qcorr=qcorr,
        identm=np.eye(128, dtype=BF),
        ones_col=np.ones((128, 1), BF),
        ones_row=np.ones((1, 128), BF),
    )


# version 11
# speedup vs baseline: 1.1512x; 1.0015x over previous
import numpy as np
import ml_dtypes
import concourse.bass as bass
import concourse.mybir as mybir
from concourse.bass_utils import run_bass_kernel_spmd

F32 = mybir.dt.float32
BF16 = mybir.dt.bfloat16
I32 = mybir.dt.int32
AF = mybir.ActivationFunctionType
ALU = mybir.AluOpType
BF = ml_dtypes.bfloat16

NCORES = 8
S = 2048
D = 4096
FC = D // 128
QT = S // 512
LNEPS = 1e-5


_GSEQ = [0]


class Q:

    def __init__(self, name, sem, kind):
        self.name, self.sem, self.kind = name, sem, kind
        self.n = 0
        self.waited = {}
        self.items = []

    def _need_wait(self, sem, val):
        key = id(sem)
        if self.waited.get(key, 0) >= val:
            return False
        self.waited[key] = val
        return True

    def op(self, fn, *evs):
        waits = []
        for e in evs:
            if e is None:
                continue
            sem, val = e
            if sem is self.sem:
                if val <= self.n:
                    continue
                raise RuntimeError(f"{self.name}: wait on own future event")
            if self._need_wait(sem, val):
                waits.append((sem, val))
        if self.kind in ("cmp", "dma") and self.n > 0:
            if self._need_wait(self.sem, self.n):
                waits.append((self.sem, self.n))
        step = 16 if self.kind == "dma" else 1
        self.n += step
        _GSEQ[0] += 1
        self.items.append((_GSEQ[0], waits, fn, step))
        return (self.sem, self.n)

    def ev(self):
        return (self.sem, self.n) if self.n > 0 else None

    def barrier(self, *evs):
        return self.op(lambda q: q.nop(), *evs)

    def dma(self, out, in_, *evs, **kw):
        return self.op(lambda q: q.dma_start(out=out, in_=in_, **kw), *evs)

    def replay(self, q):
        replay_merged(q, [self])


def replay_merged(q, queues):
    items = []
    for qu in queues:
        items += [(seq, waits, fn, step, qu.sem) for seq, waits, fn, step in qu.items]
    items.sort(key=lambda x: x[0])
    for seq, waits, fn, step, sem in items:
        for wsem, val in waits:
            q.wait_ge(wsem, val)
        fn(q).then_inc(sem, step)


class LB:

    def __init__(self, name):
        self.nc = bass.Bass("TRN2", target_bir_lowering=False)
        self.name = name
        self._ctx = []
        self._by_name = {}

    def _enter(self, cm):
        h = cm.__enter__()
        self._ctx.append(cm)
        return h

    def sbuf(self, name, shape, dtype):
        self._uid = getattr(self, "_uid", 0) + 1
        h = self._enter(self.nc.sbuf_tensor(f"{name}_u{self._uid}", shape, dtype))
        self._by_name.setdefault(name, []).append(self._ctx[-1])
        return h

    def free(self, *names):
        for name in names:
            cm = self._by_name[name].pop()
            self._ctx.remove(cm)
            cm.__exit__(None, None, None)

    def psum(self, name, shape, dtype=F32):
        return self._enter(self.nc.psum_tensor(name, shape, dtype))

    def mkqueues(self):
        self.q = {}
        for k, kind in (("pe", "pe"), ("act", "cmp"), ("dve", "cmp"),
                        ("pool", "cmp"), ("sd", "dma"), ("gd", "dma"),
                        ("ad", "dma")):
            sem = self._enter(self.nc.semaphore("sem_" + k))
            self.q[k] = Q(k, sem, kind)
        return self.q

    def emit(self):
        blk = self._enter(self.nc.Block())
        q = self.q

        @blk.tensor
        def _(e):
            q["pe"].replay(e)

        @blk.scalar
        def _(e):
            q["act"].replay(e)

        @blk.vector
        def _(e):
            q["dve"].replay(e)

        @blk.gpsimd
        def _(e):
            replay_merged(e, [q["pool"], q["gd"]])

        @blk.sync
        def _(e):
            replay_merged(e, [q["sd"], q["ad"]])

        for t in reversed(self._ctx):
            t.__exit__(None, None, None)
        self._ctx = []



def col_load(gd, dst, wT, oc, ncols=128, dep=None, *deps):
    src = wT[:, oc * ncols:(oc + 1) * ncols].rearrange("(fc p) o -> p fc o", p=128)
    return gd.dma(dst.rearrange("p (fc o) -> p fc o", o=ncols), src, dep, *deps)


def emit_stats(P, st, src_view, n_fc, dep=None, from_f32=False, entry=(),
               tiles=None):
    pe, act, dve, sd = P["pe"], P["act"], P["dve"], P["sd"]
    pool = P["pool"]
    ps, xs, sq, onc = st["ps"], st["xs"], st["sq"], st["onc"]
    last_mm = None
    consumed = dep
    for fc in range(n_fc):
        if tiles is not None:
            if from_f32:
                cv = P["gd"].dma(tiles[fc][:], src_view(fc),
                                 *(entry if fc == 0 else ()))
            else:
                qd = sd if fc % 2 == 0 else P["gd"]
                cv = qd.dma(tiles[fc][:], src_view(fc),
                            *(entry if fc < 2 else ()))
            xb = tiles[fc]
        elif from_f32:
            ld = sd.dma(st["xsf"][:], src_view(fc), consumed,
                        *(entry if fc == 0 else ()))
            cv = pool.op(lambda q: q.tensor_copy(xs[:], st["xsf"][:]), ld)
            xb = xs
        else:
            ld = sd.dma(xs[:], src_view(fc), consumed,
                        *(entry if fc == 0 else ()))
            xb, cv = xs, ld
        for qt in range(QT):
            sqe = dve.op(lambda q, qt=qt, xb=xb: q.tensor_tensor(
                sq[:], xb[:, qt * 512:(qt + 1) * 512],
                xb[:, qt * 512:(qt + 1) * 512], ALU.mult), cv, last_mm)
            pe.op(lambda q, qt=qt, fc=fc, xb=xb: q.matmul(
                ps[0:1, qt, :], onc[:], xb[:, qt * 512:(qt + 1) * 512],
                start=(fc == 0), stop=(fc == n_fc - 1)), cv, dep,
                *(entry if fc == 0 else ()))
            last_mm = pe.op(lambda q, qt=qt, fc=fc: q.matmul(
                ps[0:1, 4 + qt, :], onc[:], sq[:],
                start=(fc == 0), stop=(fc == n_fc - 1)), sqe)
        consumed = last_mm
    ev = None
    for qt in range(QT):
        act.op(lambda q, qt=qt: q.copy(st["ra"][:, qt * 512:(qt + 1) * 512],
                                       ps[0:1, qt, :]), last_mm)
        ev = act.op(lambda q, qt=qt: q.copy(st["rb"][:, qt * 512:(qt + 1) * 512],
                                            ps[0:1, 4 + qt, :]), last_mm)
    return ev


def emit_mrstd(P, st, stats_ev, nfeat=D, eps=LNEPS):
    act, dve = P["act"], P["dve"]
    ra, rb, rc = st["ra"], st["rb"], st["rc"]
    ev = dve.op(lambda q: q.tensor_scalar(ra[:], ra[:], 1.0 / nfeat, None, ALU.mult), stats_ev)
    ev = dve.op(lambda q: q.tensor_tensor(rc[:], ra[:], ra[:], ALU.mult), ev)
    ev = dve.op(lambda q: q.tensor_scalar(rb[:], rb[:], 1.0 / nfeat, None, ALU.mult), ev)
    ev = dve.op(lambda q: q.tensor_tensor(rb[:], rb[:], rc[:], ALU.subtract), ev)
    ev = dve.op(lambda q: q.tensor_scalar(rb[:], rb[:], float(eps), None, ALU.add), ev)
    ev = act.op(lambda q: q.activation(rb[:], rb[:], AF.Ln), ev)
    ev = act.op(lambda q: q.activation(rb[:], rb[:], AF.Exp, scale=-0.5), ev)
    return ev


def emit_bcast_row(P, st, src_ev, row_ap, dst, bank, nrows=128, war=None):
    pe, act, dve = P["pe"], P["act"], P["dve"]
    ps, onr = st["ps"], st["onr"]
    rowb = st["rowb"][0:1, :]
    cv = dve.op(lambda q: q.tensor_copy(rowb, row_ap), src_ev, war)
    ev = None
    for qt in range(QT):
        mm = pe.op(lambda q, qt=qt: q.matmul(
            ps[0:nrows, bank, :], onr[0:1, 0:nrows],
            rowb[:, qt * 512:(qt + 1) * 512], start=True, stop=True),
            cv, ev, war)
        ev = act.op(lambda q, qt=qt: q.copy(
            dst[0:nrows, qt * 512:(qt + 1) * 512], ps[0:nrows, bank, :]), mm)
    return ev


def emit_rowsum8(P, st, src_b, dst_row, bank, deps):
    pe, act = P["pe"], P["act"]
    ps, onc = st["ps"], st["onc"]
    ev = None
    for qt in range(QT):
        mm = pe.op(lambda q, qt=qt: q.matmul(
            ps[0:1, bank, :], onc[0:8, :], src_b[0:8, qt * 512:(qt + 1) * 512],
            start=True, stop=True), *( list(deps) + [ev]))
        ev = act.op(lambda q, qt=qt: q.copy(
            dst_row[0:1, qt * 512:(qt + 1) * 512], ps[0:1, bank, :]), mm)
    return ev


def act_recip(P, row_ap, *evs):
    act = P["act"]
    ev = act.op(lambda q: q.activation(row_ap, row_ap, AF.Ln), *evs)
    return act.op(lambda q: q.activation(row_ap, row_ap, AF.Exp, scale=-1.0), ev)


def emit_ln(P, st, x_dram, ln_tiles, mB, rB, dep, n_fc=FC, from_f32=False,
            resident=False):
    pool, sd, dve = P["pool"], P["sd"], P["dve"]
    ev = dep
    if resident:
        for fc in range(n_fc):
            e1 = dve.op(lambda q, fc=fc: q.tensor_tensor(
                ln_tiles[fc][:], ln_tiles[fc][:], mB[:], ALU.subtract), dep)
            ev = dve.op(lambda q, fc=fc: q.tensor_tensor(
                ln_tiles[fc][:], ln_tiles[fc][:], rB[:], ALU.mult), e1)
        return ev
    xb = [st.get("xsf"), st.get("xsf2", st.get("xsf"))] if from_f32 else None
    nb = 2 if (from_f32 and "xsf2" in st) else 1
    for fc in range(n_fc):
        if from_f32:
            ld = sd.dma(xb[fc % 2][:], x_dram[fc * 128:(fc + 1) * 128, :],
                        st.get(("lncv", fc - nb)),
                        dep if fc < nb else None)
            ld = pool.op(lambda q, fc=fc: q.tensor_copy(ln_tiles[fc][:], xb[fc % 2][:]), ld)
            st[("lncv", fc)] = ld
        else:
            ld = sd.dma(ln_tiles[fc][:], x_dram[fc * 128:(fc + 1) * 128, :])
        e1 = pool.op(lambda q, fc=fc: q.tensor_tensor(ln_tiles[fc][:], ln_tiles[fc][:], mB[:], ALU.subtract), ld, dep)
        ev = pool.op(lambda q, fc=fc: q.tensor_tensor(ln_tiles[fc][:], ln_tiles[fc][:], rB[:], ALU.mult), e1)
    return ev



FFN0_EXT = dict(x1=([D, S], BF16), wbt=([D, D], F32), bcols=([128, 32], F32),
                atp=([128, D], F32), gtp=([128, FC * 16], F32), bp=([128, D], F32),
                pat8=([8, 128], BF16), ones_col=([128, 1], BF16),
                ones_row=([1, 128], BF16), bthr=([1, 16], F32))


def emit_ffn0(lb, P, ext, ps, entry=()):
    x1, wbt, bcols, atp = ext["x1"], ext["wbt"], ext["bcols"], ext["atp"]
    gtp, bp, pat8 = ext["gtp"], ext["bp"], ext["pat8"]
    ones_col, ones_row, bthr = ext["ones_col"], ext["ones_row"], ext["bthr"]
    x2 = ext["x2"]

    ln = [lb.sbuf(f"ln{i}", [128, S], BF16) for i in range(FC)]
    xs = lb.sbuf("xs", [128, S], BF16)
    onc = lb.sbuf("onc", [128, 1], BF16)
    onr = lb.sbuf("onr", [1, 128], BF16)
    p8 = lb.sbuf("p8", [8, 128], BF16)
    bth = lb.sbuf("bth", [1, 16], F32)
    bcs = lb.sbuf("bcs", [128, 32], F32)
    hw_sb = lb.sbuf("hw_sb", [128, S], BF16)
    PH1 = ["sq", "ra", "rb", "rc", "rowbt", "mB", "rB", "acol", "gcol",
           "h_sb", "g_f", "g_b", "recB", "thrB", "wBq"]
    sq = lb.sbuf("sq", [128, 512], BF16)
    ra = lb.sbuf("ra", [1, S], F32)
    rb = lb.sbuf("rb", [1, S], F32)
    rc = lb.sbuf("rc", [1, S], F32)
    rowbt = lb.sbuf("rowbt", [1, S], BF16)
    mB = lb.sbuf("mB", [128, S], BF16)
    rB = lb.sbuf("rB", [128, S], BF16)
    acol = lb.sbuf("acol", [128, D], BF16)
    gcol = lb.sbuf("gcol", [128, FC * 16], BF16)
    h_sb = lb.sbuf("h_sb", [128, S], BF16)
    g_f = lb.sbuf("g_f", [8, S], F32)
    g_b = lb.sbuf("g_b", [8, S], BF16)
    recB = lb.sbuf("recB", [8, S], BF16)
    thrB = lb.sbuf("thrB", [8, S], BF16)
    wBq = lb.sbuf("wBq", [128, 512], BF16)
    pe, act, dve = P["pe"], P["act"], P["dve"]
    sd, gd, ad = P["sd"], P["gd"], P["ad"]

    st = dict(ps=ps, xs=xs, sq=sq, onc=onc, onr=onr, rowb=rowbt,
              ra=ra, rb=rb, rc=rc)

    c1 = sd.dma(onc[:], ones_col[:, :], *entry)
    c2 = sd.dma(onr[:], ones_row[:, :])
    c3 = sd.dma(p8[:], pat8[:, :])
    c4 = sd.dma(bth[:], bthr[:, :])
    c5 = sd.dma(bcs[:], bcols[:, :])
    a_ld = gd.dma(acol[:], atp[:, :], *entry)
    g_ld = gd.dma(gcol[:], gtp[:, :])

    sev = emit_stats(P, st, lambda fc: x1[fc * 128:(fc + 1) * 128, :], FC,
                     dep=c5, entry=entry, tiles=ln)
    mev = emit_mrstd(P, st, sev)
    bm = emit_bcast_row(P, st, mev, ra[:], mB, 2)
    br = emit_bcast_row(P, st, mev, rb[:], rB, 3, war=bm)
    lev = emit_ln(P, st, x1, ln, mB, rB, br, resident=True)

    mm_h = None
    for fc in range(FC):
        for qt in range(QT):
            mm_h = pe.op(lambda q, fc=fc, qt=qt: q.matmul(
                ps[:, 4 + qt, :], acol[:, fc * 128:(fc + 1) * 128],
                ln[fc][:, qt * 512:(qt + 1) * 512],
                start=(fc == 0), stop=(fc == FC - 1)),
                *([a_ld, lev] if fc == 0 and qt == 0 else []))
    eh = None
    for qt in range(QT):
        eh = act.op(lambda q, qt=qt: q.copy(h_sb[:, qt * 512:(qt + 1) * 512],
                                            ps[:, 4 + qt, :]), mm_h)
    mm_g = None
    for fc in range(FC):
        for qt in range(QT):
            mm_g = pe.op(lambda q, fc=fc, qt=qt: q.matmul(
                ps[0:8, qt, :], gcol[:, fc * 16:fc * 16 + 8],
                ln[fc][:, qt * 512:(qt + 1) * 512],
                start=(fc == 0), stop=(fc == FC - 1)),
                *([g_ld, lev, bm] if fc == 0 and qt == 0 else []))
    eg = None
    for qt in range(QT):
        eg = act.op(lambda q, qt=qt: q.copy(g_f[:, qt * 512:(qt + 1) * 512],
                                            ps[0:8, qt, :]), mm_g)
    mm_t = None
    for fc in range(FC):
        for qt in range(QT):
            mm_t = pe.op(lambda q, fc=fc, qt=qt: q.matmul(
                ps[0:1, qt, :], gcol[:, fc * 16 + 8:fc * 16 + 9],
                ln[fc][:, qt * 512:(qt + 1) * 512],
                start=(fc == 0), stop=(fc == FC - 1)),
                *([eg] if fc == 0 and qt == 0 else []))
    etv = None
    for qt in range(QT):
        etv = act.op(lambda q, qt=qt: q.copy(ra[0:1, qt * 512:(qt + 1) * 512],
                                             ps[0:1, qt, :]), mm_t, bm)

    pool = P["pool"]
    e1 = act.op(lambda q: q.activation(g_f[0:8, :], g_f[0:8, :], AF.Exp), eg)
    e2 = act.op(lambda q: q.activation(ra[:], ra[:], AF.Identity, bias=bth[0:1, 0:1]), etv, c4)
    e3 = act.op(lambda q: q.activation(ra[:], ra[:], AF.Exp, scale=-1.0), e2)
    e4 = dve.op(lambda q: q.tensor_scalar(ra[:], ra[:], 1.0, None, ALU.add), e3)
    e5 = act_recip(P, ra[:], e4)
    e6 = dve.op(lambda q: q.tensor_scalar(ra[:], ra[:], 0.125, None, ALU.mult), e5)
    e7 = dve.op(lambda q: q.tensor_copy(g_b[0:8, :], g_f[0:8, :]), e1)
    e8 = emit_rowsum8(P, st, g_b, rb[:], 4, [e7, eh, br])
    e9 = act_recip(P, rb[:], e8)
    eB = emit_bcast_row(P, st, e9, rb[:], recB, 5, nrows=8, war=eh)
    eD = emit_bcast_row(P, st, e6, ra[:], thrB, 5, nrows=8, war=eB)
    eE = dve.op(lambda q: q.tensor_tensor(g_f[0:8, :], g_f[0:8, :], recB[0:8, :], ALU.mult), eB, e1)
    eF = dve.op(lambda q: q.tensor_tensor(g_f[0:8, :], g_f[0:8, :], thrB[0:8, :], ALU.subtract), eD, eE)
    eG = dve.op(lambda q: q.tensor_scalar(g_f[0:8, :], g_f[0:8, :], 0.0, None, ALU.max), eF)
    eH = dve.op(lambda q: q.tensor_copy(g_b[0:8, :], g_f[0:8, :]), eG)
    eI = emit_rowsum8(P, st, g_b, rb[:], 4, [eH, e8])
    eJ = dve.op(lambda q: q.tensor_scalar(rc[:], rb[:], 0.0, None, ALU.is_equal), eI, eD)
    eK = dve.op(lambda q: q.tensor_tensor(rb[:], rb[:], rc[:], ALU.add), eJ)
    eL = act_recip(P, rb[:], eK)
    eM = emit_bcast_row(P, st, eL, rb[:], recB, 5, nrows=8, war=eD)
    eO = dve.op(lambda q: q.tensor_tensor(g_f[0:8, :], g_f[0:8, :], recB[0:8, :], ALU.mult), eM, eG)
    eP = dve.op(lambda q: q.tensor_copy(g_b[0:8, :], g_f[0:8, :]), eO)
    hw_ev = None
    eQ = None
    for qt in range(QT):
        mm5 = pe.op(lambda q, qt=qt: q.matmul(
            ps[:, 6, :], p8[0:8, :], g_b[0:8, qt * 512:(qt + 1) * 512],
            start=True, stop=True), eP, c3, hw_ev, eh)
        eQ = act.op(lambda q, qt=qt: q.copy(wBq[:], ps[:, 6, :]), mm5)
        hw_ev = dve.op(lambda q, qt=qt: q.tensor_tensor(
            hw_sb[:, qt * 512:(qt + 1) * 512], h_sb[:, qt * 512:(qt + 1) * 512],
            wBq[:], ALU.mult), eQ, eh)

    dbg_ev = None

    lb.free(*reversed(PH1))
    wcol = [lb.sbuf(f"wcol{i}", [128, D], BF16) for i in range(2)]
    bps = lb.sbuf("bps", [128, D], BF16)
    x1r = [lb.sbuf(f"x1r{i}", [128, S], BF16) for i in range(2)]
    x2t = [lb.sbuf(f"x2t{i}", [128, S], BF16) for i in range(2)]
    ev32 = [lb.sbuf(f"ev32_{i}", [128, 512], F32) for i in range(4)]
    b_ld = gd.dma(bps[:], bp[:, :], hw_ev, dbg_ev)
    wld = [None, None]
    x1ld = [None, None]
    mm_done = {}
    evac_done = {}
    out_done = {}
    wld[0] = col_load(gd, wcol[0], wbt, 0, dep=b_ld)
    x1ld[0] = sd.dma(x1r[0][:], x1[0:128, :], lev, hw_ev)
    for oc in range(FC):
        pb = oc % 2
        half = pb * 4
        if oc + 1 < FC:
            nb = (oc + 1) % 2
            wld[nb] = col_load(gd, wcol[nb], wbt, oc + 1, dep=mm_done.get(oc - 1))
            x1ld[nb] = sd.dma(x1r[nb][:], x1[(oc + 1) * 128:(oc + 2) * 128, :],
                              evac_done.get(oc - 1))
        deps = [wld[pb], lev, evac_done.get(oc - 2)]
        if oc == 0:
            deps.append(etv)
        if oc == 1:
            deps.append(hw_ev)
        mm = None
        for fc in range(FC):
            for qt in range(QT):
                mm = pe.op(lambda q, fc=fc, qt=qt, pb=pb, half=half: q.matmul(
                    ps[:, half + qt, :], wcol[pb][:, fc * 128:(fc + 1) * 128],
                    ln[fc][:, qt * 512:(qt + 1) * 512],
                    start=(fc == 0), stop=False),
                    *(deps if fc == 0 and qt == 0 else []))
        for qt in range(QT):
            mm = pe.op(lambda q, qt=qt, oc=oc, half=half: q.matmul(
                ps[:, half + qt, :], bps[:, oc * 128:(oc + 1) * 128],
                hw_sb[:, qt * 512:(qt + 1) * 512],
                start=False, stop=True), *([hw_ev] if qt == 0 else []))
        mm_done[oc] = mm
        ee = None
        for qt in range(QT):
            eb = qt
            ea = act.op(lambda q, qt=qt, half=half, eb=eb, oc=oc: q.activation(
                ev32[eb][:, :], ps[:, half + qt, :], AF.Identity,
                bias=bcs[:, oc:oc + 1]),
                mm, c5, evac_done.get(oc - 1))
            ee = pool.op(lambda q, qt=qt, oc=oc, pb=pb, eb=eb: q.tensor_tensor(
                x2t[pb][:, qt * 512:(qt + 1) * 512], ev32[eb][:, :],
                x1r[pb][:, qt * 512:(qt + 1) * 512], ALU.add),
                ea, x1ld[pb], out_done.get(oc - 2))
        evac_done[oc] = ee
        out_done[oc] = ad.dma(x2[oc * 128:(oc + 1) * 128, :], x2t[pb][:], ee)

    lb.free("ev32_3", "ev32_2", "ev32_1", "ev32_0", "x2t1", "x2t0", "x1r1",
            "x1r0", "bps", "wcol1", "wcol0", "hw_sb", "bcs", "bth", "p8",
            "onr", "onc", "xs")
    for i in reversed(range(FC)):
        lb.free(f"ln{i}")
    return out_done[FC - 1]


def build_ffn0():
    lb = LB("ffn0")
    nc = lb.nc
    ext = {k: nc.dram_tensor(k, sh, dt, kind="ExternalInput").ap()
           for k, (sh, dt) in FFN0_EXT.items()}
    ext["x2"] = nc.dram_tensor("x2", [D, S], BF16, kind="ExternalOutput").ap()
    ps = lb.psum("ps", [128, 8, 512], F32)
    P = lb.mkqueues()
    done = emit_ffn0(lb, P, ext, ps)
    P["pe"].barrier(done)
    lb.emit()
    return lb



def _prechunk_T(wT):
    Dd, O = wT.shape
    return np.ascontiguousarray(
        wT.reshape(Dd // 128, 128, O).transpose(1, 0, 2).reshape(128, (Dd // 128) * O))


def _pat8():
    p = np.zeros((8, 128), BF)
    for e in range(8):
        p[e, e * 16:(e + 1) * 16] = 2.0
    return p


def prep_ffn0_inputs(inp, x1_fm_bf16):
    d = dict(
        wbt=np.ascontiguousarray(inp["L0_ffn_W_base"].astype(np.float32).T),
        bcols=np.ascontiguousarray(inp["L0_ffn_b_base"].astype(np.float32).reshape(32, 128).T),
        atp=_prechunk_T(inp["L0_ffn_A_w"].astype(np.float32).reshape(128, D).T),
        gtp=_prechunk_T(np.ascontiguousarray(np.concatenate(
            [inp["L0_ffn_W_gate"].astype(np.float32),
             inp["L0_ffn_W_thr"].astype(np.float32)[None, :],
             np.zeros((7, D), np.float32)], axis=0).T)),
        bp=np.ascontiguousarray(
            inp["L0_ffn_B_w"].astype(np.float32).transpose(0, 2, 1).reshape(128, D)),
        pat8=_pat8(),
        ones_col=np.ones((128, 1), BF),
        ones_row=np.ones((1, 128), BF),
        bthr=np.concatenate([inp["L0_ffn_b_thr"].astype(np.float32),
                             np.zeros(15, np.float32)]).reshape(1, 16),
    )
    if x1_fm_bf16 is not None:
        d["x1"] = np.ascontiguousarray(x1_fm_bf16)
    return d


_BUILT = {}


def _get(name, fn):
    if name not in _BUILT:
        _BUILT[name] = fn()
    return _BUILT[name]


def build_mega():
    lb = LB("mega")
    nc = lb.nc
    ext = {}
    for table in (ATTN0_EXT, FFN0_EXT, ATTN1_EXT, FFN1_EXT, DEQ_EXT):
        for k, (sh, dt) in table.items():
            if k not in ext and k not in ("x1", "x2", "x3"):
                ext[k] = nc.dram_tensor(k, sh, dt, kind="ExternalInput").ap()
    ext["out"] = nc.dram_tensor("out", [D, S], F32, kind="ExternalOutput").ap()
    ext["qkv_s"] = nc.dram_tensor("qkv_s", [3 * D, S], BF16)
    ext["sigq_s"] = nc.dram_tensor("sigq_s", [1024, S], BF16).ap()
    ext["x1"] = nc.dram_tensor("x1_s", [D, S], BF16).ap()
    ext["x2"] = nc.dram_tensor("x2_s", [D, S], BF16).ap()
    ext["x3"] = nc.dram_tensor("x3_s", [D, S], BF16).ap()
    ext["wlT"] = nc.dram_tensor("wlT", [D, 512], BF16).ap()
    ext["whT"] = nc.dram_tensor("whT", [D, 512], BF16).ap()
    ext["wl_o"] = nc.dram_tensor("wl_o", [512, D], BF16).ap()
    ext["wh_o"] = nc.dram_tensor("wh_o", [512, D], BF16).ap()
    ext["gl"] = nc.dram_tensor("gl", [NCORES * D, 512], BF16, addr_space="Shared").ap()
    ext["gh"] = nc.dram_tensor("gh", [NCORES * D, 512], BF16, addr_space="Shared").ap()
    ps = lb.psum("ps", [128, 8, 512], F32)
    P = lb.mkqueues()

    def entry(*extra):
        evs = tuple(e for e in (P[k].ev() for k in ("pe", "act", "dve", "pool"))
                    if e is not None)
        return evs + tuple(e for e in extra if e is not None)

    hook = lambda dep, bufs: emit_dequant_gen(lb, P, ext, bufs, entry=(dep,))
    d1 = emit_attn0(lb, P, ext, ps, entry=entry(), mid_hook=hook)
    d2 = emit_ffn0(lb, P, ext, ps, entry=entry(d1))
    d3 = emit_attn1(lb, P, ext, ps, entry=entry(d2))
    e1, e2 = emit_dequant_tr(lb, P, ext, ps, ext["deq_ev"], entry=entry(d3))
    ent4 = entry(d3, e1, e2)
    cc1 = P["pool"].op(lambda q: q.collective_compute(
        "AllGather", ALU.bypass, replica_groups=[list(range(NCORES))],
        ins=[ext["wlT"][:, :]], outs=[ext["gl"][:, :]]), e1)
    cc2 = P["pool"].op(lambda q: q.collective_compute(
        "AllGather", ALU.bypass, replica_groups=[list(range(NCORES))],
        ins=[ext["whT"][:, :]], outs=[ext["gh"][:, :]]), e2)
    ext["cc_ev"] = (cc1, cc2)
    d4 = emit_ffn1(lb, P, ext, ps, entry=ent4)
    P["pe"].barrier(d4)
    lb.emit()
    return lb


def prep_mega_inputs(inputs, core):
    d = {}
    d.update(prep_dequant_inputs(inputs, core))
    d.update(prep_attn0_inputs(inputs, core))
    x = inputs["L0_attn_x"][core]
    d.update(prep_ffn0_inputs(inputs, None))
    d.update(prep_attn1_inputs(inputs, None))
    d.update(prep_ffn1_inputs(inputs, None, None, None))
    for k in ("x1", "x2", "x3", "gl", "gh"):
        d.pop(k, None)
    return d


def kernel(**inputs):
    cores = list(range(NCORES))
    lbM = _get("mega", build_mega)
    res = run_bass_kernel_spmd(
        lbM.nc, [prep_mega_inputs(inputs, c) for c in cores], core_ids=cores)
    out = np.stack([res.results[c]["out"].T for c in cores], axis=0)
    return np.ascontiguousarray(out)



def emit_eqsum(q, acc, ind, c_b, lut, t0, t1, deps):
    ev = q.op(lambda qq, t=t0: qq.tensor_scalar(
        acc[:], c_b[:], float(t), lut[:, t:t + 1], ALU.is_equal, ALU.mult),
        *deps)
    for t in range(t0 + 1, t1):
        q.op(lambda qq, t=t: qq.tensor_scalar(
            ind[:], c_b[:], float(t), lut[:, t:t + 1], ALU.is_equal, ALU.mult), ev)
        ev = q.op(lambda qq, t=t: qq.tensor_tensor(acc[:], acc[:], ind[:], ALU.add))
    return ev


DEQ_EXT = dict(qwp=([512, 1024], I32), maskc=([128, 32], I32),
               lut4c=([512, 16], F32), lut8c=([512, 256], F32),
               ident=([128, 128], BF16))

DVE_HI = 160


def alloc_dequant(lb):
    DQF = ["qwq", "tmask", "tti", "c8b", "indd", "accl", "acch", "l4", "l8"]
    b = dict(
        qw=lb.sbuf("qwq", [128, 1024], I32),
        tmask=lb.sbuf("tmask", [128, 32], I32),
        tti=lb.sbuf("tti", [128, 1024], I32),
        c8b=lb.sbuf("c8b", [128, 2048], BF16),
        indd=lb.sbuf("indd", [128, 2048], BF16),
        accl=lb.sbuf("accl", [128, 2048], BF16),
        acch=lb.sbuf("acch", [128, 2048], BF16),
        l4=lb.sbuf("l4", [128, 16], F32),
        l8=lb.sbuf("l8", [128, 256], F32),
    )
    b["names"] = DQF
    return b


def emit_dequant_gen(lb, P, ext, bufs, entry=()):
    qwp, maskc = ext["qwp"], ext["maskc"]
    lut4c, lut8c = ext["lut4c"], ext["lut8c"]
    wl_o, wh_o = ext["wl_o"], ext["wh_o"]

    DQF = bufs["names"]
    qw, tmask, tti = bufs["qw"], bufs["tmask"], bufs["tti"]
    c8b, indd = bufs["c8b"], bufs["indd"]
    accl, acch, l4, l8 = bufs["accl"], bufs["acch"], bufs["l4"], bufs["l8"]
    tb = indd[:, 0:1024]

    dve, pool, gd = P["dve"], P["pool"], P["gd"]
    cm = gd.dma(tmask[:], maskc[:, :], *entry)
    ld = gd.dma(qw[:], qwp[0:128, :])
    ll4 = gd.dma(l4[:], lut4c[0:128, :])
    ll8 = gd.dma(l8[:], lut8c[0:128, :])
    yield

    mexp = tmask[:].unsqueeze(1).broadcast_to([128, 32, 32])
    tti3 = tti[:].rearrange("p (w s) -> p w s", s=32)
    prev = {}
    for ot in range(4):
        for hf in range(2):
            def wexp(pl, hf=hf, sub=0):
                base = pl * 128 + hf * 64 + sub * 32
                return (qw[:, base:base + 32]
                        .unsqueeze(2).broadcast_to([128, 32, 32]))

            e = None
            for sub in range(2):
                csl = slice(sub * 1024, (sub + 1) * 1024)
                e = dve.op(lambda q, wexp=wexp, sub=sub: q.tensor_tensor(
                    tti3, wexp(0, sub=sub), mexp, ALU.bitwise_and), ld, cm)
                e = dve.op(lambda q, csl=csl: q.tensor_scalar(
                    c8b[:, csl], tti[:], 0, None, ALU.not_equal))
                for pl in range(1, 4):
                    e = dve.op(lambda q, pl=pl, wexp=wexp, sub=sub: q.tensor_tensor(
                        tti3, wexp(pl, sub=sub), mexp, ALU.bitwise_and))
                    e = dve.op(lambda q: q.tensor_scalar(
                        tb, tti[:], 0, None, ALU.not_equal))
                    e = dve.op(lambda q, pl=pl, csl=csl: q.scalar_tensor_tensor(
                        c8b[:, csl], tb, float(1 << pl), c8b[:, csl],
                        ALU.mult, ALU.add))
            evl = emit_eqsum(dve, accl, indd, c8b, l4, 0, 16,
                             [e, ll4, prev.get("sl")])
            e = evl
            for sub in range(2):
                csl = slice(sub * 1024, (sub + 1) * 1024)
                for pl in range(4, 8):
                    e = dve.op(lambda q, pl=pl, wexp=wexp, sub=sub: q.tensor_tensor(
                        tti3, wexp(pl, sub=sub), mexp, ALU.bitwise_and))
                    e = dve.op(lambda q: q.tensor_scalar(
                        tb, tti[:], 0, None, ALU.not_equal))
                    e = dve.op(lambda q, pl=pl, csl=csl: q.scalar_tensor_tensor(
                        c8b[:, csl], tb, float(1 << pl), c8b[:, csl],
                        ALU.mult, ALU.add))
            prev["extr"] = e
            prev["sl"] = gd.dma(wl_o[ot * 128:(ot + 1) * 128,
                                     hf * 2048:(hf + 1) * 2048], accl[:], evl)
            yield
            evh = emit_eqsum(dve, acch, indd, c8b, l8, 0, 256,
                             [e, ll8, prev.get("sh")])
            prev["sh"] = gd.dma(wh_o[ot * 128:(ot + 1) * 128,
                                     hf * 2048:(hf + 1) * 2048], acch[:], evh)
            if hf == 1 and ot + 1 < 4:
                ld = gd.dma(qw[:], qwp[(ot + 1) * 128:(ot + 2) * 128, :],
                            prev["extr"])
                ll4 = gd.dma(l4[:], lut4c[(ot + 1) * 128:(ot + 2) * 128, :],
                             evl)
                ll8 = gd.dma(l8[:], lut8c[(ot + 1) * 128:(ot + 2) * 128, :],
                             evh)
            yield
    ext["deq_ev"] = (prev["sl"], prev["sh"])
    ext["deq_done"] = (prev["sh"], prev["sl"])
    ext["deq_names"] = DQF


def emit_dequant_tr(lb, P, ext, ps, deps, entry=()):
    pe, act, ad, sd = P["pe"], P["act"], P["ad"], P["sd"]
    wl_o, wh_o = ext["wl_o"], ext["wh_o"]
    wlT, whT = ext["wlT"], ext["whT"]
    idt2 = lb.sbuf("idt2", [128, 128], BF16)
    acc2 = [lb.sbuf(f"acc2_{i}", [128, D], BF16) for i in range(2)]
    slab = lb.sbuf("slab", [128, FC * 512], BF16)
    pst = [ps[:, 2 * i, :].bitcast(BF16) for i in range(4)]
    ci = sd.dma(idt2[:], ext["ident"][:, :], *(deps + tuple(entry)))
    evs = []
    tr_ev = {}
    for mi, (src_d, dst_d) in enumerate(((wl_o, wlT), (wh_o, whT))):
        for ot in range(4):
            lda = sd.dma(acc2[ot % 2][:], src_d[ot * 128:(ot + 1) * 128, :],
                         tr_ev.get((mi, ot - 2, FC - 1)))
            for ic in range(FC):
                b = 2 * (ot % 2) + ic % 2
                mmt = pe.op(lambda q, ic=ic, b=b, ot=ot: q.transpose(
                    pst[b][:, 0:128], acc2[ot % 2][:, ic * 128:(ic + 1) * 128],
                    idt2[:]), lda, ci, tr_ev.get((mi, ot, ic - 2)),
                    tr_ev.get((mi - 1, ot, ic)) if False else None,
                    *(entry if mi == 0 and ot == 0 and ic < 2 else ()))
                tv = act.op(lambda q, ic=ic, ot=ot, b=b: q.copy(
                    slab[:, ic * 512 + ot * 128: ic * 512 + (ot + 1) * 128],
                    pst[b][:, 0:128]), mmt, evs[-1] if (ot == 0 and ic == 0 and evs) else None)
                tr_ev[(mi, ot, ic)] = tv
        evs.append(ad.dma(dst_d.rearrange("(ic p) o -> p ic o", p=128),
                          slab.rearrange("p (ic o) -> p ic o", o=512),
                          tr_ev[(mi, 3, FC - 1)]))
    lb.free("slab", "acc2_1", "acc2_0", "idt2")
    return tuple(evs)


def build_dequant():
    lb = LB("dequant")
    nc = lb.nc
    ext = {k: nc.dram_tensor(k, sh, dt, kind="ExternalInput").ap()
           for k, (sh, dt) in DEQ_EXT.items()}
    ext["wlT"] = nc.dram_tensor("wlT", [D, 512], BF16, kind="ExternalOutput").ap()
    ext["whT"] = nc.dram_tensor("whT", [D, 512], BF16, kind="ExternalOutput").ap()
    ext["wl_o"] = nc.dram_tensor("wl_o", [512, D], BF16).ap()
    ext["wh_o"] = nc.dram_tensor("wh_o", [512, D], BF16).ap()
    ps = lb.psum("ps", [128, 8, 512], F32)
    P = lb.mkqueues()
    e1, e2, _ = emit_dequant(lb, P, ext, ps)
    e3, e4 = emit_dequant_tr(lb, P, ext, ps, (e1, e2))
    P["pe"].barrier(e3, e4)
    lb.emit()
    return lb


def prep_dequant_inputs(inp, core):
    qw = inp["L1_ffn_qweight"]
    rows = slice(512 * core, 512 * (core + 1))
    return dict(
        qwp=np.ascontiguousarray(qw[:, rows, :].transpose(1, 0, 2).reshape(512, 1024)),
        maskc=np.broadcast_to((np.int32(1) << np.arange(32, dtype=np.int32))[None, :], (128, 32)).copy(),
        lut4c=np.ascontiguousarray(inp["L1_ffn_lut4"][rows].astype(np.float32)),
        lut8c=np.ascontiguousarray(inp["L1_ffn_lut8"][rows].astype(np.float32)),
        ident=np.eye(128, dtype=BF),
    )



FFN1_EXT = dict(x3=([D, S], BF16), bcols1=([128, 32], F32),
                ones_col=([128, 1], BF16), ones_row=([1, 128], BF16))


def emit_ffn1(lb, P, ext, ps, entry=()):
    x3, gl, gh = ext["x3"], ext["gl"], ext["gh"]
    bcols = ext["bcols1"]
    ones_col, ones_row = ext["ones_col"], ext["ones_row"]
    out = ext["out"]

    ln = [lb.sbuf(f"ln{i}", [128, S], BF16) for i in range(FC)]
    onc = lb.sbuf("onc", [128, 1], BF16)
    onr = lb.sbuf("onr", [1, 128], BF16)
    bcs = lb.sbuf("bcs", [128, 32], F32)
    mB = lb.sbuf("mB", [128, S], BF16)
    rB = lb.sbuf("rB", [128, S], BF16)
    maskB = lb.sbuf("maskB", [128, S], BF16)
    pe, act, dve = P["pe"], P["act"], P["dve"]
    pool = P["pool"]
    sd, gd, ad = P["sd"], P["gd"], P["ad"]

    PH1 = ["xs", "sq", "ra", "rb", "rc", "rd", "rowbt"]
    xs = lb.sbuf("xs", [128, S], BF16)
    sq = lb.sbuf("sq", [128, 512], BF16)
    ra = lb.sbuf("ra", [1, S], F32)
    rb = lb.sbuf("rb", [1, S], F32)
    rc = lb.sbuf("rc", [1, S], F32)
    rd = lb.sbuf("rd", [1, S], F32)
    rowbt = lb.sbuf("rowbt", [1, S], BF16)

    st = dict(ps=ps, xs=xs, sq=sq, onc=onc, onr=onr, rowb=rowbt,
              ra=ra, rb=rb, rc=rc)

    c1 = sd.dma(onc[:], ones_col[:, :], *entry)
    c2 = sd.dma(onr[:], ones_row[:, :])
    c5 = sd.dma(bcs[:], bcols[:, :])

    sev = emit_stats(P, st, lambda fc: x3[fc * 128:(fc + 1) * 128, :], FC,
                     dep=c5, entry=entry, tiles=ln)
    me = dve.op(lambda q: q.tensor_tensor(rc[:], ra[:], ra[:], ALU.mult), sev)
    me = dve.op(lambda q: q.tensor_scalar(rc[:], rc[:], -1.0 / D, None, ALU.mult), me)
    me = dve.op(lambda q: q.tensor_tensor(rc[:], rc[:], rb[:], ALU.add), me)
    me = dve.op(lambda q: q.tensor_scalar(rd[:], rc[:], 1.0 / D, None, ALU.mult), me)
    me = dve.op(lambda q: q.tensor_scalar(rd[:], rd[:], float(LNEPS), None, ALU.add), me)
    me = act_recip(P, rd[:], me)
    me = dve.op(lambda q: q.tensor_tensor(rd[:], rc[:], rd[:], ALU.mult), me)
    me = dve.op(lambda q: q.tensor_scalar(rd[:], rd[:], float(D), None, ALU.is_gt), me)
    mev = emit_mrstd(P, st, me)
    bm = emit_bcast_row(P, st, mev, ra[:], mB, 2)
    br = emit_bcast_row(P, st, mev, rb[:], rB, 3, war=bm)
    bk = emit_bcast_row(P, st, me, rd[:], maskB, 4, war=br)
    lev = emit_ln(P, st, x3, ln, mB, rB, br, resident=True)

    lb.free(*reversed(PH1))
    wcol = [lb.sbuf(f"wcol{i}", [128, D], BF16) for i in range(2)]
    hcol = [lb.sbuf(f"hcol{i}", [128, D], BF16) for i in range(2)]
    x3r = [lb.sbuf(f"x3r{i}", [128, S], BF16) for i in range(2)]
    o32 = [lb.sbuf(f"o32_{i}", [128, S], F32) for i in range(2)]
    ev32 = [lb.sbuf(f"ev32_{i}", [128, 512], F32) for i in range(4)]

    def gcol_load(dst, g, oc, dep, dep2=None):
        c = oc // 4
        off = (oc % 4) * 128
        src = g[c * D:(c + 1) * D, off:off + 128].rearrange("(fc p) o -> p fc o", p=128)
        return sd.dma(dst.rearrange("p (fc o) -> p fc o", o=128), src, dep, dep2)

    wld = [None, None]
    hld = [None, None]
    x3ld = [None, None]
    mm_done = {}
    evac_done = {}
    pdone = {}
    out_done = {}
    if ext.get("cc_ev") is not None:
        for _cce in ext["cc_ev"]:
            sd.op(lambda q: q.nop(), _cce)
    wld[0] = gcol_load(wcol[0], gl, 0, bk, lev)
    hld[0] = gcol_load(hcol[0], gh, 0, None)
    x3ld[0] = sd.dma(x3r[0][:], x3[0:128, :], lev)
    for oc in range(FC):
        pb = oc % 2
        if oc + 1 < FC:
            nb = (oc + 1) % 2
            wld[nb] = gcol_load(wcol[nb], gl, oc + 1, mm_done.get(oc - 1))
            hld[nb] = gcol_load(hcol[nb], gh, oc + 1, None)
            x3ld[nb] = sd.dma(x3r[nb][:], x3[(oc + 1) * 128:(oc + 2) * 128, :],
                              evac_done.get(oc - 1))
        deps = [wld[pb], hld[pb], lev, pdone.get(oc - 1), bk]
        mm = None
        for fc in range(FC):
            for qt in range(QT):
                mm = pe.op(lambda q, fc=fc, qt=qt, pb=pb: q.matmul(
                    ps[:, qt, :], wcol[pb][:, fc * 128:(fc + 1) * 128],
                    ln[fc][:, qt * 512:(qt + 1) * 512],
                    start=(fc == 0), stop=(fc == FC - 1)),
                    *(deps if fc == 0 and qt == 0 else []))
        for fc in range(FC):
            for qt in range(QT):
                mm = pe.op(lambda q, fc=fc, qt=qt, pb=pb: q.matmul(
                    ps[:, 4 + qt, :], hcol[pb][:, fc * 128:(fc + 1) * 128],
                    ln[fc][:, qt * 512:(qt + 1) * 512],
                    start=(fc == 0), stop=(fc == FC - 1)))
        mm_done[oc] = mm
        ee = None
        for qt in range(QT):
            ea = act.op(lambda q, qt=qt, oc=oc: q.activation(
                ev32[qt][:, :], ps[:, qt, :], AF.Identity,
                bias=bcs[:, oc:oc + 1]), mm, c5, evac_done.get(oc - 1))
            eh2 = act.op(lambda q, qt=qt, oc=oc, pb=pb: q.activation(
                o32[pb][:, qt * 512:(qt + 1) * 512], ps[:, 4 + qt, :],
                AF.Identity, bias=bcs[:, oc:oc + 1]),
                out_done.get(oc - 2))
            e1 = pool.op(lambda q, qt=qt, pb=pb: q.tensor_tensor(
                o32[pb][:, qt * 512:(qt + 1) * 512],
                o32[pb][:, qt * 512:(qt + 1) * 512], ev32[qt][:, :],
                ALU.subtract), ea, eh2)
            e2 = pool.op(lambda q, qt=qt, pb=pb: q.tensor_tensor(
                o32[pb][:, qt * 512:(qt + 1) * 512],
                o32[pb][:, qt * 512:(qt + 1) * 512],
                maskB[:, qt * 512:(qt + 1) * 512], ALU.mult), bk)
            e3 = pool.op(lambda q, qt=qt, pb=pb: q.tensor_tensor(
                o32[pb][:, qt * 512:(qt + 1) * 512],
                o32[pb][:, qt * 512:(qt + 1) * 512], ev32[qt][:, :], ALU.add))
            ee = pool.op(lambda q, qt=qt, pb=pb: q.tensor_tensor(
                o32[pb][:, qt * 512:(qt + 1) * 512],
                o32[pb][:, qt * 512:(qt + 1) * 512],
                x3r[pb][:, qt * 512:(qt + 1) * 512], ALU.add), x3ld[pb])
            pdone[oc] = eh2
        evac_done[oc] = ee
        out_done[oc] = ad.dma(out[oc * 128:(oc + 1) * 128, :], o32[pb][:], ee)

    return out_done[FC - 1]


def build_ffn1():
    lb = LB("ffn1")
    nc = lb.nc
    ext = {k: nc.dram_tensor(k, sh, dt, kind="ExternalInput").ap()
           for k, (sh, dt) in FFN1_EXT.items()}
    ext["gl"] = nc.dram_tensor("gl", [NCORES * D, 512], BF16, kind="ExternalInput").ap()
    ext["gh"] = nc.dram_tensor("gh", [NCORES * D, 512], BF16, kind="ExternalInput").ap()
    ext["out"] = nc.dram_tensor("out", [D, S], F32, kind="ExternalOutput").ap()
    ps = lb.psum("ps", [128, 8, 512], F32)
    P = lb.mkqueues()
    done = emit_ffn1(lb, P, ext, ps)
    P["pe"].barrier(done)
    lb.emit()
    return lb


def prep_ffn1_inputs(inp, x3_fm_bf16, gl, gh):
    d = dict(
        bcols1=np.ascontiguousarray(inp["L1_ffn_bias"].astype(np.float32).reshape(32, 128).T),
        ones_col=np.ones((128, 1), BF),
        ones_row=np.ones((1, 128), BF),
    )
    if x3_fm_bf16 is not None:
        d["x3"] = np.ascontiguousarray(x3_fm_bf16)
        d["gl"] = gl
        d["gh"] = gh
    return d



ATTN1_EXT = dict(x2=([D, S], BF16), wqt=([D, 1024], F32), bqc=([128, 8], F32),
                 mrows=([1024, 128], F32), zc=([128, 8], F32), wot=([D, D], F32),
                 ones_col=([128, 1], BF16), ones_row=([1, 128], BF16))


def emit_attn1(lb, P, ext, ps, entry=()):
    pool = None
    x2, wqt, bqc = ext["x2"], ext["wqt"], ext["bqc"]
    mrows, zc, wot = ext["mrows"], ext["zc"], ext["wot"]
    ones_col, ones_row = ext["ones_col"], ext["ones_row"]
    x3, sigq_s = ext["x3"], ext["sigq_s"]

    mcols = lb.sbuf("mcols", [128, 8 * 128], BF16)
    zcs = lb.sbuf("zcs", [128, 8], BF16)
    bqs = lb.sbuf("bqs", [128, 8], F32)
    onc = lb.sbuf("onc", [128, 1], BF16)
    onr = lb.sbuf("onr", [1, 128], BF16)
    pe, act, dve = P["pe"], P["act"], P["dve"]
    pool = P["pool"]
    sd, gd, ad = P["sd"], P["gd"], P["ad"]

    c1 = sd.dma(onc[:], ones_col[:, :], *entry)
    c2 = sd.dma(onr[:], ones_row[:, :])
    c3 = sd.dma(bqs[:], bqc[:, :])
    cm = gd.dma(mcols.rearrange("p (kv e) -> p kv e", e=128),
                mrows.rearrange("(kv p) e -> p kv e", p=128), *entry)
    cz = gd.dma(zcs[:], zc[:, :])

    PH1 = ["xs", "sq", "ra", "rb", "rc", "rowbt", "mB", "rB"] + \
          [f"lnt{i}" for i in range(FC)] + ["wq0", "wq1", "e32a", "e32b", "sgb"]
    xs = lb.sbuf("xs", [128, S], BF16)
    sq = lb.sbuf("sq", [128, 512], BF16)
    ra = lb.sbuf("ra", [1, S], F32)
    rb = lb.sbuf("rb", [1, S], F32)
    rc = lb.sbuf("rc", [1, S], F32)
    rowbt = lb.sbuf("rowbt", [1, S], BF16)
    mB = lb.sbuf("mB", [128, S], BF16)
    rB = lb.sbuf("rB", [128, S], BF16)
    ln = [lb.sbuf(f"lnt{i}", [128, S], BF16) for i in range(FC)]
    wq = [lb.sbuf("wq0", [128, D], BF16), lb.sbuf("wq1", [128, D], BF16)]
    e32 = [lb.sbuf("e32a", [128, 512], F32), lb.sbuf("e32b", [128, 512], F32)]
    sgb = lb.sbuf("sgb", [128, S], BF16)

    st = dict(ps=ps, xs=xs, sq=sq, onc=onc, onr=onr, rowb=rowbt,
              ra=ra, rb=rb, rc=rc)
    sev = emit_stats(P, st, lambda fc: x2[fc * 128:(fc + 1) * 128, :], FC,
                     dep=c3, entry=entry, tiles=ln)
    mev = emit_mrstd(P, st, sev)
    bm = emit_bcast_row(P, st, mev, ra[:], mB, 2)
    br = emit_bcast_row(P, st, mev, rb[:], rB, 3, war=bm)
    lev = emit_ln(P, st, x2, ln, mB, rB, br, resident=True)

    wld = [None, None]
    mm_done = {}
    evac_done = {}
    sg_done = {}
    wld[0] = col_load(gd, wq[0], wqt, 0, dep=cz)
    for oc in range(8):
        pb = oc % 2
        half = pb * 4
        if oc + 1 < 8:
            nb = (oc + 1) % 2
            wld[nb] = col_load(gd, wq[nb], wqt, oc + 1, dep=mm_done.get(oc - 1))
        deps = [wld[pb], lev, evac_done.get(oc - 2)]
        mm = None
        for fc in range(FC):
            for qt in range(QT):
                mm = pe.op(lambda q, fc=fc, qt=qt, pb=pb, half=half: q.matmul(
                    ps[:, half + qt, :], wq[pb][:, fc * 128:(fc + 1) * 128],
                    ln[fc][:, qt * 512:(qt + 1) * 512],
                    start=(fc == 0), stop=(fc == FC - 1)),
                    *(deps if fc == 0 and qt == 0 else []))
        mm_done[oc] = mm
        ee = None
        for qt in range(QT):
            eb = qt % 2
            ea = act.op(lambda q, qt=qt, half=half, eb=eb, oc=oc: q.activation(
                e32[eb][:, :], ps[:, half + qt, :], AF.Identity,
                bias=bqs[:, oc:oc + 1]), mm, c3, evac_done.get(oc - 1),
                sg_done.get((oc, qt - 2)))
            e1 = pool.op(lambda q, qt=qt, eb=eb: q.tensor_scalar(
                sgb[:, qt * 512:(qt + 1) * 512], e32[eb][:, :], 0.0, None, ALU.min),
                ea, sg_done.get(oc - 1) if qt == 0 else None)
            e2 = act.op(lambda q, qt=qt: q.activation(
                sgb[:, qt * 512:(qt + 1) * 512], sgb[:, qt * 512:(qt + 1) * 512],
                AF.Exp), e1)
            e3 = pool.op(lambda q, qt=qt, eb=eb: q.tensor_scalar(
                e32[eb][:, :], e32[eb][:, :], 0.0, None, ALU.max), e2)
            ee = pool.op(lambda q, qt=qt, eb=eb: q.tensor_tensor(
                sgb[:, qt * 512:(qt + 1) * 512],
                sgb[:, qt * 512:(qt + 1) * 512], e32[eb][:, :], ALU.add))
            sg_done[(oc, qt)] = ee
        evac_done[oc] = ee
        sg_done[oc] = ad.dma(sigq_s[oc * 128:(oc + 1) * 128, :], sgb[:], ee)

    lb.free(*reversed(PH1))
    ret = [lb.sbuf(f"ret{i}", [128, S], BF16) for i in range(8)]
    sgt = lb.sbuf("sgt", [128, S], BF16)
    recB = lb.sbuf("recB", [128, S], F32)
    rd2 = lb.sbuf("rd2", [1, S], F32)
    rw2 = lb.sbuf("rw2", [1, S], BF16)
    x2r = [lb.sbuf(f"x2r{i}", [128, S], BF16) for i in range(2)]
    x3t = [lb.sbuf(f"x3t{i}", [128, S], BF16) for i in range(2)]
    wo = [lb.sbuf("wo0", [128, D], BF16), lb.sbuf("wo1", [128, D], BF16)]
    f32e = [lb.sbuf(f"f32e{i}", [128, 512], F32) for i in range(4)]

    ret_done = {}
    prevkv = {}
    ph1_done = sg_done[7]
    for kv in range(8):
        sld = sd.dma(sgt[:], sigq_s[kv * 128:(kv + 1) * 128, :], ph1_done,
                     prevkv.get(kv - 1))
        mmn = None
        for qt in range(QT):
            mmn = pe.op(lambda q, qt=qt, kv=kv: q.matmul(
                ps[:, qt, :], mcols[:, kv * 128:(kv + 1) * 128],
                sgt[:, qt * 512:(qt + 1) * 512], start=True, stop=True),
                sld, cm, prevkv.get(("num", kv - 1)))
        edn = None
        for qt in range(QT):
            mmd = pe.op(lambda q, qt=qt, kv=kv: q.matmul(
                ps[0:1, 4, :], zcs[:, kv:kv + 1],
                sgt[:, qt * 512:(qt + 1) * 512], start=True, stop=True),
                edn if edn else (prevkv.get(("den", kv - 1)) or sld))
            edn = act.op(lambda q, qt=qt: q.copy(rd2[0:1, qt * 512:(qt + 1) * 512],
                                                 ps[0:1, 4, :]), mmd)
        er = dve.op(lambda q: q.tensor_scalar(rd2[:], rd2[:], 1e-6, None, ALU.add), edn)
        er = act_recip(P, rd2[:], er)
        er = dve.op(lambda q: q.tensor_copy(rw2[0:1, :], rd2[:]), er, prevkv.get(("bc", kv - 1)))
        eb2 = None
        for qt in range(QT):
            mmb = pe.op(lambda q, qt=qt: q.matmul(
                ps[:, 5, :], onr[:], rw2[0:1, qt * 512:(qt + 1) * 512],
                start=True, stop=True), er, eb2)
            eb2 = act.op(lambda q, qt=qt: q.copy(
                recB[:, qt * 512:(qt + 1) * 512], ps[:, 5, :]), mmb)
        prevkv[("bc", kv)] = eb2
        ev = None
        for qt in range(QT):
            eva = act.op(lambda q, qt=qt: q.copy(
                sgt[:, qt * 512:(qt + 1) * 512], ps[:, qt, :]), mmn, eb2,
                prevkv.get(("rm", kv - 1)))
            ev = dve.op(lambda q, qt=qt, kv=kv: q.tensor_tensor(
                ret[kv][:, qt * 512:(qt + 1) * 512],
                sgt[:, qt * 512:(qt + 1) * 512],
                recB[:, qt * 512:(qt + 1) * 512], ALU.mult), eva)
        prevkv[("rm", kv)] = ev
        prevkv[("num", kv)] = ev
        prevkv[("den", kv)] = edn
        prevkv[kv] = ev
        ret_done[kv] = ev
    all_ret = ret_done[7]

    wld = [None, None]
    mm_done = {}
    evac_done = {}
    out_done = {}
    x2ld = [None, None]
    wld[0] = col_load(gd, wo[0], wot, 0, dep=ph1_done)
    x2ld[0] = sd.dma(x2r[0][:], x2[0:128, :], all_ret)
    for oc in range(FC):
        pb = oc % 2
        half = pb * 4
        if oc + 1 < FC:
            nb = (oc + 1) % 2
            wld[nb] = col_load(gd, wo[nb], wot, oc + 1, dep=mm_done.get(oc - 1))
            x2ld[nb] = sd.dma(x2r[nb][:], x2[(oc + 1) * 128:(oc + 2) * 128, :],
                              evac_done.get(oc - 1))
        deps = [wld[pb], all_ret, evac_done.get(oc - 2)]
        mm = None
        for hc in range(FC):
            for qt in range(QT):
                mm = pe.op(lambda q, hc=hc, qt=qt, pb=pb, half=half: q.matmul(
                    ps[:, half + qt, :], wo[pb][:, hc * 128:(hc + 1) * 128],
                    ret[hc // 4][:, qt * 512:(qt + 1) * 512],
                    start=(hc == 0), stop=(hc == FC - 1)),
                    *(deps if hc == 0 and qt == 0 else []))
        mm_done[oc] = mm
        ee = None
        for qt in range(QT):
            ea = act.op(lambda q, qt=qt, half=half: q.copy(
                f32e[qt][:, :], ps[:, half + qt, :]), mm, evac_done.get(oc - 1))
            ee = pool.op(lambda q, qt=qt, pb=pb: q.tensor_tensor(
                x3t[pb][:, qt * 512:(qt + 1) * 512], f32e[qt][:, :],
                x2r[pb][:, qt * 512:(qt + 1) * 512], ALU.add),
                ea, x2ld[pb], out_done.get(oc - 2))
        evac_done[oc] = ee
        out_done[oc] = ad.dma(x3[oc * 128:(oc + 1) * 128, :], x3t[pb][:], ee)

    lb.free("f32e3", "f32e2", "f32e1", "f32e0", "wo1", "wo0", "x3t1", "x3t0",
            "x2r1", "x2r0", "rw2", "rd2", "recB", "sgt",
            "ret7", "ret6", "ret5", "ret4", "ret3", "ret2", "ret1", "ret0",
            "onr", "onc", "bqs", "zcs", "mcols")
    return out_done[FC - 1]


def build_attn1():
    lb = LB("attn1")
    nc = lb.nc
    ext = {k: nc.dram_tensor(k, sh, dt, kind="ExternalInput").ap()
           for k, (sh, dt) in ATTN1_EXT.items()}
    ext["x3"] = nc.dram_tensor("x3", [D, S], BF16, kind="ExternalOutput").ap()
    ext["sigq_s"] = nc.dram_tensor("sigq_s", [1024, S], BF16).ap()
    ps = lb.psum("ps", [128, 8, 512], F32)
    P = lb.mkqueues()
    done = emit_attn1(lb, P, ext, ps)
    P["pe"].barrier(done)
    lb.emit()
    return lb


def prep_attn1_inputs(inp, x2_fm_bf16):
    sel = np.concatenate([np.arange(h * 128, (h + 1) * 128) for h in range(0, 32, 4)])
    Wq = inp["L1_attn_Wq"].astype(np.float32)
    bq = inp["L1_attn_bq"].astype(np.float32)
    d = dict(
        wqt=np.ascontiguousarray(Wq[sel].T),
        bqc=np.ascontiguousarray(bq[sel].reshape(8, 128).T),
        mrows=np.ascontiguousarray(inp["L1_attn_M"].astype(np.float32).reshape(1024, 128)),
        zc=np.ascontiguousarray(inp["L1_attn_z"].astype(np.float32).T),
        wot=np.ascontiguousarray(inp["L1_attn_Wo"].astype(np.float32).T),
        ones_col=np.ones((128, 1), BF),
        ones_row=np.ones((1, 128), BF),
    )
    if x2_fm_bf16 is not None:
        d["x2"] = np.ascontiguousarray(x2_fm_bf16)
    return d



ATTN0_EXT = dict(xf=([D, S], F32), wqkvt=([D, 3 * D], F32), bqkc=([128, 96], F32),
                 wout=([D, D], F32), boc=([128, 32], F32), alibi=([128, 512], F32),
                 masks=([128, 4 * 512], BF16), qcorr=([32, S], BF16),
                 identm=([128, 128], BF16), ones_col=([128, 1], BF16),
                 ones_row=([1, 128], BF16))


def emit_attn0(lb, P, ext, ps, entry=(), nheads=32, mid_hook=None):
    no_tr = no_mask = no_den = no_av = False
    xf, wqkvt, bqkc, wout, boc = (ext["xf"], ext["wqkvt"], ext["bqkc"],
                                  ext["wout"], ext["boc"])
    alibi, masks, qcorr, identm = (ext["alibi"], ext["masks"], ext["qcorr"],
                                   ext["identm"])
    ones_col, ones_row = ext["ones_col"], ext["ones_row"]
    x1, qkv_s = ext["x1"], ext["qkv_s"]
    pool = P["pool"]

    alb = lb.sbuf("alb", [128, 512], F32)
    msk = lb.sbuf("msk", [128, 4 * 512], BF16)
    onc = lb.sbuf("onc", [128, 1], BF16)
    onr = lb.sbuf("onr", [1, 128], BF16)
    bqk = lb.sbuf("bqk", [128, 96], F32)
    bos = lb.sbuf("bos", [128, 32], F32)
    idt = lb.sbuf("idt", [128, 128], BF16)
    pe, act, dve = P["pe"], P["act"], P["dve"]
    sd, gd, ad = P["sd"], P["gd"], P["ad"]

    c1 = sd.dma(onc[:], ones_col[:, :], *entry)
    c2 = sd.dma(onr[:], ones_row[:, :])
    c3 = sd.dma(bqk[:], bqkc[:, :])
    c4 = sd.dma(alb[:], alibi[:, :])
    c5 = sd.dma(msk[:], masks[:, :])
    c6 = sd.dma(bos[:], boc[:, :])
    c8 = sd.dma(idt[:], identm[:, :])

    deq_bufs = alloc_dequant(lb) if mid_hook is not None else None
    ln = [lb.sbuf(f"ln{i}", [128, S], BF16) for i in range(FC)]
    PH0 = ["sq", "ra", "rb", "rc", "rowbt", "mB", "rB"]
    sq = lb.sbuf("sq", [128, 512], BF16)
    ra = lb.sbuf("ra", [1, S], F32)
    rb = lb.sbuf("rb", [1, S], F32)
    rc = lb.sbuf("rc", [1, S], F32)
    rowbt = lb.sbuf("rowbt", [1, S], BF16)
    mB = lb.sbuf("mB", [128, S], BF16)
    rB = lb.sbuf("rB", [128, S], BF16)
    st = dict(ps=ps, xs=None, sq=sq, onc=onc, onr=onr,
              rowb=rowbt, ra=ra, rb=rb, rc=rc)
    sev = emit_stats(P, st, lambda fc: xf[fc * 128:(fc + 1) * 128, :], FC,
                     dep=c8, from_f32=True, entry=entry, tiles=ln)
    mev = emit_mrstd(P, st, sev)
    bm = emit_bcast_row(P, st, mev, ra[:], mB, 2)
    br = emit_bcast_row(P, st, mev, rb[:], rB, 3, war=bm)
    lev = emit_ln(P, st, xf, ln, mB, rB, br, resident=True)
    lb.free(*reversed(PH0))

    deq_gen, deq_done, deq_names = None, (), None
    if mid_hook is not None:
        deq_gen = mid_hook(lev, deq_bufs)
        next(deq_gen, None)

    PH1 = ["wc0", "wc1", "qkb0", "qkb1"]
    wc = [lb.sbuf("wc0", [128, D], BF16), lb.sbuf("wc1", [128, D], BF16)]
    qkb = [lb.sbuf("qkb0", [128, S], BF16), lb.sbuf("qkb1", [128, S], BF16)]

    wld = [None, None]
    mm_done = {}
    evac_done = {}
    out_done = {}
    wld[0] = col_load(gd, wc[0], wqkvt, 0, 128, lev, *entry)
    for oc in range(96):
        if deq_gen is not None and oc >= 12 and (oc - 12) % 9 == 0:
            next(deq_gen, None)
        pb = oc % 2
        if oc + 1 < 96:
            nb = (oc + 1) % 2
            wld[nb] = col_load(gd, wc[nb], wqkvt, oc + 1, dep=mm_done.get(oc - 1))
        half = pb * 4
        deps = [wld[pb], lev, evac_done.get(oc - 2)]
        mm = None
        for fc in range(FC):
            for qt in range(QT):
                mm = pe.op(lambda q, fc=fc, qt=qt, pb=pb, half=half: q.matmul(
                    ps[:, half + qt, :], wc[pb][:, fc * 128:(fc + 1) * 128],
                    ln[fc][:, qt * 512:(qt + 1) * 512],
                    start=(fc == 0), stop=(fc == FC - 1)),
                    *(deps if fc == 0 and qt == 0 else []))
        ee = None
        for qt in range(QT):
            ee = act.op(lambda q, qt=qt, half=half, oc=oc, pb=pb: q.activation(
                qkb[pb][:, qt * 512:(qt + 1) * 512], ps[:, half + qt, :],
                AF.Identity, bias=bqk[:, oc:oc + 1]), mm, c3,
                evac_done.get(oc - 1), out_done.get(oc - 2))
            evac_done[(oc, qt)] = ee
        evac_done[oc] = ee
        mm_done[oc] = mm
        out_done[oc] = ad.dma(qkv_s[oc * 128:(oc + 1) * 128, :], qkb[pb][:],
                              evac_done[oc])
    qkv_done = out_done[95]
    pe_a1 = pe.ev()
    act_a1 = act.ev()
    lb.free(*reversed(PH1))
    lb.free(*[f"ln{i}" for i in reversed(range(FC))])

    SCL = float(1.0 / np.sqrt(128.0))
    afm = [lb.sbuf(f"afm{i}", [128, S], BF16) for i in range(FC)]
    PH2 = ["qh_sb", "qcr", "kh_sb", "vh_sb", "vht", "ex0", "ex1", "rdn", "rwb",
           "recB", "ava"]
    qh_sb = lb.sbuf("qh_sb", [128, S], BF16)
    qcr = lb.sbuf("qcr", [1, S], BF16)
    kh_sb = lb.sbuf("kh_sb", [128, S], BF16)
    vh_sb = lb.sbuf("vh_sb", [128, S], BF16)
    vht = lb.sbuf("vht", [128, S], BF16)
    ex = [lb.sbuf("ex0", [128, 512], BF16), lb.sbuf("ex1", [128, 512], BF16)]
    rdn = lb.sbuf("rdn", [1, S], F32)
    rwb = lb.sbuf("rwb", [1, S], BF16)
    recB = lb.sbuf("recB", [128, 512], F32)
    ava = lb.sbuf("ava", [128, 512], F32)

    hd_ld = {}
    hev = {}
    for h in range(nheads):
        if deq_gen is not None and h in (1, 4, 6, 9, 11, 14):
            next(deq_gen, None)
        lq = sd.dma(qh_sb[:], qkv_s[h * 128:(h + 1) * 128, :],
                    qkv_done, hev.get(("smm", h - 1)),
                    pe_a1 if h == 0 else None, act_a1 if h == 0 else None)
        lk = sd.dma(kh_sb[:], qkv_s[D + h * 128:D + (h + 1) * 128, :])
        lqc = sd.dma(qcr[:], qcorr[h:h + 1, :], hev.get(("smm", h - 1)))
        lv = sd.dma(vh_sb[:], qkv_s[2 * D + h * 128:2 * D + (h + 1) * 128, :],
                    hev.get(("tr", h - 1)))
        tv = None
        pstv = [ps[:, 4, :].bitcast(BF16), ps[:, 5, :].bitcast(BF16)]
        for kc in range(0 if no_tr else 16):
            tb = kc % 2
            mmt = pe.op(lambda q, kc=kc, tb=tb, pstv=pstv: q.transpose(
                pstv[tb][:, 0:128],
                vh_sb[:, kc * 128:(kc + 1) * 128], idt[:]),
                lv, c8, hev.get(("av", h - 1)), hev.get(("trev", (h, kc - 2))),
                hev.get(("rb", (h - 1, 3))))
            tv = act.op(lambda q, kc=kc, tb=tb, pstv=pstv: q.copy(
                vht[:, kc * 128:(kc + 1) * 128], pstv[tb][:, 0:128]), mmt)
            hev[("trev", (h, kc))] = tv
        hev[("tr", h)] = tv
        smm_last = None
        for qtg in range(4):
            kmax = 4 * qtg + 4

            def smm_pair(kc0, qtg=qtg, h=h):
                diag = kc0 >= 4 * qtg
                ends = {}
                for kc in (kc0, kc0 + 1):
                    pe.op(lambda q, kc=kc, sb=kc % 2, qtg=qtg: q.matmul(
                        ps[:, sb, :], kh_sb[:, kc * 128:(kc + 1) * 128],
                        qh_sb[:, qtg * 512:(qtg + 1) * 512], start=True,
                        stop=False),
                        lq, lk, hev.get(("exp", (h, qtg, kc - 2))),
                        hev.get(("sprev", h)) if qtg == 0 and kc == 0 else None)
                for kc in (kc0, kc0 + 1):
                    ends[kc] = pe.op(lambda q, sb=kc % 2, qtg=qtg, diag=diag:
                                     q.matmul(
                        ps[:, sb, :], onr[:], qcr[0:1, qtg * 512:(qtg + 1) * 512],
                        start=False, stop=not diag), lqc)
                if diag:
                    for kc in (kc0, kc0 + 1):
                        ends[kc] = pe.op(lambda q, sb=kc % 2, v=kc - 4 * qtg:
                                         q.matmul(
                            ps[:, sb, :], idt[:], msk[:, v * 512:(v + 1) * 512],
                            start=False, stop=True), c5, c8)
                return ends

            def exp_emit(kc, smm, qtg=qtg, h=h):
                sb = kc % 2
                eex = act.op(lambda q, kc=kc, sb=sb, h=h: q.activation(
                    ex[sb][:, :], ps[:, sb, :], AF.Exp, scale=SCL,
                    bias=alb[:, h * 16 + kc:h * 16 + kc + 1]),
                    smm, c4, hev.get(("avrd", (h, qtg, kc - 2))))
                hev[("exp", (h, qtg, kc))] = eex
                return eex

            def avden_emit(kc, qtg=qtg, h=h, kmax=kmax):
                sb = kc % 2
                eex = hev[("exp", (h, qtg, kc))]
                pe.op(lambda q, kc=kc, sb=sb, kmax=kmax: q.matmul(
                    ps[:, 2, :], vht[:, kc * 128:(kc + 1) * 128], ex[sb][:, :],
                    start=(kc == 0), stop=(kc == kmax - 1)),
                    eex, tv, hev.get(("attn", (h, qtg - 1))) if kc == 0 else None)
                den_mm = pe.op(lambda q, kc=kc, sb=sb, kmax=kmax: q.matmul(
                    ps[0:1, 3, :], onc[:], ex[sb][:, :],
                    start=(kc == 0), stop=(kc == kmax - 1)),
                    hev.get(("dene", (h, qtg - 1))) if kc == 0 else None)
                hev[("avrd", (h, qtg, kc))] = den_mm
                return den_mm

            for kc0 in range(0, kmax, 2):
                ends = smm_pair(kc0)
                smm_last = ends[kc0 + 1]
                if kc0 >= 2:
                    avden_emit(kc0 - 2)
                    avden_emit(kc0 - 1)
                exp_emit(kc0, ends[kc0])
                exp_emit(kc0 + 1, ends[kc0 + 1])
            avden_emit(kmax - 2)
            den_mm = avden_emit(kmax - 1)
            av_last = den_mm
            ed = act.op(lambda q, qtg=qtg: q.copy(rdn[0:1, qtg * 512:(qtg + 1) * 512],
                                                  ps[0:1, 3, :]), den_mm)
            hev[("dene", (h, qtg))] = ed
            er = act_recip(P, rdn[0:1, qtg * 512:(qtg + 1) * 512], ed)
            er = pool.op(lambda q, qtg=qtg: q.tensor_copy(
                rwb[0:1, qtg * 512:(qtg + 1) * 512],
                rdn[0:1, qtg * 512:(qtg + 1) * 512]), er)
            bmm = pe.op(lambda q, qtg=qtg: q.matmul(
                ps[:, 4, :], onr[:], rwb[0:1, qtg * 512:(qtg + 1) * 512],
                start=True, stop=True), er, hev.get(("rb", (h, qtg - 1))),
                hev.get(("tr", h)))
            eb = act.op(lambda q: q.copy(recB[:, :], ps[:, 4, :]), bmm)
            hev[("rb", (h, qtg))] = eb
            eva = act.op(lambda q, qtg=qtg: q.copy(ava[:, :], ps[:, 2, :]),
                         av_last, hev.get(("ava", (h, qtg - 1))))
            ea = pool.op(lambda q, h=h, qtg=qtg: q.tensor_tensor(
                afm[h][:, qtg * 512:(qtg + 1) * 512], ava[:, :], recB[:, :],
                ALU.mult), eva, eb)
            hev[("ava", (h, qtg))] = ea
            hev[("attn", (h, qtg))] = ea
        hev[("smm", h)] = smm_last
        hev[("av", h)] = hev[("attn", (h, 3))]
        hev[("sprev", h + 1)] = hev[("exp", (h, 3, 15))]
    attn_all = hev[("attn", (nheads - 1, 3))]
    pe_a2 = pe.ev()
    act_a2 = act.ev()
    if deq_gen is not None:
        for _ in deq_gen:
            pass
        deq_done = ext["deq_done"]
        deq_names = ext["deq_names"]
    lb.free(*reversed(PH2))

    wc3 = [lb.sbuf("wo0", [128, D], BF16), lb.sbuf("wo1", [128, D], BF16)]
    xr = [lb.sbuf(f"xr{i}", [128, S], F32) for i in range(2)]
    x1t = [lb.sbuf(f"x1t{i}", [128, S], BF16) for i in range(2)]
    f32e = [lb.sbuf(f"f32o{i}", [128, 512], F32) for i in range(2)]
    wld = [None, None]
    mm_done = {}
    evac_done = {}
    out_done = {}
    xld = [None, None]
    wld[0] = col_load(gd, wc3[0], wout, 0, 128, attn_all, pe_a2, act_a2)
    xld[0] = sd.dma(xr[0][:], xf[0:128, :], attn_all, pe_a2, act_a2,
                    *deq_done)
    for oc in range(FC):
        pb = oc % 2
        if oc + 1 < FC:
            nb = (oc + 1) % 2
            wld[nb] = col_load(gd, wc3[nb], wout, oc + 1, dep=mm_done.get(oc - 1))
            xld[nb] = sd.dma(xr[nb][:], xf[(oc + 1) * 128:(oc + 2) * 128, :],
                             evac_done.get(oc - 1))
        half = pb * 4
        deps = [wld[pb], attn_all, evac_done.get(oc - 2)]
        mm = None
        for hc in range(FC):
            for qt in range(QT):
                mm = pe.op(lambda q, hc=hc, qt=qt, pb=pb, half=half: q.matmul(
                    ps[:, half + qt, :], wc3[pb][:, hc * 128:(hc + 1) * 128],
                    afm[hc][:, qt * 512:(qt + 1) * 512],
                    start=(hc == 0), stop=(hc == FC - 1)),
                    *(deps if hc == 0 and qt == 0 else []))
        ee = None
        ees = {}
        for qt in range(QT):
            ea = act.op(lambda q, qt=qt, half=half, oc=oc: q.activation(
                f32e[qt % 2][:, :], ps[:, half + qt, :], AF.Identity,
                bias=bos[:, oc:oc + 1]), mm, c6, evac_done.get(oc - 1),
                out_done.get(oc - 2), ees.get(qt - 2))
            ee = pool.op(lambda q, qt=qt, pb=pb: q.tensor_tensor(
                x1t[pb][:, qt * 512:(qt + 1) * 512], f32e[qt % 2][:, :],
                xr[pb][:, qt * 512:(qt + 1) * 512], ALU.add), ea, xld[pb])
            ees[qt] = ee
        evac_done[oc] = ee
        mm_done[oc] = mm
        out_done[oc] = ad.dma(x1[oc * 128:(oc + 1) * 128, :], x1t[pb][:], ee)

    lb.free("f32o1", "f32o0", "x1t1", "x1t0", "xr1", "xr0",
            "wo1", "wo0")
    for i in reversed(range(FC)):
        lb.free(f"afm{i}")
    if deq_names:
        lb.free(*reversed(deq_names))
    lb.free("idt", "bos", "bqk", "onr", "onc", "msk", "alb")
    return out_done[FC - 1]


def build_attn0():
    lb = LB("attn0")
    nc = lb.nc
    ext = {k: nc.dram_tensor(k, sh, dt, kind="ExternalInput").ap()
           for k, (sh, dt) in ATTN0_EXT.items()}
    ext["x1"] = nc.dram_tensor("x1", [D, S], BF16, kind="ExternalOutput").ap()
    ext["qkv_s"] = nc.dram_tensor("qkv_s", [3 * D, S], BF16)
    ps = lb.psum("ps", [128, 8, 512], F32)
    P = lb.mkqueues()
    done = emit_attn0(lb, P, ext, ps)
    P["pe"].barrier(done)
    lb.emit()
    return lb


def prep_attn0_inputs(inp, core):
    x = inp["L0_attn_x"][core].astype(np.float32)
    wqkv = inp["L0_attn_w_qkv"].astype(np.float32)
    bqkv = inp["L0_attn_b_qkv"].astype(np.float32)
    slopes = 1.0 / np.power(2.0, np.arange(1, 33, dtype=np.float32) * 0.25)
    alibi = np.zeros((128, 512), np.float32)
    for h in range(32):
        for kc in range(16):
            k = 128 * kc + np.arange(128)
            alibi[:, h * 16 + kc] = slopes[h] * (k - 2047.0)
    masks = np.zeros((128, 4 * 512), BF)
    for v in range(4):
        for p in range(128):
            js = np.arange(512)
            masks[p, v * 512 + js] = np.where(js >= 128 * v + p, 0.0, -1.0e9)
    qs = np.arange(2048, dtype=np.float32)
    qcorr = np.stack([(-slopes[h] * (qs - 2047.0) - 8.0) * np.sqrt(128.0)
                      for h in range(32)]).astype(BF)
    return dict(
        xf=np.ascontiguousarray(x.T),
        wqkvt=np.ascontiguousarray(wqkv.T),
        bqkc=np.ascontiguousarray(bqkv.reshape(96, 128).T),
        wout=np.ascontiguousarray(inp["L0_attn_w_out"].astype(np.float32).T),
        boc=np.ascontiguousarray(inp["L0_attn_b_out"].astype(np.float32).reshape(32, 128).T),
        alibi=alibi,
        masks=masks,
        qcorr=qcorr,
        identm=np.eye(128, dtype=BF),
        ones_col=np.ones((128, 1), BF),
        ones_row=np.ones((1, 128), BF),
    )


# version 12 (final)
# speedup vs baseline: 1.1774x; 1.0227x over previous
import numpy as np
import ml_dtypes
import concourse.bass as bass
import concourse.mybir as mybir
from concourse.bass_utils import run_bass_kernel_spmd

F32 = mybir.dt.float32
BF16 = mybir.dt.bfloat16
I32 = mybir.dt.int32
AF = mybir.ActivationFunctionType
ALU = mybir.AluOpType
BF = ml_dtypes.bfloat16

NCORES = 8
S = 2048
D = 4096
FC = D // 128
QT = S // 512
LNEPS = 1e-5


_GSEQ = [0]


class Q:

    def __init__(self, name, sem, kind):
        self.name, self.sem, self.kind = name, sem, kind
        self.n = 0
        self.waited = {}
        self.items = []

    def _need_wait(self, sem, val):
        key = id(sem)
        if self.waited.get(key, 0) >= val:
            return False
        self.waited[key] = val
        return True

    def op(self, fn, *evs):
        waits = []
        for e in evs:
            if e is None:
                continue
            sem, val = e
            if sem is self.sem:
                if val <= self.n:
                    continue
                raise RuntimeError(f"{self.name}: wait on own future event")
            if self._need_wait(sem, val):
                waits.append((sem, val))
        if self.kind in ("cmp", "dma") and self.n > 0:
            if self._need_wait(self.sem, self.n):
                waits.append((self.sem, self.n))
        step = 16 if self.kind == "dma" else 1
        self.n += step
        _GSEQ[0] += 1
        self.items.append((_GSEQ[0], waits, fn, step))
        return (self.sem, self.n)

    def ev(self):
        return (self.sem, self.n) if self.n > 0 else None

    def barrier(self, *evs):
        return self.op(lambda q: q.nop(), *evs)

    def dma(self, out, in_, *evs, **kw):
        return self.op(lambda q: q.dma_start(out=out, in_=in_, **kw), *evs)

    def replay(self, q):
        replay_merged(q, [self])


def replay_merged(q, queues):
    items = []
    for qu in queues:
        items += [(seq, waits, fn, step, qu.sem) for seq, waits, fn, step in qu.items]
    items.sort(key=lambda x: x[0])
    for seq, waits, fn, step, sem in items:
        for wsem, val in waits:
            q.wait_ge(wsem, val)
        fn(q).then_inc(sem, step)


class LB:

    def __init__(self, name):
        self.nc = bass.Bass("TRN2", target_bir_lowering=False)
        self.name = name
        self._ctx = []
        self._by_name = {}

    def _enter(self, cm):
        h = cm.__enter__()
        self._ctx.append(cm)
        return h

    def sbuf(self, name, shape, dtype):
        self._uid = getattr(self, "_uid", 0) + 1
        h = self._enter(self.nc.sbuf_tensor(f"{name}_u{self._uid}", shape, dtype))
        self._by_name.setdefault(name, []).append(self._ctx[-1])
        return h

    def free(self, *names):
        for name in names:
            cm = self._by_name[name].pop()
            self._ctx.remove(cm)
            cm.__exit__(None, None, None)

    def psum(self, name, shape, dtype=F32):
        return self._enter(self.nc.psum_tensor(name, shape, dtype))

    def mkqueues(self):
        self.q = {}
        for k, kind in (("pe", "pe"), ("act", "cmp"), ("dve", "cmp"),
                        ("pool", "cmp"), ("sd", "dma"), ("gd", "dma"),
                        ("ad", "dma")):
            sem = self._enter(self.nc.semaphore("sem_" + k))
            self.q[k] = Q(k, sem, kind)
        return self.q

    def emit(self):
        blk = self._enter(self.nc.Block())
        q = self.q

        @blk.tensor
        def _(e):
            q["pe"].replay(e)

        @blk.scalar
        def _(e):
            q["act"].replay(e)

        @blk.vector
        def _(e):
            q["dve"].replay(e)

        @blk.gpsimd
        def _(e):
            replay_merged(e, [q["pool"], q["gd"]])

        @blk.sync
        def _(e):
            replay_merged(e, [q["sd"], q["ad"]])

        for t in reversed(self._ctx):
            t.__exit__(None, None, None)
        self._ctx = []



def col_load(gd, dst, wT, oc, ncols=128, dep=None, *deps):
    src = wT[:, oc * ncols:(oc + 1) * ncols].rearrange("(fc p) o -> p fc o", p=128)
    return gd.dma(dst.rearrange("p (fc o) -> p fc o", o=ncols), src, dep, *deps)


def emit_stats(P, st, src_view, n_fc, dep=None, from_f32=False, entry=(),
               tiles=None):
    pe, act, dve, sd = P["pe"], P["act"], P["dve"], P["sd"]
    pool = P["pool"]
    ps, xs, sq, onc = st["ps"], st["xs"], st["sq"], st["onc"]
    last_mm = None
    consumed = dep
    for fc in range(n_fc):
        if tiles is not None:
            if from_f32:
                cv = P["gd"].dma(tiles[fc][:], src_view(fc),
                                 *(entry if fc == 0 else ()))
            else:
                qd = sd if fc % 2 == 0 else P["gd"]
                cv = qd.dma(tiles[fc][:], src_view(fc),
                            *(entry if fc < 2 else ()))
            xb = tiles[fc]
        elif from_f32:
            ld = sd.dma(st["xsf"][:], src_view(fc), consumed,
                        *(entry if fc == 0 else ()))
            cv = pool.op(lambda q: q.tensor_copy(xs[:], st["xsf"][:]), ld)
            xb = xs
        else:
            ld = sd.dma(xs[:], src_view(fc), consumed,
                        *(entry if fc == 0 else ()))
            xb, cv = xs, ld
        for qt in range(QT):
            sqe = dve.op(lambda q, qt=qt, xb=xb: q.tensor_tensor(
                sq[:], xb[:, qt * 512:(qt + 1) * 512],
                xb[:, qt * 512:(qt + 1) * 512], ALU.mult), cv, last_mm)
            pe.op(lambda q, qt=qt, fc=fc, xb=xb: q.matmul(
                ps[0:1, qt, :], onc[:], xb[:, qt * 512:(qt + 1) * 512],
                start=(fc == 0), stop=(fc == n_fc - 1)), cv, dep,
                *(entry if fc == 0 else ()))
            last_mm = pe.op(lambda q, qt=qt, fc=fc: q.matmul(
                ps[0:1, 4 + qt, :], onc[:], sq[:],
                start=(fc == 0), stop=(fc == n_fc - 1)), sqe)
        consumed = last_mm
    ev = None
    for qt in range(QT):
        act.op(lambda q, qt=qt: q.copy(st["ra"][:, qt * 512:(qt + 1) * 512],
                                       ps[0:1, qt, :]), last_mm)
        ev = act.op(lambda q, qt=qt: q.copy(st["rb"][:, qt * 512:(qt + 1) * 512],
                                            ps[0:1, 4 + qt, :]), last_mm)
    return ev


def emit_mrstd(P, st, stats_ev, nfeat=D, eps=LNEPS):
    act, dve = P["act"], P["dve"]
    ra, rb, rc = st["ra"], st["rb"], st["rc"]
    ev = dve.op(lambda q: q.tensor_scalar(ra[:], ra[:], 1.0 / nfeat, None, ALU.mult), stats_ev)
    ev = dve.op(lambda q: q.tensor_tensor(rc[:], ra[:], ra[:], ALU.mult), ev)
    ev = dve.op(lambda q: q.tensor_scalar(rb[:], rb[:], 1.0 / nfeat, None, ALU.mult), ev)
    ev = dve.op(lambda q: q.tensor_tensor(rb[:], rb[:], rc[:], ALU.subtract), ev)
    ev = dve.op(lambda q: q.tensor_scalar(rb[:], rb[:], float(eps), None, ALU.add), ev)
    ev = act.op(lambda q: q.activation(rb[:], rb[:], AF.Ln), ev)
    ev = act.op(lambda q: q.activation(rb[:], rb[:], AF.Exp, scale=-0.5), ev)
    return ev


def emit_bcast_row(P, st, src_ev, row_ap, dst, bank, nrows=128, war=None):
    pe, act, dve = P["pe"], P["act"], P["dve"]
    ps, onr = st["ps"], st["onr"]
    rowb = st["rowb"][0:1, :]
    cv = dve.op(lambda q: q.tensor_copy(rowb, row_ap), src_ev, war)
    ev = None
    for qt in range(QT):
        mm = pe.op(lambda q, qt=qt: q.matmul(
            ps[0:nrows, bank, :], onr[0:1, 0:nrows],
            rowb[:, qt * 512:(qt + 1) * 512], start=True, stop=True),
            cv, ev, war)
        ev = act.op(lambda q, qt=qt: q.copy(
            dst[0:nrows, qt * 512:(qt + 1) * 512], ps[0:nrows, bank, :]), mm)
    return ev


def emit_rowsum8(P, st, src_b, dst_row, bank, deps):
    pe, act = P["pe"], P["act"]
    ps, onc = st["ps"], st["onc"]
    ev = None
    for qt in range(QT):
        mm = pe.op(lambda q, qt=qt: q.matmul(
            ps[0:1, bank, :], onc[0:8, :], src_b[0:8, qt * 512:(qt + 1) * 512],
            start=True, stop=True), *( list(deps) + [ev]))
        ev = act.op(lambda q, qt=qt: q.copy(
            dst_row[0:1, qt * 512:(qt + 1) * 512], ps[0:1, bank, :]), mm)
    return ev


def act_recip(P, row_ap, *evs):
    act = P["act"]
    ev = act.op(lambda q: q.activation(row_ap, row_ap, AF.Ln), *evs)
    return act.op(lambda q: q.activation(row_ap, row_ap, AF.Exp, scale=-1.0), ev)


def emit_ln(P, st, x_dram, ln_tiles, mB, rB, dep, n_fc=FC, from_f32=False,
            resident=False):
    pool, sd, dve = P["pool"], P["sd"], P["dve"]
    ev = dep
    if resident:
        for fc in range(n_fc):
            e1 = dve.op(lambda q, fc=fc: q.tensor_tensor(
                ln_tiles[fc][:], ln_tiles[fc][:], mB[:], ALU.subtract), dep)
            ev = dve.op(lambda q, fc=fc: q.tensor_tensor(
                ln_tiles[fc][:], ln_tiles[fc][:], rB[:], ALU.mult), e1)
        return ev
    xb = [st.get("xsf"), st.get("xsf2", st.get("xsf"))] if from_f32 else None
    nb = 2 if (from_f32 and "xsf2" in st) else 1
    for fc in range(n_fc):
        if from_f32:
            ld = sd.dma(xb[fc % 2][:], x_dram[fc * 128:(fc + 1) * 128, :],
                        st.get(("lncv", fc - nb)),
                        dep if fc < nb else None)
            ld = pool.op(lambda q, fc=fc: q.tensor_copy(ln_tiles[fc][:], xb[fc % 2][:]), ld)
            st[("lncv", fc)] = ld
        else:
            ld = sd.dma(ln_tiles[fc][:], x_dram[fc * 128:(fc + 1) * 128, :])
        e1 = pool.op(lambda q, fc=fc: q.tensor_tensor(ln_tiles[fc][:], ln_tiles[fc][:], mB[:], ALU.subtract), ld, dep)
        ev = pool.op(lambda q, fc=fc: q.tensor_tensor(ln_tiles[fc][:], ln_tiles[fc][:], rB[:], ALU.mult), e1)
    return ev



FFN0_EXT = dict(x1=([D, S], BF16), wbt=([D, D], F32), bcols=([128, 32], F32),
                atp=([128, D], F32), gtp=([128, FC * 16], F32), bp=([128, D], F32),
                pat8=([8, 128], BF16), ones_col=([128, 1], BF16),
                ones_row=([1, 128], BF16), bthr=([1, 16], F32))


def emit_ffn0(lb, P, ext, ps, entry=()):
    x1, wbt, bcols, atp = ext["x1"], ext["wbt"], ext["bcols"], ext["atp"]
    gtp, bp, pat8 = ext["gtp"], ext["bp"], ext["pat8"]
    ones_col, ones_row, bthr = ext["ones_col"], ext["ones_row"], ext["bthr"]
    x2 = ext["x2"]

    ln = [lb.sbuf(f"ln{i}", [128, S], BF16) for i in range(FC)]
    xs = lb.sbuf("xs", [128, S], BF16)
    onc = lb.sbuf("onc", [128, 1], BF16)
    onr = lb.sbuf("onr", [1, 128], BF16)
    p8 = lb.sbuf("p8", [8, 128], BF16)
    bth = lb.sbuf("bth", [1, 16], F32)
    bcs = lb.sbuf("bcs", [128, 32], F32)
    hw_sb = lb.sbuf("hw_sb", [128, S], BF16)
    PH1 = ["sq", "ra", "rb", "rc", "rowbt", "mB", "rB", "acol", "gcol",
           "h_sb", "g_f", "g_b", "recB", "thrB", "wBq"]
    sq = lb.sbuf("sq", [128, 512], BF16)
    ra = lb.sbuf("ra", [1, S], F32)
    rb = lb.sbuf("rb", [1, S], F32)
    rc = lb.sbuf("rc", [1, S], F32)
    rowbt = lb.sbuf("rowbt", [1, S], BF16)
    mB = lb.sbuf("mB", [128, S], BF16)
    rB = lb.sbuf("rB", [128, S], BF16)
    acol = lb.sbuf("acol", [128, D], BF16)
    gcol = lb.sbuf("gcol", [128, FC * 16], BF16)
    h_sb = lb.sbuf("h_sb", [128, S], BF16)
    g_f = lb.sbuf("g_f", [8, S], F32)
    g_b = lb.sbuf("g_b", [8, S], BF16)
    recB = lb.sbuf("recB", [8, S], BF16)
    thrB = lb.sbuf("thrB", [8, S], BF16)
    wBq = lb.sbuf("wBq", [128, 512], BF16)
    pe, act, dve = P["pe"], P["act"], P["dve"]
    sd, gd, ad = P["sd"], P["gd"], P["ad"]

    st = dict(ps=ps, xs=xs, sq=sq, onc=onc, onr=onr, rowb=rowbt,
              ra=ra, rb=rb, rc=rc)

    c1 = sd.dma(onc[:], ones_col[:, :], *entry)
    c2 = sd.dma(onr[:], ones_row[:, :])
    c3 = sd.dma(p8[:], pat8[:, :])
    c4 = sd.dma(bth[:], bthr[:, :])
    c5 = sd.dma(bcs[:], bcols[:, :])
    a_ld = gd.dma(acol[:], atp[:, :], *entry)
    g_ld = gd.dma(gcol[:], gtp[:, :])

    sev = emit_stats(P, st, lambda fc: x1[fc * 128:(fc + 1) * 128, :], FC,
                     dep=c5, entry=entry, tiles=ln)
    mev = emit_mrstd(P, st, sev)
    bm = emit_bcast_row(P, st, mev, ra[:], mB, 2)
    br = emit_bcast_row(P, st, mev, rb[:], rB, 3, war=bm)
    lev = emit_ln(P, st, x1, ln, mB, rB, br, resident=True)

    mm_h = None
    for fc in range(FC):
        for qt in range(QT):
            mm_h = pe.op(lambda q, fc=fc, qt=qt: q.matmul(
                ps[:, 4 + qt, :], acol[:, fc * 128:(fc + 1) * 128],
                ln[fc][:, qt * 512:(qt + 1) * 512],
                start=(fc == 0), stop=(fc == FC - 1)),
                *([a_ld, lev] if fc == 0 and qt == 0 else []))
    eh = None
    for qt in range(QT):
        eh = act.op(lambda q, qt=qt: q.copy(h_sb[:, qt * 512:(qt + 1) * 512],
                                            ps[:, 4 + qt, :]), mm_h)
    mm_g = None
    for fc in range(FC):
        for qt in range(QT):
            mm_g = pe.op(lambda q, fc=fc, qt=qt: q.matmul(
                ps[0:8, qt, :], gcol[:, fc * 16:fc * 16 + 8],
                ln[fc][:, qt * 512:(qt + 1) * 512],
                start=(fc == 0), stop=(fc == FC - 1)),
                *([g_ld, lev, bm] if fc == 0 and qt == 0 else []))
    eg = None
    for qt in range(QT):
        eg = act.op(lambda q, qt=qt: q.copy(g_f[:, qt * 512:(qt + 1) * 512],
                                            ps[0:8, qt, :]), mm_g)
    mm_t = None
    for fc in range(FC):
        for qt in range(QT):
            mm_t = pe.op(lambda q, fc=fc, qt=qt: q.matmul(
                ps[0:1, qt, :], gcol[:, fc * 16 + 8:fc * 16 + 9],
                ln[fc][:, qt * 512:(qt + 1) * 512],
                start=(fc == 0), stop=(fc == FC - 1)),
                *([eg] if fc == 0 and qt == 0 else []))
    etv = None
    for qt in range(QT):
        etv = act.op(lambda q, qt=qt: q.copy(ra[0:1, qt * 512:(qt + 1) * 512],
                                             ps[0:1, qt, :]), mm_t, bm)

    pool = P["pool"]
    e1 = act.op(lambda q: q.activation(g_f[0:8, :], g_f[0:8, :], AF.Exp), eg)
    e2 = act.op(lambda q: q.activation(ra[:], ra[:], AF.Identity, bias=bth[0:1, 0:1]), etv, c4)
    e3 = act.op(lambda q: q.activation(ra[:], ra[:], AF.Exp, scale=-1.0), e2)
    e4 = dve.op(lambda q: q.tensor_scalar(ra[:], ra[:], 1.0, None, ALU.add), e3)
    e5 = act_recip(P, ra[:], e4)
    e6 = dve.op(lambda q: q.tensor_scalar(ra[:], ra[:], 0.125, None, ALU.mult), e5)
    e7 = dve.op(lambda q: q.tensor_copy(g_b[0:8, :], g_f[0:8, :]), e1)
    e8 = emit_rowsum8(P, st, g_b, rb[:], 4, [e7, eh, br])
    e9 = act_recip(P, rb[:], e8)
    eB = emit_bcast_row(P, st, e9, rb[:], recB, 5, nrows=8, war=eh)
    eD = emit_bcast_row(P, st, e6, ra[:], thrB, 5, nrows=8, war=eB)
    eE = dve.op(lambda q: q.tensor_tensor(g_f[0:8, :], g_f[0:8, :], recB[0:8, :], ALU.mult), eB, e1)
    eF = dve.op(lambda q: q.tensor_tensor(g_f[0:8, :], g_f[0:8, :], thrB[0:8, :], ALU.subtract), eD, eE)
    eG = dve.op(lambda q: q.tensor_scalar(g_f[0:8, :], g_f[0:8, :], 0.0, None, ALU.max), eF)
    eH = dve.op(lambda q: q.tensor_copy(g_b[0:8, :], g_f[0:8, :]), eG)
    eI = emit_rowsum8(P, st, g_b, rb[:], 4, [eH, e8])
    eJ = dve.op(lambda q: q.tensor_scalar(rc[:], rb[:], 0.0, None, ALU.is_equal), eI, eD)
    eK = dve.op(lambda q: q.tensor_tensor(rb[:], rb[:], rc[:], ALU.add), eJ)
    eL = act_recip(P, rb[:], eK)
    eM = emit_bcast_row(P, st, eL, rb[:], recB, 5, nrows=8, war=eD)
    eO = dve.op(lambda q: q.tensor_tensor(g_f[0:8, :], g_f[0:8, :], recB[0:8, :], ALU.mult), eM, eG)
    eP = dve.op(lambda q: q.tensor_copy(g_b[0:8, :], g_f[0:8, :]), eO)
    hw_ev = None
    eQ = None
    for qt in range(QT):
        mm5 = pe.op(lambda q, qt=qt: q.matmul(
            ps[:, 6, :], p8[0:8, :], g_b[0:8, qt * 512:(qt + 1) * 512],
            start=True, stop=True), eP, c3, hw_ev, eh)
        eQ = act.op(lambda q, qt=qt: q.copy(wBq[:], ps[:, 6, :]), mm5)
        hw_ev = dve.op(lambda q, qt=qt: q.tensor_tensor(
            hw_sb[:, qt * 512:(qt + 1) * 512], h_sb[:, qt * 512:(qt + 1) * 512],
            wBq[:], ALU.mult), eQ, eh)

    dbg_ev = None

    lb.free(*reversed(PH1))
    wcol = [lb.sbuf(f"wcol{i}", [128, D], BF16) for i in range(2)]
    bps = lb.sbuf("bps", [128, D], BF16)
    x1r = [lb.sbuf(f"x1r{i}", [128, S], BF16) for i in range(2)]
    x2t = [lb.sbuf(f"x2t{i}", [128, S], BF16) for i in range(2)]
    ev32 = [lb.sbuf(f"ev32_{i}", [128, 512], F32) for i in range(4)]
    b_ld = gd.dma(bps[:], bp[:, :], hw_ev, dbg_ev)
    wld = [None, None]
    x1ld = [None, None]
    mm_done = {}
    evac_done = {}
    out_done = {}
    wld[0] = col_load(gd, wcol[0], wbt, 0, dep=b_ld)
    x1ld[0] = sd.dma(x1r[0][:], x1[0:128, :], lev, hw_ev)
    for oc in range(FC):
        pb = oc % 2
        half = pb * 4
        if oc + 1 < FC:
            nb = (oc + 1) % 2
            wld[nb] = col_load(gd, wcol[nb], wbt, oc + 1, dep=mm_done.get(oc - 1))
            x1ld[nb] = sd.dma(x1r[nb][:], x1[(oc + 1) * 128:(oc + 2) * 128, :],
                              evac_done.get(oc - 1))
        deps = [wld[pb], lev, evac_done.get(oc - 2)]
        if oc == 0:
            deps.append(etv)
        if oc == 1:
            deps.append(hw_ev)
        mm = None
        for fc in range(FC):
            for qt in range(QT):
                mm = pe.op(lambda q, fc=fc, qt=qt, pb=pb, half=half: q.matmul(
                    ps[:, half + qt, :], wcol[pb][:, fc * 128:(fc + 1) * 128],
                    ln[fc][:, qt * 512:(qt + 1) * 512],
                    start=(fc == 0), stop=False),
                    *(deps if fc == 0 and qt == 0 else []))
        for qt in range(QT):
            mm = pe.op(lambda q, qt=qt, oc=oc, half=half: q.matmul(
                ps[:, half + qt, :], bps[:, oc * 128:(oc + 1) * 128],
                hw_sb[:, qt * 512:(qt + 1) * 512],
                start=False, stop=True), *([hw_ev] if qt == 0 else []))
        mm_done[oc] = mm
        ee = None
        for qt in range(QT):
            eb = qt
            ea = act.op(lambda q, qt=qt, half=half, eb=eb, oc=oc: q.activation(
                ev32[eb][:, :], ps[:, half + qt, :], AF.Identity,
                bias=bcs[:, oc:oc + 1]),
                mm, c5, evac_done.get(oc - 1))
            ee = dve.op(lambda q, qt=qt, oc=oc, pb=pb, eb=eb: q.tensor_tensor(
                x2t[pb][:, qt * 512:(qt + 1) * 512], ev32[eb][:, :],
                x1r[pb][:, qt * 512:(qt + 1) * 512], ALU.add),
                ea, x1ld[pb], out_done.get(oc - 2))
        evac_done[oc] = ee
        out_done[oc] = ad.dma(x2[oc * 128:(oc + 1) * 128, :], x2t[pb][:], ee)

    lb.free("ev32_3", "ev32_2", "ev32_1", "ev32_0", "x2t1", "x2t0", "x1r1",
            "x1r0", "bps", "wcol1", "wcol0", "hw_sb", "bcs", "bth", "p8",
            "onr", "onc", "xs")
    for i in reversed(range(FC)):
        lb.free(f"ln{i}")
    return out_done[FC - 1]


def build_ffn0():
    lb = LB("ffn0")
    nc = lb.nc
    ext = {k: nc.dram_tensor(k, sh, dt, kind="ExternalInput").ap()
           for k, (sh, dt) in FFN0_EXT.items()}
    ext["x2"] = nc.dram_tensor("x2", [D, S], BF16, kind="ExternalOutput").ap()
    ps = lb.psum("ps", [128, 8, 512], F32)
    P = lb.mkqueues()
    done = emit_ffn0(lb, P, ext, ps)
    P["pe"].barrier(done)
    lb.emit()
    return lb



def _prechunk_T(wT):
    Dd, O = wT.shape
    return np.ascontiguousarray(
        wT.reshape(Dd // 128, 128, O).transpose(1, 0, 2).reshape(128, (Dd // 128) * O))


def _pat8():
    p = np.zeros((8, 128), BF)
    for e in range(8):
        p[e, e * 16:(e + 1) * 16] = 2.0
    return p


def prep_ffn0_inputs(inp, x1_fm_bf16):
    d = dict(
        wbt=np.ascontiguousarray(inp["L0_ffn_W_base"].astype(np.float32).T),
        bcols=np.ascontiguousarray(inp["L0_ffn_b_base"].astype(np.float32).reshape(32, 128).T),
        atp=_prechunk_T(inp["L0_ffn_A_w"].astype(np.float32).reshape(128, D).T),
        gtp=_prechunk_T(np.ascontiguousarray(np.concatenate(
            [inp["L0_ffn_W_gate"].astype(np.float32),
             inp["L0_ffn_W_thr"].astype(np.float32)[None, :],
             np.zeros((7, D), np.float32)], axis=0).T)),
        bp=np.ascontiguousarray(
            inp["L0_ffn_B_w"].astype(np.float32).transpose(0, 2, 1).reshape(128, D)),
        pat8=_pat8(),
        ones_col=np.ones((128, 1), BF),
        ones_row=np.ones((1, 128), BF),
        bthr=np.concatenate([inp["L0_ffn_b_thr"].astype(np.float32),
                             np.zeros(15, np.float32)]).reshape(1, 16),
    )
    if x1_fm_bf16 is not None:
        d["x1"] = np.ascontiguousarray(x1_fm_bf16)
    return d


_BUILT = {}


def _get(name, fn):
    if name not in _BUILT:
        _BUILT[name] = fn()
    return _BUILT[name]


def build_mega():
    lb = LB("mega")
    nc = lb.nc
    ext = {}
    for table in (ATTN0_EXT, FFN0_EXT, ATTN1_EXT, FFN1_EXT, DEQ_EXT):
        for k, (sh, dt) in table.items():
            if k not in ext and k not in ("x1", "x2", "x3"):
                ext[k] = nc.dram_tensor(k, sh, dt, kind="ExternalInput").ap()
    ext["out"] = nc.dram_tensor("out", [D, S], F32, kind="ExternalOutput").ap()
    ext["qkv_s"] = nc.dram_tensor("qkv_s", [3 * D, S], BF16)
    ext["sigq_s"] = nc.dram_tensor("sigq_s", [1024, S], BF16).ap()
    ext["x1"] = nc.dram_tensor("x1_s", [D, S], BF16).ap()
    ext["x2"] = nc.dram_tensor("x2_s", [D, S], BF16).ap()
    ext["x3"] = nc.dram_tensor("x3_s", [D, S], BF16).ap()
    ext["wlT"] = nc.dram_tensor("wlT", [D, 512], BF16).ap()
    ext["whT"] = nc.dram_tensor("whT", [D, 512], BF16).ap()
    ext["wl_o"] = nc.dram_tensor("wl_o", [512, D], BF16).ap()
    ext["wh_o"] = nc.dram_tensor("wh_o", [512, D], BF16).ap()
    ext["gl"] = nc.dram_tensor("gl", [NCORES * D, 512], BF16, addr_space="Shared").ap()
    ext["gh"] = nc.dram_tensor("gh", [NCORES * D, 512], BF16, addr_space="Shared").ap()
    ps = lb.psum("ps", [128, 8, 512], F32)
    P = lb.mkqueues()

    def entry(*extra):
        evs = tuple(e for e in (P[k].ev() for k in ("pe", "act", "dve", "pool"))
                    if e is not None)
        return evs + tuple(e for e in extra if e is not None)

    hook = lambda dep, bufs: emit_dequant_gen(lb, P, ext, bufs, entry=(dep,))
    d1 = emit_attn0(lb, P, ext, ps, entry=entry(), mid_hook=hook)
    d2 = emit_ffn0(lb, P, ext, ps, entry=entry(d1))
    d3 = emit_attn1(lb, P, ext, ps, entry=entry(d2))
    e1, e2 = emit_dequant_tr(lb, P, ext, ps, ext["deq_ev"], entry=entry(d3))
    ent4 = entry(d3, e1, e2)
    cc1 = P["pool"].op(lambda q: q.collective_compute(
        "AllGather", ALU.bypass, replica_groups=[list(range(NCORES))],
        ins=[ext["wlT"][:, :]], outs=[ext["gl"][:, :]]), e1)
    cc2 = P["pool"].op(lambda q: q.collective_compute(
        "AllGather", ALU.bypass, replica_groups=[list(range(NCORES))],
        ins=[ext["whT"][:, :]], outs=[ext["gh"][:, :]]), e2)
    ext["cc_ev"] = (cc1, cc2)
    d4 = emit_ffn1(lb, P, ext, ps, entry=ent4)
    P["pe"].barrier(d4)
    lb.emit()
    return lb


def prep_mega_inputs(inputs, core):
    d = {}
    d.update(prep_dequant_inputs(inputs, core))
    d.update(prep_attn0_inputs(inputs, core))
    x = inputs["L0_attn_x"][core]
    d.update(prep_ffn0_inputs(inputs, None))
    d.update(prep_attn1_inputs(inputs, None))
    d.update(prep_ffn1_inputs(inputs, None, None, None))
    for k in ("x1", "x2", "x3", "gl", "gh"):
        d.pop(k, None)
    return d


def kernel(**inputs):
    cores = list(range(NCORES))
    lbM = _get("mega", build_mega)
    res = run_bass_kernel_spmd(
        lbM.nc, [prep_mega_inputs(inputs, c) for c in cores], core_ids=cores)
    out = np.stack([res.results[c]["out"].T for c in cores], axis=0)
    return np.ascontiguousarray(out)



def emit_eqsum(q, acc, ind, c_b, lut, t0, t1, deps):
    ev = q.op(lambda qq, t=t0: qq.tensor_scalar(
        acc[:], c_b[:], float(t), lut[:, t:t + 1], ALU.is_equal, ALU.mult),
        *deps)
    for t in range(t0 + 1, t1):
        q.op(lambda qq, t=t: qq.tensor_scalar(
            ind[:], c_b[:], float(t), lut[:, t:t + 1], ALU.is_equal, ALU.mult), ev)
        ev = q.op(lambda qq, t=t: qq.tensor_tensor(acc[:], acc[:], ind[:], ALU.add))
    return ev


DEQ_EXT = dict(qwp=([512, 1024], I32), maskc=([128, 32], I32),
               lut4c=([512, 16], F32), lut8c=([512, 256], F32),
               ident=([128, 128], BF16))

DVE_HI = 160


def alloc_dequant(lb):
    DQF = ["qwq", "tmask", "tti", "c8b", "indd", "accl", "acch", "l4", "l8"]
    b = dict(
        qw=lb.sbuf("qwq", [128, 1024], I32),
        tmask=lb.sbuf("tmask", [128, 32], I32),
        tti=lb.sbuf("tti", [128, 1024], I32),
        c8b=lb.sbuf("c8b", [128, 2048], BF16),
        indd=lb.sbuf("indd", [128, 2048], BF16),
        accl=lb.sbuf("accl", [128, 2048], BF16),
        acch=lb.sbuf("acch", [128, 2048], BF16),
        l4=lb.sbuf("l4", [128, 16], F32),
        l8=lb.sbuf("l8", [128, 256], F32),
    )
    b["names"] = DQF
    return b


def emit_dequant_gen(lb, P, ext, bufs, entry=()):
    qwp, maskc = ext["qwp"], ext["maskc"]
    lut4c, lut8c = ext["lut4c"], ext["lut8c"]
    wl_o, wh_o = ext["wl_o"], ext["wh_o"]

    DQF = bufs["names"]
    qw, tmask, tti = bufs["qw"], bufs["tmask"], bufs["tti"]
    c8b, indd = bufs["c8b"], bufs["indd"]
    accl, acch, l4, l8 = bufs["accl"], bufs["acch"], bufs["l4"], bufs["l8"]
    tb = indd[:, 0:1024]

    dve, pool, gd = P["dve"], P["pool"], P["gd"]
    cm = gd.dma(tmask[:], maskc[:, :], *entry)
    ld = gd.dma(qw[:], qwp[0:128, :])
    ll4 = gd.dma(l4[:], lut4c[0:128, :])
    ll8 = gd.dma(l8[:], lut8c[0:128, :])
    yield

    mexp = tmask[:].unsqueeze(1).broadcast_to([128, 32, 32])
    tti3 = tti[:].rearrange("p (w s) -> p w s", s=32)
    prev = {}
    for ot in range(4):
        for hf in range(2):
            def wexp(pl, hf=hf, sub=0):
                base = pl * 128 + hf * 64 + sub * 32
                return (qw[:, base:base + 32]
                        .unsqueeze(2).broadcast_to([128, 32, 32]))

            e = None
            for sub in range(2):
                csl = slice(sub * 1024, (sub + 1) * 1024)
                e = dve.op(lambda q, wexp=wexp, sub=sub: q.tensor_tensor(
                    tti3, wexp(0, sub=sub), mexp, ALU.bitwise_and), ld, cm)
                e = dve.op(lambda q, csl=csl: q.tensor_scalar(
                    c8b[:, csl], tti[:], 0, None, ALU.not_equal))
                for pl in range(1, 4):
                    e = dve.op(lambda q, pl=pl, wexp=wexp, sub=sub: q.tensor_tensor(
                        tti3, wexp(pl, sub=sub), mexp, ALU.bitwise_and))
                    e = dve.op(lambda q: q.tensor_scalar(
                        tb, tti[:], 0, None, ALU.not_equal))
                    e = dve.op(lambda q, pl=pl, csl=csl: q.scalar_tensor_tensor(
                        c8b[:, csl], tb, float(1 << pl), c8b[:, csl],
                        ALU.mult, ALU.add))
            evl = emit_eqsum(dve, accl, indd, c8b, l4, 0, 16,
                             [e, ll4, prev.get("sl")])
            e = evl
            for sub in range(2):
                csl = slice(sub * 1024, (sub + 1) * 1024)
                for pl in range(4, 8):
                    e = dve.op(lambda q, pl=pl, wexp=wexp, sub=sub: q.tensor_tensor(
                        tti3, wexp(pl, sub=sub), mexp, ALU.bitwise_and))
                    e = dve.op(lambda q: q.tensor_scalar(
                        tb, tti[:], 0, None, ALU.not_equal))
                    e = dve.op(lambda q, pl=pl, csl=csl: q.scalar_tensor_tensor(
                        c8b[:, csl], tb, float(1 << pl), c8b[:, csl],
                        ALU.mult, ALU.add))
            prev["extr"] = e
            prev["sl"] = gd.dma(wl_o[ot * 128:(ot + 1) * 128,
                                     hf * 2048:(hf + 1) * 2048], accl[:], evl)
            yield
            evh = emit_eqsum(dve, acch, indd, c8b, l8, 0, 256,
                             [e, ll8, prev.get("sh")])
            prev["sh"] = gd.dma(wh_o[ot * 128:(ot + 1) * 128,
                                     hf * 2048:(hf + 1) * 2048], acch[:], evh)
            if hf == 1 and ot + 1 < 4:
                ld = gd.dma(qw[:], qwp[(ot + 1) * 128:(ot + 2) * 128, :],
                            prev["extr"])
                ll4 = gd.dma(l4[:], lut4c[(ot + 1) * 128:(ot + 2) * 128, :],
                             evl)
                ll8 = gd.dma(l8[:], lut8c[(ot + 1) * 128:(ot + 2) * 128, :],
                             evh)
            yield
    ext["deq_ev"] = (prev["sl"], prev["sh"])
    ext["deq_done"] = (prev["sh"], prev["sl"])
    ext["deq_names"] = DQF


def emit_dequant_tr(lb, P, ext, ps, deps, entry=()):
    pe, act, ad, sd = P["pe"], P["act"], P["ad"], P["sd"]
    wl_o, wh_o = ext["wl_o"], ext["wh_o"]
    wlT, whT = ext["wlT"], ext["whT"]
    idt2 = lb.sbuf("idt2", [128, 128], BF16)
    acc2 = [lb.sbuf(f"acc2_{i}", [128, D], BF16) for i in range(2)]
    slab = lb.sbuf("slab", [128, FC * 512], BF16)
    pst = [ps[:, 2 * i, :].bitcast(BF16) for i in range(4)]
    ci = sd.dma(idt2[:], ext["ident"][:, :], *(deps + tuple(entry)))
    evs = []
    tr_ev = {}
    for mi, (src_d, dst_d) in enumerate(((wl_o, wlT), (wh_o, whT))):
        for ot in range(4):
            lda = sd.dma(acc2[ot % 2][:], src_d[ot * 128:(ot + 1) * 128, :],
                         tr_ev.get((mi, ot - 2, FC - 1)))
            for ic in range(FC):
                b = 2 * (ot % 2) + ic % 2
                mmt = pe.op(lambda q, ic=ic, b=b, ot=ot: q.transpose(
                    pst[b][:, 0:128], acc2[ot % 2][:, ic * 128:(ic + 1) * 128],
                    idt2[:]), lda, ci, tr_ev.get((mi, ot, ic - 2)),
                    tr_ev.get((mi - 1, ot, ic)) if False else None,
                    *(entry if mi == 0 and ot == 0 and ic < 2 else ()))
                tv = act.op(lambda q, ic=ic, ot=ot, b=b: q.copy(
                    slab[:, ic * 512 + ot * 128: ic * 512 + (ot + 1) * 128],
                    pst[b][:, 0:128]), mmt, evs[-1] if (ot == 0 and ic == 0 and evs) else None)
                tr_ev[(mi, ot, ic)] = tv
        evs.append(ad.dma(dst_d.rearrange("(ic p) o -> p ic o", p=128),
                          slab.rearrange("p (ic o) -> p ic o", o=512),
                          tr_ev[(mi, 3, FC - 1)]))
    lb.free("slab", "acc2_1", "acc2_0", "idt2")
    return tuple(evs)


def build_dequant():
    lb = LB("dequant")
    nc = lb.nc
    ext = {k: nc.dram_tensor(k, sh, dt, kind="ExternalInput").ap()
           for k, (sh, dt) in DEQ_EXT.items()}
    ext["wlT"] = nc.dram_tensor("wlT", [D, 512], BF16, kind="ExternalOutput").ap()
    ext["whT"] = nc.dram_tensor("whT", [D, 512], BF16, kind="ExternalOutput").ap()
    ext["wl_o"] = nc.dram_tensor("wl_o", [512, D], BF16).ap()
    ext["wh_o"] = nc.dram_tensor("wh_o", [512, D], BF16).ap()
    ps = lb.psum("ps", [128, 8, 512], F32)
    P = lb.mkqueues()
    e1, e2, _ = emit_dequant(lb, P, ext, ps)
    e3, e4 = emit_dequant_tr(lb, P, ext, ps, (e1, e2))
    P["pe"].barrier(e3, e4)
    lb.emit()
    return lb


def prep_dequant_inputs(inp, core):
    qw = inp["L1_ffn_qweight"]
    rows = slice(512 * core, 512 * (core + 1))
    return dict(
        qwp=np.ascontiguousarray(qw[:, rows, :].transpose(1, 0, 2).reshape(512, 1024)),
        maskc=np.broadcast_to((np.int32(1) << np.arange(32, dtype=np.int32))[None, :], (128, 32)).copy(),
        lut4c=np.ascontiguousarray(inp["L1_ffn_lut4"][rows].astype(np.float32)),
        lut8c=np.ascontiguousarray(inp["L1_ffn_lut8"][rows].astype(np.float32)),
        ident=np.eye(128, dtype=BF),
    )



FFN1_EXT = dict(x3=([D, S], BF16), bcols1=([128, 32], F32),
                ones_col=([128, 1], BF16), ones_row=([1, 128], BF16))


def emit_ffn1(lb, P, ext, ps, entry=()):
    x3, gl, gh = ext["x3"], ext["gl"], ext["gh"]
    bcols = ext["bcols1"]
    ones_col, ones_row = ext["ones_col"], ext["ones_row"]
    out = ext["out"]

    ln = [lb.sbuf(f"ln{i}", [128, S], BF16) for i in range(FC)]
    onc = lb.sbuf("onc", [128, 1], BF16)
    onr = lb.sbuf("onr", [1, 128], BF16)
    bcs = lb.sbuf("bcs", [128, 32], F32)
    mB = lb.sbuf("mB", [128, S], BF16)
    rB = lb.sbuf("rB", [128, S], BF16)
    maskB = lb.sbuf("maskB", [128, S], BF16)
    pe, act, dve = P["pe"], P["act"], P["dve"]
    pool = P["pool"]
    sd, gd, ad = P["sd"], P["gd"], P["ad"]

    PH1 = ["xs", "sq", "ra", "rb", "rc", "rd", "rowbt"]
    xs = lb.sbuf("xs", [128, S], BF16)
    sq = lb.sbuf("sq", [128, 512], BF16)
    ra = lb.sbuf("ra", [1, S], F32)
    rb = lb.sbuf("rb", [1, S], F32)
    rc = lb.sbuf("rc", [1, S], F32)
    rd = lb.sbuf("rd", [1, S], F32)
    rowbt = lb.sbuf("rowbt", [1, S], BF16)

    st = dict(ps=ps, xs=xs, sq=sq, onc=onc, onr=onr, rowb=rowbt,
              ra=ra, rb=rb, rc=rc)

    c1 = sd.dma(onc[:], ones_col[:, :], *entry)
    c2 = sd.dma(onr[:], ones_row[:, :])
    c5 = sd.dma(bcs[:], bcols[:, :])

    sev = emit_stats(P, st, lambda fc: x3[fc * 128:(fc + 1) * 128, :], FC,
                     dep=c5, entry=entry, tiles=ln)
    me = dve.op(lambda q: q.tensor_tensor(rc[:], ra[:], ra[:], ALU.mult), sev)
    me = dve.op(lambda q: q.tensor_scalar(rc[:], rc[:], -1.0 / D, None, ALU.mult), me)
    me = dve.op(lambda q: q.tensor_tensor(rc[:], rc[:], rb[:], ALU.add), me)
    me = dve.op(lambda q: q.tensor_scalar(rd[:], rc[:], 1.0 / D, None, ALU.mult), me)
    me = dve.op(lambda q: q.tensor_scalar(rd[:], rd[:], float(LNEPS), None, ALU.add), me)
    me = act_recip(P, rd[:], me)
    me = dve.op(lambda q: q.tensor_tensor(rd[:], rc[:], rd[:], ALU.mult), me)
    me = dve.op(lambda q: q.tensor_scalar(rd[:], rd[:], float(D), None, ALU.is_gt), me)
    mev = emit_mrstd(P, st, me)
    bm = emit_bcast_row(P, st, mev, ra[:], mB, 2)
    br = emit_bcast_row(P, st, mev, rb[:], rB, 3, war=bm)
    bk = emit_bcast_row(P, st, me, rd[:], maskB, 4, war=br)
    lev = emit_ln(P, st, x3, ln, mB, rB, br, resident=True)

    lb.free(*reversed(PH1))
    wcol = [lb.sbuf(f"wcol{i}", [128, D], BF16) for i in range(2)]
    hcol = [lb.sbuf(f"hcol{i}", [128, D], BF16) for i in range(2)]
    x3r = [lb.sbuf(f"x3r{i}", [128, S], BF16) for i in range(2)]
    o32 = [lb.sbuf(f"o32_{i}", [128, S], F32) for i in range(2)]
    ev32 = [lb.sbuf(f"ev32_{i}", [128, 512], F32) for i in range(4)]

    def gcol_load(dst, g, oc, dep, dep2=None):
        c = oc // 4
        off = (oc % 4) * 128
        src = g[c * D:(c + 1) * D, off:off + 128].rearrange("(fc p) o -> p fc o", p=128)
        return sd.dma(dst.rearrange("p (fc o) -> p fc o", o=128), src, dep, dep2)

    wld = [None, None]
    hld = [None, None]
    x3ld = [None, None]
    mm_done = {}
    evac_done = {}
    pdone = {}
    out_done = {}
    if ext.get("cc_ev") is not None:
        for _cce in ext["cc_ev"]:
            sd.op(lambda q: q.nop(), _cce)
    wld[0] = gcol_load(wcol[0], gl, 0, bk, lev)
    hld[0] = gcol_load(hcol[0], gh, 0, None)
    x3ld[0] = sd.dma(x3r[0][:], x3[0:128, :], lev)
    for oc in range(FC):
        pb = oc % 2
        if oc + 1 < FC:
            nb = (oc + 1) % 2
            wld[nb] = gcol_load(wcol[nb], gl, oc + 1, mm_done.get(oc - 1))
            hld[nb] = gcol_load(hcol[nb], gh, oc + 1, None)
            x3ld[nb] = sd.dma(x3r[nb][:], x3[(oc + 1) * 128:(oc + 2) * 128, :],
                              evac_done.get(oc - 1))
        deps = [wld[pb], hld[pb], lev, pdone.get(oc - 1), bk]
        mm = None
        for fc in range(FC):
            for qt in range(QT):
                mm = pe.op(lambda q, fc=fc, qt=qt, pb=pb: q.matmul(
                    ps[:, qt, :], wcol[pb][:, fc * 128:(fc + 1) * 128],
                    ln[fc][:, qt * 512:(qt + 1) * 512],
                    start=(fc == 0), stop=(fc == FC - 1)),
                    *(deps if fc == 0 and qt == 0 else []))
        for fc in range(FC):
            for qt in range(QT):
                mm = pe.op(lambda q, fc=fc, qt=qt, pb=pb: q.matmul(
                    ps[:, 4 + qt, :], hcol[pb][:, fc * 128:(fc + 1) * 128],
                    ln[fc][:, qt * 512:(qt + 1) * 512],
                    start=(fc == 0), stop=(fc == FC - 1)))
        mm_done[oc] = mm
        ee = None
        for qt in range(QT):
            ea = act.op(lambda q, qt=qt, oc=oc: q.activation(
                ev32[qt][:, :], ps[:, qt, :], AF.Identity,
                bias=bcs[:, oc:oc + 1]), mm, c5, evac_done.get(oc - 1))
            eh2 = act.op(lambda q, qt=qt, oc=oc, pb=pb: q.activation(
                o32[pb][:, qt * 512:(qt + 1) * 512], ps[:, 4 + qt, :],
                AF.Identity, bias=bcs[:, oc:oc + 1]),
                out_done.get(oc - 2))
            e1 = dve.op(lambda q, qt=qt, pb=pb: q.tensor_tensor(
                o32[pb][:, qt * 512:(qt + 1) * 512],
                o32[pb][:, qt * 512:(qt + 1) * 512], ev32[qt][:, :],
                ALU.subtract), ea, eh2)
            e2 = dve.op(lambda q, qt=qt, pb=pb: q.tensor_tensor(
                o32[pb][:, qt * 512:(qt + 1) * 512],
                o32[pb][:, qt * 512:(qt + 1) * 512],
                maskB[:, qt * 512:(qt + 1) * 512], ALU.mult), bk)
            e3 = dve.op(lambda q, qt=qt, pb=pb: q.tensor_tensor(
                o32[pb][:, qt * 512:(qt + 1) * 512],
                o32[pb][:, qt * 512:(qt + 1) * 512], ev32[qt][:, :], ALU.add))
            ee = dve.op(lambda q, qt=qt, pb=pb: q.tensor_tensor(
                o32[pb][:, qt * 512:(qt + 1) * 512],
                o32[pb][:, qt * 512:(qt + 1) * 512],
                x3r[pb][:, qt * 512:(qt + 1) * 512], ALU.add), x3ld[pb])
            pdone[oc] = eh2
        evac_done[oc] = ee
        out_done[oc] = ad.dma(out[oc * 128:(oc + 1) * 128, :], o32[pb][:], ee)

    return out_done[FC - 1]


def build_ffn1():
    lb = LB("ffn1")
    nc = lb.nc
    ext = {k: nc.dram_tensor(k, sh, dt, kind="ExternalInput").ap()
           for k, (sh, dt) in FFN1_EXT.items()}
    ext["gl"] = nc.dram_tensor("gl", [NCORES * D, 512], BF16, kind="ExternalInput").ap()
    ext["gh"] = nc.dram_tensor("gh", [NCORES * D, 512], BF16, kind="ExternalInput").ap()
    ext["out"] = nc.dram_tensor("out", [D, S], F32, kind="ExternalOutput").ap()
    ps = lb.psum("ps", [128, 8, 512], F32)
    P = lb.mkqueues()
    done = emit_ffn1(lb, P, ext, ps)
    P["pe"].barrier(done)
    lb.emit()
    return lb


def prep_ffn1_inputs(inp, x3_fm_bf16, gl, gh):
    d = dict(
        bcols1=np.ascontiguousarray(inp["L1_ffn_bias"].astype(np.float32).reshape(32, 128).T),
        ones_col=np.ones((128, 1), BF),
        ones_row=np.ones((1, 128), BF),
    )
    if x3_fm_bf16 is not None:
        d["x3"] = np.ascontiguousarray(x3_fm_bf16)
        d["gl"] = gl
        d["gh"] = gh
    return d



ATTN1_EXT = dict(x2=([D, S], BF16), wqt=([D, 1024], F32), bqc=([128, 8], F32),
                 mrows=([1024, 128], F32), zc=([128, 8], F32), wot=([D, D], F32),
                 ones_col=([128, 1], BF16), ones_row=([1, 128], BF16))


def emit_attn1(lb, P, ext, ps, entry=()):
    pool = None
    x2, wqt, bqc = ext["x2"], ext["wqt"], ext["bqc"]
    mrows, zc, wot = ext["mrows"], ext["zc"], ext["wot"]
    ones_col, ones_row = ext["ones_col"], ext["ones_row"]
    x3, sigq_s = ext["x3"], ext["sigq_s"]

    mcols = lb.sbuf("mcols", [128, 8 * 128], BF16)
    zcs = lb.sbuf("zcs", [128, 8], BF16)
    bqs = lb.sbuf("bqs", [128, 8], F32)
    onc = lb.sbuf("onc", [128, 1], BF16)
    onr = lb.sbuf("onr", [1, 128], BF16)
    pe, act, dve = P["pe"], P["act"], P["dve"]
    pool = P["pool"]
    sd, gd, ad = P["sd"], P["gd"], P["ad"]

    c1 = sd.dma(onc[:], ones_col[:, :], *entry)
    c2 = sd.dma(onr[:], ones_row[:, :])
    c3 = sd.dma(bqs[:], bqc[:, :])
    cm = gd.dma(mcols.rearrange("p (kv e) -> p kv e", e=128),
                mrows.rearrange("(kv p) e -> p kv e", p=128), *entry)
    cz = gd.dma(zcs[:], zc[:, :])

    PH1 = ["xs", "sq", "ra", "rb", "rc", "rowbt", "mB", "rB"] + \
          [f"lnt{i}" for i in range(FC)] + ["wq0", "wq1", "e32a", "e32b", "sgb"]
    xs = lb.sbuf("xs", [128, S], BF16)
    sq = lb.sbuf("sq", [128, 512], BF16)
    ra = lb.sbuf("ra", [1, S], F32)
    rb = lb.sbuf("rb", [1, S], F32)
    rc = lb.sbuf("rc", [1, S], F32)
    rowbt = lb.sbuf("rowbt", [1, S], BF16)
    mB = lb.sbuf("mB", [128, S], BF16)
    rB = lb.sbuf("rB", [128, S], BF16)
    ln = [lb.sbuf(f"lnt{i}", [128, S], BF16) for i in range(FC)]
    wq = [lb.sbuf("wq0", [128, D], BF16), lb.sbuf("wq1", [128, D], BF16)]
    e32 = [lb.sbuf("e32a", [128, 512], F32), lb.sbuf("e32b", [128, 512], F32)]
    sgb = lb.sbuf("sgb", [128, S], BF16)

    st = dict(ps=ps, xs=xs, sq=sq, onc=onc, onr=onr, rowb=rowbt,
              ra=ra, rb=rb, rc=rc)
    sev = emit_stats(P, st, lambda fc: x2[fc * 128:(fc + 1) * 128, :], FC,
                     dep=c3, entry=entry, tiles=ln)
    mev = emit_mrstd(P, st, sev)
    bm = emit_bcast_row(P, st, mev, ra[:], mB, 2)
    br = emit_bcast_row(P, st, mev, rb[:], rB, 3, war=bm)
    lev = emit_ln(P, st, x2, ln, mB, rB, br, resident=True)

    wld = [None, None]
    mm_done = {}
    evac_done = {}
    sg_done = {}
    wld[0] = col_load(gd, wq[0], wqt, 0, dep=cz)
    for oc in range(8):
        pb = oc % 2
        half = pb * 4
        if oc + 1 < 8:
            nb = (oc + 1) % 2
            wld[nb] = col_load(gd, wq[nb], wqt, oc + 1, dep=mm_done.get(oc - 1))
        deps = [wld[pb], lev, evac_done.get(oc - 2)]
        mm = None
        for fc in range(FC):
            for qt in range(QT):
                mm = pe.op(lambda q, fc=fc, qt=qt, pb=pb, half=half: q.matmul(
                    ps[:, half + qt, :], wq[pb][:, fc * 128:(fc + 1) * 128],
                    ln[fc][:, qt * 512:(qt + 1) * 512],
                    start=(fc == 0), stop=(fc == FC - 1)),
                    *(deps if fc == 0 and qt == 0 else []))
        mm_done[oc] = mm
        ee = None
        for qt in range(QT):
            eb = qt % 2
            ea = act.op(lambda q, qt=qt, half=half, eb=eb, oc=oc: q.activation(
                e32[eb][:, :], ps[:, half + qt, :], AF.Identity,
                bias=bqs[:, oc:oc + 1]), mm, c3, evac_done.get(oc - 1),
                sg_done.get((oc, qt - 2)))
            e1 = dve.op(lambda q, qt=qt, eb=eb: q.tensor_scalar(
                sgb[:, qt * 512:(qt + 1) * 512], e32[eb][:, :], 0.0, None, ALU.min),
                ea, sg_done.get(oc - 1) if qt == 0 else None)
            e2 = act.op(lambda q, qt=qt: q.activation(
                sgb[:, qt * 512:(qt + 1) * 512], sgb[:, qt * 512:(qt + 1) * 512],
                AF.Exp), e1)
            e3 = dve.op(lambda q, qt=qt, eb=eb: q.tensor_scalar(
                e32[eb][:, :], e32[eb][:, :], 0.0, None, ALU.max), e2)
            ee = dve.op(lambda q, qt=qt, eb=eb: q.tensor_tensor(
                sgb[:, qt * 512:(qt + 1) * 512],
                sgb[:, qt * 512:(qt + 1) * 512], e32[eb][:, :], ALU.add))
            sg_done[(oc, qt)] = ee
        evac_done[oc] = ee
        sg_done[oc] = ad.dma(sigq_s[oc * 128:(oc + 1) * 128, :], sgb[:], ee)

    lb.free(*reversed(PH1))
    ret = [lb.sbuf(f"ret{i}", [128, S], BF16) for i in range(8)]
    sgt = lb.sbuf("sgt", [128, S], BF16)
    recB = lb.sbuf("recB", [128, S], F32)
    rd2 = lb.sbuf("rd2", [1, S], F32)
    rw2 = lb.sbuf("rw2", [1, S], BF16)
    x2r = [lb.sbuf(f"x2r{i}", [128, S], BF16) for i in range(2)]
    x3t = [lb.sbuf(f"x3t{i}", [128, S], BF16) for i in range(2)]
    wo = [lb.sbuf("wo0", [128, D], BF16), lb.sbuf("wo1", [128, D], BF16)]
    f32e = [lb.sbuf(f"f32e{i}", [128, 512], F32) for i in range(4)]

    ret_done = {}
    prevkv = {}
    ph1_done = sg_done[7]
    for kv in range(8):
        sld = sd.dma(sgt[:], sigq_s[kv * 128:(kv + 1) * 128, :], ph1_done,
                     prevkv.get(kv - 1))
        mmn = None
        for qt in range(QT):
            mmn = pe.op(lambda q, qt=qt, kv=kv: q.matmul(
                ps[:, qt, :], mcols[:, kv * 128:(kv + 1) * 128],
                sgt[:, qt * 512:(qt + 1) * 512], start=True, stop=True),
                sld, cm, prevkv.get(("num", kv - 1)))
        edn = None
        for qt in range(QT):
            mmd = pe.op(lambda q, qt=qt, kv=kv: q.matmul(
                ps[0:1, 4, :], zcs[:, kv:kv + 1],
                sgt[:, qt * 512:(qt + 1) * 512], start=True, stop=True),
                edn if edn else (prevkv.get(("den", kv - 1)) or sld))
            edn = act.op(lambda q, qt=qt: q.copy(rd2[0:1, qt * 512:(qt + 1) * 512],
                                                 ps[0:1, 4, :]), mmd)
        er = dve.op(lambda q: q.tensor_scalar(rd2[:], rd2[:], 1e-6, None, ALU.add), edn)
        er = act_recip(P, rd2[:], er)
        er = dve.op(lambda q: q.tensor_copy(rw2[0:1, :], rd2[:]), er, prevkv.get(("bc", kv - 1)))
        eb2 = None
        for qt in range(QT):
            mmb = pe.op(lambda q, qt=qt: q.matmul(
                ps[:, 5, :], onr[:], rw2[0:1, qt * 512:(qt + 1) * 512],
                start=True, stop=True), er, eb2)
            eb2 = act.op(lambda q, qt=qt: q.copy(
                recB[:, qt * 512:(qt + 1) * 512], ps[:, 5, :]), mmb)
        prevkv[("bc", kv)] = eb2
        ev = None
        for qt in range(QT):
            eva = act.op(lambda q, qt=qt: q.copy(
                sgt[:, qt * 512:(qt + 1) * 512], ps[:, qt, :]), mmn, eb2,
                prevkv.get(("rm", kv - 1)))
            ev = dve.op(lambda q, qt=qt, kv=kv: q.tensor_tensor(
                ret[kv][:, qt * 512:(qt + 1) * 512],
                sgt[:, qt * 512:(qt + 1) * 512],
                recB[:, qt * 512:(qt + 1) * 512], ALU.mult), eva)
        prevkv[("rm", kv)] = ev
        prevkv[("num", kv)] = ev
        prevkv[("den", kv)] = edn
        prevkv[kv] = ev
        ret_done[kv] = ev
    all_ret = ret_done[7]

    wld = [None, None]
    mm_done = {}
    evac_done = {}
    out_done = {}
    x2ld = [None, None]
    wld[0] = col_load(gd, wo[0], wot, 0, dep=ph1_done)
    x2ld[0] = sd.dma(x2r[0][:], x2[0:128, :], all_ret)
    for oc in range(FC):
        pb = oc % 2
        half = pb * 4
        if oc + 1 < FC:
            nb = (oc + 1) % 2
            wld[nb] = col_load(gd, wo[nb], wot, oc + 1, dep=mm_done.get(oc - 1))
            x2ld[nb] = sd.dma(x2r[nb][:], x2[(oc + 1) * 128:(oc + 2) * 128, :],
                              evac_done.get(oc - 1))
        deps = [wld[pb], all_ret, evac_done.get(oc - 2)]
        mm = None
        for hc in range(FC):
            for qt in range(QT):
                mm = pe.op(lambda q, hc=hc, qt=qt, pb=pb, half=half: q.matmul(
                    ps[:, half + qt, :], wo[pb][:, hc * 128:(hc + 1) * 128],
                    ret[hc // 4][:, qt * 512:(qt + 1) * 512],
                    start=(hc == 0), stop=(hc == FC - 1)),
                    *(deps if hc == 0 and qt == 0 else []))
        mm_done[oc] = mm
        ee = None
        for qt in range(QT):
            ea = act.op(lambda q, qt=qt, half=half: q.copy(
                f32e[qt][:, :], ps[:, half + qt, :]), mm, evac_done.get(oc - 1))
            ee = dve.op(lambda q, qt=qt, pb=pb: q.tensor_tensor(
                x3t[pb][:, qt * 512:(qt + 1) * 512], f32e[qt][:, :],
                x2r[pb][:, qt * 512:(qt + 1) * 512], ALU.add),
                ea, x2ld[pb], out_done.get(oc - 2))
        evac_done[oc] = ee
        out_done[oc] = ad.dma(x3[oc * 128:(oc + 1) * 128, :], x3t[pb][:], ee)

    lb.free("f32e3", "f32e2", "f32e1", "f32e0", "wo1", "wo0", "x3t1", "x3t0",
            "x2r1", "x2r0", "rw2", "rd2", "recB", "sgt",
            "ret7", "ret6", "ret5", "ret4", "ret3", "ret2", "ret1", "ret0",
            "onr", "onc", "bqs", "zcs", "mcols")
    return out_done[FC - 1]


def build_attn1():
    lb = LB("attn1")
    nc = lb.nc
    ext = {k: nc.dram_tensor(k, sh, dt, kind="ExternalInput").ap()
           for k, (sh, dt) in ATTN1_EXT.items()}
    ext["x3"] = nc.dram_tensor("x3", [D, S], BF16, kind="ExternalOutput").ap()
    ext["sigq_s"] = nc.dram_tensor("sigq_s", [1024, S], BF16).ap()
    ps = lb.psum("ps", [128, 8, 512], F32)
    P = lb.mkqueues()
    done = emit_attn1(lb, P, ext, ps)
    P["pe"].barrier(done)
    lb.emit()
    return lb


def prep_attn1_inputs(inp, x2_fm_bf16):
    sel = np.concatenate([np.arange(h * 128, (h + 1) * 128) for h in range(0, 32, 4)])
    Wq = inp["L1_attn_Wq"].astype(np.float32)
    bq = inp["L1_attn_bq"].astype(np.float32)
    d = dict(
        wqt=np.ascontiguousarray(Wq[sel].T),
        bqc=np.ascontiguousarray(bq[sel].reshape(8, 128).T),
        mrows=np.ascontiguousarray(inp["L1_attn_M"].astype(np.float32).reshape(1024, 128)),
        zc=np.ascontiguousarray(inp["L1_attn_z"].astype(np.float32).T),
        wot=np.ascontiguousarray(inp["L1_attn_Wo"].astype(np.float32).T),
        ones_col=np.ones((128, 1), BF),
        ones_row=np.ones((1, 128), BF),
    )
    if x2_fm_bf16 is not None:
        d["x2"] = np.ascontiguousarray(x2_fm_bf16)
    return d



ATTN0_EXT = dict(xf=([D, S], F32), wqkvt=([D, 3 * D], F32), bqkc=([128, 96], F32),
                 wout=([D, D], F32), boc=([128, 32], F32), alibi=([128, 512], F32),
                 masks=([128, 4 * 512], BF16), qcorr=([32, S], BF16),
                 identm=([128, 128], BF16), ones_col=([128, 1], BF16),
                 ones_row=([1, 128], BF16))


def emit_attn0(lb, P, ext, ps, entry=(), nheads=32, mid_hook=None):
    no_tr = no_mask = no_den = no_av = False
    xf, wqkvt, bqkc, wout, boc = (ext["xf"], ext["wqkvt"], ext["bqkc"],
                                  ext["wout"], ext["boc"])
    alibi, masks, qcorr, identm = (ext["alibi"], ext["masks"], ext["qcorr"],
                                   ext["identm"])
    ones_col, ones_row = ext["ones_col"], ext["ones_row"]
    x1, qkv_s = ext["x1"], ext["qkv_s"]
    pool = P["pool"]

    alb = lb.sbuf("alb", [128, 512], F32)
    msk = lb.sbuf("msk", [128, 4 * 512], BF16)
    onc = lb.sbuf("onc", [128, 1], BF16)
    onr = lb.sbuf("onr", [1, 128], BF16)
    bqk = lb.sbuf("bqk", [128, 96], F32)
    bos = lb.sbuf("bos", [128, 32], F32)
    idt = lb.sbuf("idt", [128, 128], BF16)
    pe, act, dve = P["pe"], P["act"], P["dve"]
    sd, gd, ad = P["sd"], P["gd"], P["ad"]

    c1 = sd.dma(onc[:], ones_col[:, :], *entry)
    c2 = sd.dma(onr[:], ones_row[:, :])
    c3 = sd.dma(bqk[:], bqkc[:, :])
    c4 = sd.dma(alb[:], alibi[:, :])
    c5 = sd.dma(msk[:], masks[:, :])
    c6 = sd.dma(bos[:], boc[:, :])
    c8 = sd.dma(idt[:], identm[:, :])

    deq_bufs = alloc_dequant(lb) if mid_hook is not None else None
    ln = [lb.sbuf(f"ln{i}", [128, S], BF16) for i in range(FC)]
    PH0 = ["sq", "ra", "rb", "rc", "rowbt", "mB", "rB"]
    sq = lb.sbuf("sq", [128, 512], BF16)
    ra = lb.sbuf("ra", [1, S], F32)
    rb = lb.sbuf("rb", [1, S], F32)
    rc = lb.sbuf("rc", [1, S], F32)
    rowbt = lb.sbuf("rowbt", [1, S], BF16)
    mB = lb.sbuf("mB", [128, S], BF16)
    rB = lb.sbuf("rB", [128, S], BF16)
    st = dict(ps=ps, xs=None, sq=sq, onc=onc, onr=onr,
              rowb=rowbt, ra=ra, rb=rb, rc=rc)
    sev = emit_stats(P, st, lambda fc: xf[fc * 128:(fc + 1) * 128, :], FC,
                     dep=c8, from_f32=True, entry=entry, tiles=ln)
    mev = emit_mrstd(P, st, sev)
    bm = emit_bcast_row(P, st, mev, ra[:], mB, 2)
    br = emit_bcast_row(P, st, mev, rb[:], rB, 3, war=bm)
    lev = emit_ln(P, st, xf, ln, mB, rB, br, resident=True)
    lb.free(*reversed(PH0))

    deq_gen, deq_done, deq_names = None, (), None
    if mid_hook is not None:
        deq_gen = mid_hook(lev, deq_bufs)
        next(deq_gen, None)

    PH1 = ["wc0", "wc1", "qkb0", "qkb1"]
    wc = [lb.sbuf("wc0", [128, D], BF16), lb.sbuf("wc1", [128, D], BF16)]
    qkb = [lb.sbuf("qkb0", [128, S], BF16), lb.sbuf("qkb1", [128, S], BF16)]

    wld = [None, None]
    mm_done = {}
    evac_done = {}
    out_done = {}
    wld[0] = col_load(gd, wc[0], wqkvt, 0, 128, lev, *entry)
    for oc in range(96):
        if deq_gen is not None and oc >= 12 and (oc - 12) % 9 == 0:
            next(deq_gen, None)
        pb = oc % 2
        if oc + 1 < 96:
            nb = (oc + 1) % 2
            wld[nb] = col_load(gd, wc[nb], wqkvt, oc + 1, dep=mm_done.get(oc - 1))
        half = pb * 4
        deps = [wld[pb], lev, evac_done.get(oc - 2)]
        mm = None
        for fc in range(FC):
            for qt in range(QT):
                mm = pe.op(lambda q, fc=fc, qt=qt, pb=pb, half=half: q.matmul(
                    ps[:, half + qt, :], wc[pb][:, fc * 128:(fc + 1) * 128],
                    ln[fc][:, qt * 512:(qt + 1) * 512],
                    start=(fc == 0), stop=(fc == FC - 1)),
                    *(deps if fc == 0 and qt == 0 else []))
        ee = None
        for qt in range(QT):
            ee = act.op(lambda q, qt=qt, half=half, oc=oc, pb=pb: q.activation(
                qkb[pb][:, qt * 512:(qt + 1) * 512], ps[:, half + qt, :],
                AF.Identity, bias=bqk[:, oc:oc + 1]), mm, c3,
                evac_done.get(oc - 1), out_done.get(oc - 2))
            evac_done[(oc, qt)] = ee
        evac_done[oc] = ee
        mm_done[oc] = mm
        out_done[oc] = ad.dma(qkv_s[oc * 128:(oc + 1) * 128, :], qkb[pb][:],
                              evac_done[oc])
    qkv_done = out_done[95]
    pe_a1 = pe.ev()
    act_a1 = act.ev()
    lb.free(*reversed(PH1))
    lb.free(*[f"ln{i}" for i in reversed(range(FC))])

    SCL = float(1.0 / np.sqrt(128.0))
    afm = [lb.sbuf(f"afm{i}", [128, S], BF16) for i in range(FC)]
    PH2 = ["qh_sb", "qcr", "kh_sb", "vh_sb", "vht", "ex0", "ex1", "rdn", "rwb",
           "recB", "ava"]
    qh_sb = lb.sbuf("qh_sb", [128, S], BF16)
    qcr = lb.sbuf("qcr", [1, S], BF16)
    kh_sb = lb.sbuf("kh_sb", [128, S], BF16)
    vh_sb = lb.sbuf("vh_sb", [128, S], BF16)
    vht = lb.sbuf("vht", [128, S], BF16)
    ex = [lb.sbuf("ex0", [128, 512], BF16), lb.sbuf("ex1", [128, 512], BF16)]
    rdn = lb.sbuf("rdn", [1, S], F32)
    rwb = lb.sbuf("rwb", [1, S], BF16)
    recB = lb.sbuf("recB", [128, 512], F32)
    ava = lb.sbuf("ava", [128, 512], F32)

    hd_ld = {}
    hev = {}
    for h in range(nheads):
        if deq_gen is not None and h in (1, 4, 6, 9, 11, 14):
            next(deq_gen, None)
        lq = sd.dma(qh_sb[:], qkv_s[h * 128:(h + 1) * 128, :],
                    qkv_done, hev.get(("smm", h - 1)),
                    pe_a1 if h == 0 else None, act_a1 if h == 0 else None)
        lk = sd.dma(kh_sb[:], qkv_s[D + h * 128:D + (h + 1) * 128, :])
        lqc = sd.dma(qcr[:], qcorr[h:h + 1, :], hev.get(("smm", h - 1)))
        lv = sd.dma(vh_sb[:], qkv_s[2 * D + h * 128:2 * D + (h + 1) * 128, :],
                    hev.get(("tr", h - 1)))
        tv = None
        pstv = [ps[:, 4, :].bitcast(BF16), ps[:, 5, :].bitcast(BF16)]
        for kc in range(0 if no_tr else 16):
            tb = kc % 2
            mmt = pe.op(lambda q, kc=kc, tb=tb, pstv=pstv: q.transpose(
                pstv[tb][:, 0:128],
                vh_sb[:, kc * 128:(kc + 1) * 128], idt[:]),
                lv, c8, hev.get(("av", h - 1)), hev.get(("trev", (h, kc - 2))),
                hev.get(("rb", (h - 1, 3))))
            tv = act.op(lambda q, kc=kc, tb=tb, pstv=pstv: q.copy(
                vht[:, kc * 128:(kc + 1) * 128], pstv[tb][:, 0:128]), mmt)
            hev[("trev", (h, kc))] = tv
        hev[("tr", h)] = tv
        smm_last = None
        for qtg in range(4):
            kmax = 4 * qtg + 4

            def smm_pair(kc0, qtg=qtg, h=h):
                diag = kc0 >= 4 * qtg
                ends = {}
                for kc in (kc0, kc0 + 1):
                    pe.op(lambda q, kc=kc, sb=kc % 2, qtg=qtg: q.matmul(
                        ps[:, sb, :], kh_sb[:, kc * 128:(kc + 1) * 128],
                        qh_sb[:, qtg * 512:(qtg + 1) * 512], start=True,
                        stop=False),
                        lq, lk, hev.get(("exp", (h, qtg, kc - 2))),
                        hev.get(("sprev", h)) if qtg == 0 and kc == 0 else None)
                for kc in (kc0, kc0 + 1):
                    ends[kc] = pe.op(lambda q, sb=kc % 2, qtg=qtg, diag=diag:
                                     q.matmul(
                        ps[:, sb, :], onr[:], qcr[0:1, qtg * 512:(qtg + 1) * 512],
                        start=False, stop=not diag), lqc)
                if diag:
                    for kc in (kc0, kc0 + 1):
                        ends[kc] = pe.op(lambda q, sb=kc % 2, v=kc - 4 * qtg:
                                         q.matmul(
                            ps[:, sb, :], idt[:], msk[:, v * 512:(v + 1) * 512],
                            start=False, stop=True), c5, c8)
                return ends

            def exp_emit(kc, smm, qtg=qtg, h=h):
                sb = kc % 2
                eex = act.op(lambda q, kc=kc, sb=sb, h=h: q.activation(
                    ex[sb][:, :], ps[:, sb, :], AF.Exp, scale=SCL,
                    bias=alb[:, h * 16 + kc:h * 16 + kc + 1]),
                    smm, c4, hev.get(("avrd", (h, qtg, kc - 2))))
                hev[("exp", (h, qtg, kc))] = eex
                return eex

            def avden_emit(kc, qtg=qtg, h=h, kmax=kmax):
                sb = kc % 2
                eex = hev[("exp", (h, qtg, kc))]
                pe.op(lambda q, kc=kc, sb=sb, kmax=kmax: q.matmul(
                    ps[:, 2, :], vht[:, kc * 128:(kc + 1) * 128], ex[sb][:, :],
                    start=(kc == 0), stop=(kc == kmax - 1)),
                    eex, tv, hev.get(("attn", (h, qtg - 1))) if kc == 0 else None)
                den_mm = pe.op(lambda q, kc=kc, sb=sb, kmax=kmax: q.matmul(
                    ps[0:1, 3, :], onc[:], ex[sb][:, :],
                    start=(kc == 0), stop=(kc == kmax - 1)),
                    hev.get(("dene", (h, qtg - 1))) if kc == 0 else None)
                hev[("avrd", (h, qtg, kc))] = den_mm
                return den_mm

            for kc0 in range(0, kmax, 2):
                ends = smm_pair(kc0)
                smm_last = ends[kc0 + 1]
                if kc0 >= 2:
                    avden_emit(kc0 - 2)
                    avden_emit(kc0 - 1)
                exp_emit(kc0, ends[kc0])
                exp_emit(kc0 + 1, ends[kc0 + 1])
            avden_emit(kmax - 2)
            den_mm = avden_emit(kmax - 1)
            av_last = den_mm
            ed = act.op(lambda q, qtg=qtg: q.copy(rdn[0:1, qtg * 512:(qtg + 1) * 512],
                                                  ps[0:1, 3, :]), den_mm)
            hev[("dene", (h, qtg))] = ed
            er = act_recip(P, rdn[0:1, qtg * 512:(qtg + 1) * 512], ed)
            er = pool.op(lambda q, qtg=qtg: q.tensor_copy(
                rwb[0:1, qtg * 512:(qtg + 1) * 512],
                rdn[0:1, qtg * 512:(qtg + 1) * 512]), er)
            bmm = pe.op(lambda q, qtg=qtg: q.matmul(
                ps[:, 4, :], onr[:], rwb[0:1, qtg * 512:(qtg + 1) * 512],
                start=True, stop=True), er, hev.get(("rb", (h, qtg - 1))),
                hev.get(("tr", h)))
            eb = act.op(lambda q: q.copy(recB[:, :], ps[:, 4, :]), bmm)
            hev[("rb", (h, qtg))] = eb
            eva = act.op(lambda q, qtg=qtg: q.copy(ava[:, :], ps[:, 2, :]),
                         av_last, hev.get(("ava", (h, qtg - 1))))
            ea = pool.op(lambda q, h=h, qtg=qtg: q.tensor_tensor(
                afm[h][:, qtg * 512:(qtg + 1) * 512], ava[:, :], recB[:, :],
                ALU.mult), eva, eb)
            hev[("ava", (h, qtg))] = ea
            hev[("attn", (h, qtg))] = ea
        hev[("smm", h)] = smm_last
        hev[("av", h)] = hev[("attn", (h, 3))]
        hev[("sprev", h + 1)] = hev[("exp", (h, 3, 15))]
    attn_all = hev[("attn", (nheads - 1, 3))]
    pe_a2 = pe.ev()
    act_a2 = act.ev()
    if deq_gen is not None:
        for _ in deq_gen:
            pass
        deq_done = ext["deq_done"]
        deq_names = ext["deq_names"]
    lb.free(*reversed(PH2))

    wc3 = [lb.sbuf("wo0", [128, D], BF16), lb.sbuf("wo1", [128, D], BF16)]
    xr = [lb.sbuf(f"xr{i}", [128, S], F32) for i in range(2)]
    x1t = [lb.sbuf(f"x1t{i}", [128, S], BF16) for i in range(2)]
    f32e = [lb.sbuf(f"f32o{i}", [128, 512], F32) for i in range(2)]
    wld = [None, None]
    mm_done = {}
    evac_done = {}
    out_done = {}
    xld = [None, None]
    wld[0] = col_load(gd, wc3[0], wout, 0, 128, attn_all, pe_a2, act_a2)
    xld[0] = sd.dma(xr[0][:], xf[0:128, :], attn_all, pe_a2, act_a2,
                    *deq_done)
    for oc in range(FC):
        pb = oc % 2
        if oc + 1 < FC:
            nb = (oc + 1) % 2
            wld[nb] = col_load(gd, wc3[nb], wout, oc + 1, dep=mm_done.get(oc - 1))
            xld[nb] = sd.dma(xr[nb][:], xf[(oc + 1) * 128:(oc + 2) * 128, :],
                             evac_done.get(oc - 1))
        half = pb * 4
        deps = [wld[pb], attn_all, evac_done.get(oc - 2)]
        mm = None
        for hc in range(FC):
            for qt in range(QT):
                mm = pe.op(lambda q, hc=hc, qt=qt, pb=pb, half=half: q.matmul(
                    ps[:, half + qt, :], wc3[pb][:, hc * 128:(hc + 1) * 128],
                    afm[hc][:, qt * 512:(qt + 1) * 512],
                    start=(hc == 0), stop=(hc == FC - 1)),
                    *(deps if hc == 0 and qt == 0 else []))
        ee = None
        ees = {}
        for qt in range(QT):
            ea = act.op(lambda q, qt=qt, half=half, oc=oc: q.activation(
                f32e[qt % 2][:, :], ps[:, half + qt, :], AF.Identity,
                bias=bos[:, oc:oc + 1]), mm, c6, evac_done.get(oc - 1),
                out_done.get(oc - 2), ees.get(qt - 2))
            ee = dve.op(lambda q, qt=qt, pb=pb: q.tensor_tensor(
                x1t[pb][:, qt * 512:(qt + 1) * 512], f32e[qt % 2][:, :],
                xr[pb][:, qt * 512:(qt + 1) * 512], ALU.add), ea, xld[pb])
            ees[qt] = ee
        evac_done[oc] = ee
        mm_done[oc] = mm
        out_done[oc] = ad.dma(x1[oc * 128:(oc + 1) * 128, :], x1t[pb][:], ee)

    lb.free("f32o1", "f32o0", "x1t1", "x1t0", "xr1", "xr0",
            "wo1", "wo0")
    for i in reversed(range(FC)):
        lb.free(f"afm{i}")
    if deq_names:
        lb.free(*reversed(deq_names))
    lb.free("idt", "bos", "bqk", "onr", "onc", "msk", "alb")
    return out_done[FC - 1]


def build_attn0():
    lb = LB("attn0")
    nc = lb.nc
    ext = {k: nc.dram_tensor(k, sh, dt, kind="ExternalInput").ap()
           for k, (sh, dt) in ATTN0_EXT.items()}
    ext["x1"] = nc.dram_tensor("x1", [D, S], BF16, kind="ExternalOutput").ap()
    ext["qkv_s"] = nc.dram_tensor("qkv_s", [3 * D, S], BF16)
    ps = lb.psum("ps", [128, 8, 512], F32)
    P = lb.mkqueues()
    done = emit_attn0(lb, P, ext, ps)
    P["pe"].barrier(done)
    lb.emit()
    return lb


def prep_attn0_inputs(inp, core):
    x = inp["L0_attn_x"][core].astype(np.float32)
    wqkv = inp["L0_attn_w_qkv"].astype(np.float32)
    bqkv = inp["L0_attn_b_qkv"].astype(np.float32)
    slopes = 1.0 / np.power(2.0, np.arange(1, 33, dtype=np.float32) * 0.25)
    alibi = np.zeros((128, 512), np.float32)
    for h in range(32):
        for kc in range(16):
            k = 128 * kc + np.arange(128)
            alibi[:, h * 16 + kc] = slopes[h] * (k - 2047.0)
    masks = np.zeros((128, 4 * 512), BF)
    for v in range(4):
        for p in range(128):
            js = np.arange(512)
            masks[p, v * 512 + js] = np.where(js >= 128 * v + p, 0.0, -1.0e9)
    qs = np.arange(2048, dtype=np.float32)
    qcorr = np.stack([(-slopes[h] * (qs - 2047.0) - 8.0) * np.sqrt(128.0)
                      for h in range(32)]).astype(BF)
    return dict(
        xf=np.ascontiguousarray(x.T),
        wqkvt=np.ascontiguousarray(wqkv.T),
        bqkc=np.ascontiguousarray(bqkv.reshape(96, 128).T),
        wout=np.ascontiguousarray(inp["L0_attn_w_out"].astype(np.float32).T),
        boc=np.ascontiguousarray(inp["L0_attn_b_out"].astype(np.float32).reshape(32, 128).T),
        alibi=alibi,
        masks=masks,
        qcorr=qcorr,
        identm=np.eye(128, dtype=BF),
        ones_col=np.ones((128, 1), BF),
        ones_row=np.ones((1, 128), BF),
    )
